# version 11
# baseline (speedup 1.0000x reference)
import numpy as np

import concourse.bass as bass
import concourse.mybir as mybir
from concourse.bass_utils import run_bass_kernel_spmd

F32 = mybir.dt.float32
I32 = mybir.dt.int32
AF = mybir.ActivationFunctionType
ALU = mybir.AluOpType

D = 1024
H = 16
DH = 64
FF = 3072
L = 4096
NB = 32
TT = 512
NT = 8
NQ = 2048
NEG = -1.0e9
CSHIFT = -4.0
EPS = 1e-5
DSCALE = 1.0 / (127.0 * 127.0)
ENGS = ("sync", "tensor", "vector", "scalar", "gpsimd")
OWN = {"tensor": "pe", "vector": "dv", "scalar": "ac", "gpsimd": "gp"}
ALL_SEMS = ("pre", "x0", "x1", "k2a", "k2b", "v2a", "v2b", "o3",
            "w3a", "w3b", "g3a", "g3b", "d3a", "d3b",
            "sps", "spq", "spv", "spo", "wcv", "out", "pe", "dv", "ac", "gp")


class Plan:

    def __init__(self):
        self.cnt = {}
        self.q = {e: [] for e in ENGS}

    def c(self, s):
        return self.cnt.get(s, 0)

    ENG_SEMS = ("pe", "dv", "ac", "gp")

    def step(self, eng, fn, waits=(), dma=None, chain=True):
        ws = {}
        for item in waits:
            if item is None:
                continue
            s, v = item
            if v:
                ws[s] = max(ws.get(s, 0), v)
        inc = None
        if dma is not None:
            prior = self.cnt.get(dma, 0)
            if prior:
                ws[dma] = max(ws.get(dma, 0), prior)
            inc = (dma, 16)
        elif eng in OWN:
            own = OWN[eng]
            if chain and eng in ("vector", "scalar", "gpsimd"):
                prev = self.cnt.get(own, 0)
                if prev:
                    ws[own] = max(ws.get(own, 0), prev)
            inc = (own, 1)
        if inc:
            self.cnt[inc[0]] = self.cnt.get(inc[0], 0) + inc[1]
        self.q[eng].append((fn, sorted(ws.items()), inc))
        return self.cnt.get(inc[0], 0) if inc else 0

    def now(self, *sems):
        return [(s, self.cnt.get(s, 0)) for s in sems]


def build_nc(debug=False):
    nc = bass.Bass("TRN2", target_bir_lowering=False)

    xT = nc.dram_tensor("xT", [D, L + 128], F32, kind="ExternalInput").ap()
    winT = nc.dram_tensor("winT", [D, 3 * D], F32, kind="ExternalInput").ap()
    woutT = nc.dram_tensor("woutT", [D, D], F32, kind="ExternalInput").ap()
    wgTi = nc.dram_tensor("wgTi", [D, FF], I32, kind="ExternalInput").ap()
    wuTi = nc.dram_tensor("wuTi", [D, FF], I32, kind="ExternalInput").ap()
    wdTi = nc.dram_tensor("wdTi", [FF, D], I32, kind="ExternalInput").ap()
    g1i = nc.dram_tensor("g1i", [128, 8], F32, kind="ExternalInput").ap()
    b1i = nc.dram_tensor("b1i", [128, 8], F32, kind="ExternalInput").ap()
    g2i = nc.dram_tensor("g2i", [128, 8], F32, kind="ExternalInput").ap()
    b2i = nc.dram_tensor("b2i", [128, 8], F32, kind="ExternalInput").ap()
    slopesi = nc.dram_tensor("slopesi", [16, 1], F32, kind="ExternalInput").ap()
    smeari = nc.dram_tensor("smeari", [128, 8], F32, kind="ExternalInput").ap()
    kposi = nc.dram_tensor("kposi", [16, L], F32, kind="ExternalInput").ap()
    maskAi = nc.dram_tensor("maskAi", [128, 128], F32, kind="ExternalInput").ap()
    maskBi = nc.dram_tensor("maskBi", [128, 128], F32, kind="ExternalInput").ap()
    bmaski = nc.dram_tensor("bmaski", [128, NB], F32, kind="ExternalInput").ap()
    identi = nc.inline_tensor(np.eye(128, dtype=np.float32), name="identi").ap()
    yT = nc.dram_tensor("yT", [D, NQ], F32, kind="ExternalOutput").ap()

    spk = "ExternalOutput" if debug else "Internal"
    BF16_ = mybir.dt.bfloat16
    kspA = nc.dram_tensor("kspA", [H * 64, L], BF16_, kind=spk).ap()
    kspB = nc.dram_tensor("kspB", [H * 68, L], BF16_, kind=spk).ap()
    vspill = nc.dram_tensor("vspill", [H * NB * 128, DH], F32, kind=spk).ap()
    qspA = nc.dram_tensor("qspA", [H * 128, NQ], BF16_, kind=spk).ap()
    qspB = nc.dram_tensor("qspB", [H * 4, NQ], BF16_, kind=spk).ap()
    ospill = nc.dram_tensor("ospill", [D, NQ], F32, kind=spk).ap()
    BF16 = mybir.dt.bfloat16
    wgb = nc.dram_tensor("wgb", [D, FF], BF16, kind="Internal").ap()
    wub = nc.dram_tensor("wub", [D, FF], BF16, kind="Internal").ap()
    wdb = nc.dram_tensor("wdb", [FF, D], BF16, kind="Internal").ap()

    ARENA = 206 * 1024
    arena = nc.alloc_sbuf_tensor("arena", [128, ARENA], mybir.dt.uint8)
    base0 = nc.lookup_mloc(arena).addr
    cur = [base0]

    def place(name, shape, dtype=F32):
        nbytes = int(np.prod(shape[1:])) * mybir.dt.size(dtype)
        off = (cur[0] + 31) // 32 * 32
        cur[0] = off + nbytes
        assert cur[0] <= base0 + ARENA, f"arena overflow at {name}"
        return nc.alloc_sbuf_tensor_at(name, shape, dtype, offset=off).ap()

    consts = place("consts", [128, 4])
    ones128 = place("ones128", [128, 128])
    maskA = place("maskA", [128, 128])
    maskB = place("maskB", [128, 128])
    g1s = place("g1s", [128, 8]); b1s = place("b1s", [128, 8])
    g2s = place("g2s", [128, 8]); b2s = place("b2s", [128, 8])
    smear_s = place("smear_s", [128, 8])
    ident = place("ident", [128, 128])
    pbase = cur[0]

    wi_s = [place(f"wi_s{i}", [128, 8 * 128]) for i in range(2)]
    kpos_s = place("kpos_s", [16, L])
    slopes_s = place("slopes_s", [16, 1])
    bmask = place("bmask", [128, NB])
    xt = [place(f"xt{i}", [128, 8 * TT]) for i in range(2)]
    sqc = place("sqc", [128, TT])
    meanb = place("meanb", [128, TT])
    rstdb = place("rstdb", [128, TT])
    tmpb = place("tmpb", [128, TT])
    kstage = place("kstage", [128, TT])
    ksm = place("ksm", [128, TT])
    ksmh = place("ksmh", [128, TT], BF16)
    ksml = place("ksml", [128, TT], BF16)
    qsh = place("qsh", [128, TT], BF16)
    qsl = place("qsl", [128, TT], BF16)
    qposb = place("qposb", [16, NQ], BF16)
    onesb = place("onesb", [16, NQ], BF16)
    kprevs = place("kprevs", [128, 8 * NB])
    vstage = place("vstage", [128, TT])
    vtstage = place("vtstage", [128, TT])
    qstage = place("qstage", [128, TT])

    kbufA = [place(f"kbufA{i}", [128, L], BF16) for i in range(2)]
    kbufB = [place(f"kbufB{i}", [128, L], BF16) for i in range(2)]
    vbuf = [place(f"vbuf{i}", [128, NB * 65]) for i in range(2)]
    qbufA = [place(f"qbufA{i}", [128, TT], BF16) for i in range(2)]
    qbufB = [place(f"qbufB{i}", [128, TT], BF16) for i in range(2)]
    pbuf = [place(f"pbuf{i}", [128, TT]) for i in range(2)]
    rs_s = place("rs_s", [65, TT])
    bc_s = place("bc_s", [64, TT])
    ostg = place("ostg", [64, TT])
    ahi16 = place("ahi16", [16, L // 2], BF16)
    amid16 = place("amid16", [16, L // 2], BF16)
    alo16 = place("alo16", [16, L // 2], BF16)
    msl16 = place("msl16", [16, L // 2], BF16)
    arow = place("arow", [16, L // 2])
    wcv_i = [place(f"wcv_i{i}", [128, FF // 4], I32) for i in range(2)]
    wcv_o = [place(f"wcv_o{i}", [128, FF // 4], BF16) for i in range(2)]

    cur[0] = pbase
    ot_s = place("ot_s", [128, 8 * TT])
    xo_s = place("xo_s", [128, 8 * TT])
    po_s = place("po_s", [128, 8 * TT])
    t1_s = place("t1_s", [128, 8 * TT])
    scr_s = place("scr_s", [128, 8 * TT])
    hh_s = place("hh_s", [128, 12 * TT], BF16)
    hl_s = place("hl_s", [128, 12 * TT], BF16)
    htmp = place("htmp", [128, TT])
    x1h = place("x1h", [128, 8 * TT], BF16)
    x1l = place("x1l", [128, 8 * TT], BF16)
    wo_s = [place(f"wo_s{i}", [128, 8 * 128]) for i in range(2)]
    wgb_s = [place(f"wgb_s{i}", [128, 8 * 128], BF16) for i in range(2)]
    wub_s = [place(f"wub_s{i}", [128, 8 * 128], BF16) for i in range(2)]
    wdb_s = [place(f"wdb_s{i}", [128, TT], BF16) for i in range(2)]
    gs_s = place("gs_s", [128, TT])
    mean2 = place("mean2", [128, TT])
    rstd2 = place("rstd2", [128, TT])
    tmp2 = place("tmp2", [128, TT])
    sqc3 = place("sqc3", [128, TT])

    pb = [nc.alloc_psum_tensor(f"pb{i}", [128, TT], F32).ap() for i in range(8)]

    SEMS = {}
    sem_ctx = []
    for s in ALL_SEMS:
        cm = nc.semaphore("s_" + s)
        SEMS[s] = cm.__enter__()
        sem_ctx.append(cm)

    P = Plan()

    for dst, src in ((g1s, g1i), (b1s, b1i), (g2s, g2i), (b2s, b2i),
                     (slopes_s, slopesi), (smear_s, smeari), (kpos_s, kposi),
                     (maskA, maskAi), (maskB, maskBi), (bmask, bmaski),
                     (ident, identi)):
        P.step("sync", (lambda e, dst=dst, src=src:
                        e.dma_start(out=dst[:], in_=src[:])), dma="pre")
    ld_pre = P.c("pre")

    P.step("vector", lambda e: e.memset(consts[:, 0:1], 1.0))
    P.step("vector", lambda e: e.memset(consts[:, 1:2], CSHIFT))
    P.step("vector", lambda e: e.memset(ones128[:], 1.0))
    dv_pre = P.c("dv")

    P.step("scalar", lambda e: e.activation(smear_s[:], smear_s[:], AF.Sigmoid),
           waits=[("pre", ld_pre)])
    ac_sig = P.c("ac")

    P.step("vector", lambda e: e.memset(onesb[:], 1.0),
           waits=[("pre", ld_pre)])
    P.step("vector", lambda e: e.tensor_copy(qposb[:], kpos_s[:, 0:NQ]),
           waits=[("pre", ld_pre)])
    dv_q45 = P.c("dv")
    for rr, tsrc in ((0, qposb), (1, onesb), (2, onesb), (3, onesb)):
        P.step("sync", (lambda e, rr=rr, tsrc=tsrc: e.dma_start(
            out=qspB.rearrange("(h r) c -> h r c", r=4)[:, rr:rr + 1, :],
            in_=tsrc[:].rearrange("h (r c) -> h r c", r=1))),
            waits=[("dv", dv_q45)], dma="spq")

    L2 = L // 2
    for chh in range(2):
        cs = slice(chh * L2, (chh + 1) * L2)
        w0 = [("pre", ld_pre)] if chh == 0 else [("sps", P.c("sps"))]
        P.step("vector", (lambda e, cs=cs: e.tensor_scalar(
            arow[:], kpos_s[:, cs], slopes_s[:, 0:1], None, ALU.mult)),
            waits=w0)
        P.step("vector", lambda e: e.tensor_copy(ahi16[:], arow[:]))
        P.step("vector", lambda e: e.tensor_sub(arow[:], arow[:], ahi16[:]))
        P.step("vector", lambda e: e.tensor_copy(amid16[:], arow[:]))
        P.step("vector", lambda e: e.tensor_sub(arow[:], arow[:], amid16[:]))
        P.step("vector", lambda e: e.tensor_copy(alo16[:], arow[:]))
        P.step("vector", (lambda e, cs=cs: e.tensor_scalar(
            arow[:], kpos_s[:, cs], 0.0, 1.0, ALU.mult, ALU.add)))
        P.step("vector", lambda e: e.tensor_scalar(
            arow[:], arow[:], slopes_s[:, 0:1], -1.0, ALU.mult, ALU.mult))
        P.step("vector", lambda e: e.tensor_copy(msl16[:], arow[:]))
        dva = P.c("dv")
        for rr, tsrc in ((0, msl16), (1, ahi16), (2, amid16), (3, alo16)):
            P.step("scalar", (lambda e, rr=rr, tsrc=tsrc, cs=cs: e.dma_start(
                out=kspB.rearrange("(h r) c -> h r c", r=68)[:, rr:rr + 1, cs],
                in_=tsrc[:].rearrange("h (r c) -> h r c", r=1))),
                waits=[("dv", dva)], dma="sps")

    tiles = [("apx", L, 128)] + [("main", t * TT, TT) for t in range(NT)]
    tile_consumed = {}
    bank_free = {4: None, 5: None}
    wi_guard = {}
    sps_at = {}
    spq_at = {}
    spv_at = {}
    S1 = {"kst": None, "vst": None, "qst": None, "statsv": None,
          "vt_evac": None, "sq_pe": None}
    vtt_free = {}

    def emit_tile(ti, pos):
        kind, c0, nw = tiles[ti]
        buf = xt[pos % 2]
        is_own = (kind == "main" and c0 < NQ)
        oc_list = (list(range(8, 16)) if kind == "apx"
                   else (list(range(24)) if is_own else list(range(8, 24))))

        P.step("sync", (lambda e, buf=buf, c0=c0, nw=nw: e.dma_start(
            out=buf[:, 0:8 * nw].rearrange("p (k c) -> p k c", k=8),
            in_=xT[:, c0:c0 + nw].rearrange("(k p) c -> p k c", p=128))),
            waits=[("pe", tile_consumed.get(pos - 2))], dma=("x0" if pos % 2 == 0 else "x1"))
        ld_x = ("x0" if pos % 2 == 0 else "x1", P.c("x0" if pos % 2 == 0 else "x1"))

        P.step("tensor", (lambda e, buf=buf, nw=nw: [
            e.matmul(pb[6][:, 0:nw], ones128[:],
                     buf[:, cc * nw:(cc + 1) * nw],
                     start=(cc == 0), stop=(cc == 7)) for cc in range(8)][-1]),
            waits=[ld_x, ("dv", dv_pre), ("dv", S1["statsv"]),
                   ("dv", S1["vt_evac"])])
        pe_cs = P.c("pe")

        pe_sq = None
        for cc in range(8):
            P.step("scalar", (lambda e, buf=buf, nw=nw, cc=cc: e.activation(
                sqc[:, 0:nw], buf[:, cc * nw:(cc + 1) * nw], AF.Square)),
                waits=[ld_x, ("pe", pe_sq),
                       ("dv", S1["statsv"] if cc == 0 else None)])
            ac_sqd = P.c("ac")
            P.step("tensor", (lambda e, nw=nw, cc=cc: e.matmul(
                pb[7][:, 0:nw], ones128[:], sqc[:, 0:nw],
                start=(cc == 0), stop=(cc == 7))),
                waits=[("ac", ac_sqd), ("pe", pe_cs if cc == 0 else None),
                       ("dv", dv_pre if cc == 0 else None)])
            pe_sq = P.c("pe")
        S1["sq_pe"] = pe_sq

        P.step("vector", (lambda e, nw=nw: e.tensor_scalar(
            meanb[:, 0:nw], pb[6][:, 0:nw], 1.0 / D, None, ALU.mult)),
            waits=[("pe", pe_sq)])
        P.step("vector", (lambda e, nw=nw: e.tensor_scalar(
            tmpb[:, 0:nw], pb[6][:, 0:nw], 1.0 / D, None, ALU.mult)))
        P.step("vector", (lambda e, nw=nw: e.tensor_mul(
            tmpb[:, 0:nw], meanb[:, 0:nw], tmpb[:, 0:nw])))
        P.step("vector", (lambda e, nw=nw: e.scalar_tensor_tensor(
            rstdb[:, 0:nw], pb[7][:, 0:nw], 1.0 / D, tmpb[:, 0:nw],
            ALU.mult, ALU.subtract)))
        P.step("vector", (lambda e, nw=nw: e.tensor_scalar(
            rstdb[:, 0:nw], rstdb[:, 0:nw], EPS, None, ALU.add)))
        P.step("vector", (lambda e, nw=nw: e.reciprocal(
            rstdb[:, 0:nw], rstdb[:, 0:nw])))
        dv_var = P.c("dv")
        P.step("scalar", (lambda e, nw=nw: e.activation(
            rstdb[:, 0:nw], rstdb[:, 0:nw], AF.Sqrt)),
            waits=[("dv", dv_var)])
        ac_rstd = P.c("ac")

        for cc in range(8):
            P.step("vector", (lambda e, buf=buf, nw=nw, cc=cc: e.tensor_sub(
                buf[:, cc * nw:(cc + 1) * nw], buf[:, cc * nw:(cc + 1) * nw],
                meanb[:, 0:nw])), waits=[("ac", ac_rstd)])
            P.step("vector", (lambda e, buf=buf, nw=nw, cc=cc: e.tensor_mul(
                buf[:, cc * nw:(cc + 1) * nw], buf[:, cc * nw:(cc + 1) * nw],
                rstdb[:, 0:nw])))
            P.step("vector", (lambda e, buf=buf, nw=nw, cc=cc: e.tensor_scalar(
                buf[:, cc * nw:(cc + 1) * nw], buf[:, cc * nw:(cc + 1) * nw],
                g1s[:, cc:cc + 1], b1s[:, cc:cc + 1], ALU.mult, ALU.add)))
        dv_std = P.c("dv")
        S1["statsv"] = dv_std

        pe_last = None
        for oi, oc in enumerate(oc_list):
            bank = 4 + oi % 2
            wsl = oi % 2
            wsem = "w3a" if wsl == 0 else "w3b"
            P.step("sync", (lambda e, oc=oc, wsl=wsl: e.dma_start(
                out=wi_s[wsl][:].rearrange("p (k c) -> p k c", k=8),
                in_=winT[:, oc * 128:(oc + 1) * 128]
                    .rearrange("(k p) c -> p k c", p=128))),
                waits=[wi_guard.get(wsl)], dma=wsem)
            ld_wi = (wsem, P.c(wsem))
            P.step("tensor", (lambda e, bank=bank, nw=nw, buf=buf, wsl=wsl: [
                e.matmul(pb[bank][:, 0:nw],
                         wi_s[wsl][:, kc * 128:(kc + 1) * 128],
                         buf[:, kc * nw:(kc + 1) * nw],
                         start=(kc == 0), stop=(kc == 7))
                for kc in range(8)][-1]),
                waits=[("dv", dv_std), bank_free[bank], ld_wi])
            pe_proj = P.c("pe")
            pe_last = pe_proj
            wi_guard[wsl] = ("pe", pe_proj)

            if oc < 8:
                P.step("scalar", (lambda e, bank=bank, nw=nw: e.activation(
                    qstage[:, 0:nw], pb[bank][:, 0:nw], AF.Copy, scale=0.125)),
                    waits=[("pe", pe_proj), S1["qst"]])
                ac_q = P.c("ac")
                bank_free[bank] = ("ac", ac_q)
                P.step("vector", (lambda e, nw=nw: e.tensor_copy(
                    qsh[:, 0:nw], qstage[:, 0:nw])),
                    waits=[("ac", ac_q), S1["qst"]])
                P.step("vector", (lambda e, nw=nw: e.tensor_sub(
                    qsl[:, 0:nw], qstage[:, 0:nw], qsh[:, 0:nw])))
                dv_qs = P.c("dv")
                P.step("sync", (lambda e, oc=oc, c0=c0, nw=nw:
                       e.dma_start(
                           out=qspA.rearrange("(h r) c -> h r c", r=128)
                               [2 * oc:2 * oc + 2, 0:64, c0:c0 + nw],
                           in_=qsh[:, 0:nw])),
                       waits=[("dv", dv_qs)], dma="spq")
                P.step("sync", (lambda e, oc=oc, c0=c0, nw=nw:
                       e.dma_start(
                           out=qspA.rearrange("(h r) c -> h r c", r=128)
                               [2 * oc:2 * oc + 2, 64:128, c0:c0 + nw],
                           in_=qsl[:, 0:nw])),
                       waits=[("dv", dv_qs)], dma="spq")
                S1["qst"] = ("spq", P.c("spq"))
            elif oc < 16:
                kc_ = oc - 8
                if kind == "apx":
                    P.step("vector", (lambda e, bank=bank, kc_=kc_:
                           e.tensor_mul(kprevs[:, kc_ * NB:kc_ * NB + NB],
                                        pb[bank][:, 0:NB], bmask[:])),
                           waits=[("pe", pe_proj), ("pre", ld_pre)])
                    bank_free[bank] = ("dv", P.c("dv"))
                else:
                    P.step("vector", (lambda e, bank=bank, nw=nw:
                           e.tensor_copy(kstage[:, 0:nw], pb[bank][:, 0:nw])),
                           waits=[("pe", pe_proj), S1["kst"]])
                    dv_kcp = P.c("dv")
                    bank_free[bank] = ("dv", dv_kcp)
                    P.step("vector", (lambda e, nw=nw: e.tensor_sub(
                        ksm[:, 1:nw], kstage[:, 0:nw - 1], kstage[:, 1:nw])),
                        waits=[("ac", ac_sig)])
                    P.step("vector", (lambda e, kc_=kc_, nw=nw:
                           e.scalar_tensor_tensor(
                               ksm[:, 1:nw], ksm[:, 1:nw],
                               smear_s[:, kc_:kc_ + 1], kstage[:, 1:nw],
                               ALU.mult, ALU.add)))
                    blk0 = c0 // 128
                    for bi2 in range(nw // 128):
                        cb = bi2 * 128
                        ap_ = kc_ * NB + blk0 + bi2
                        P.step("vector", (lambda e, cb=cb, ap_=ap_:
                               e.tensor_sub(ksm[:, cb:cb + 1],
                                            kprevs[:, ap_:ap_ + 1],
                                            kstage[:, cb:cb + 1])))
                        P.step("vector", (lambda e, cb=cb, kc_=kc_:
                               e.scalar_tensor_tensor(
                                   ksm[:, cb:cb + 1], ksm[:, cb:cb + 1],
                                   smear_s[:, kc_:kc_ + 1],
                                   kstage[:, cb:cb + 1],
                                   ALU.mult, ALU.add)))
                    P.step("vector", (lambda e, nw=nw: e.tensor_copy(
                        ksmh[:, 0:nw], ksm[:, 0:nw])))
                    P.step("vector", (lambda e, nw=nw: e.tensor_sub(
                        ksml[:, 0:nw], ksm[:, 0:nw], ksmh[:, 0:nw])))
                    dv_smr = P.c("dv")
                    P.step("scalar", (lambda e, kc_=kc_, c0=c0, nw=nw:
                           e.dma_start(
                               out=kspA.rearrange("(h r) c -> h r c", r=64)
                                   [2 * kc_:2 * kc_ + 2, :, c0:c0 + nw],
                               in_=ksmh[:, 0:nw])),
                           waits=[("dv", dv_smr)], dma="sps")
                    P.step("scalar", (lambda e, kc_=kc_, c0=c0, nw=nw:
                           e.dma_start(
                               out=kspB.rearrange("(h r) c -> h r c", r=68)
                                   [2 * kc_:2 * kc_ + 2, 4:68, c0:c0 + nw],
                               in_=ksml[:, 0:nw])),
                           waits=[("dv", dv_smr)], dma="sps")
                    S1["kst"] = ("sps", P.c("sps"))
            else:
                vc_ = oc - 16
                if kind == "apx":
                    bank_free[bank] = ("pe", pe_proj)
                else:
                    P.step("vector", (lambda e, bank=bank, nw=nw:
                           e.tensor_copy(vstage[:, 0:nw], pb[bank][:, 0:nw])),
                           waits=[("pe", pe_proj), S1["vst"]])
                    dv_vcp = P.c("dv")
                    bank_free[bank] = ("dv", dv_vcp)
                    vt_dv = []
                    prev_tr_dv = None
                    for bi2 in range(nw // 128):
                        P.step("tensor", (lambda e, bi2=bi2: e.transpose(
                            pb[6][:, 0:128],
                            vstage[:, bi2 * 128:bi2 * 128 + 128], ident[:])),
                            waits=[("dv", dv_vcp), ("pre", ld_pre),
                                   S1["vst"] if bi2 == 0 else None,
                                   ("dv", prev_tr_dv)])
                        pe_tr = P.c("pe")
                        P.step("vector", (lambda e, bi2=bi2: e.tensor_copy(
                            vtstage[:, bi2 * 128:bi2 * 128 + 128],
                            pb[6][:, 0:128])), waits=[("pe", pe_tr)])
                        vt_dv.append(P.c("dv"))
                        prev_tr_dv = vt_dv[-1]
                        S1["vt_evac"] = vt_dv[-1]
                    blk0v = c0 // 128
                    for hp in range(2):
                        hgl = 2 * vc_ + hp
                        P.step("gpsimd", (lambda e, hgl=hgl, hp=hp,
                                          blk0v=blk0v, nw=nw: e.dma_start(
                               out=vspill.rearrange("(r p) c -> p r c", p=128)
                                   [:, hgl * NB + blk0v:
                                    hgl * NB + blk0v + nw // 128, :],
                               in_=vtstage[:, 0:nw]
                                   .rearrange("p (b q) -> p b q", q=128)
                                   [:, :, hp * 64:hp * 64 + 64])),
                               waits=[("dv", vt_dv[-1])], dma="spv")
                    S1["vst"] = ("spv", P.c("spv"))
        tile_consumed[pos] = pe_last
        sps_at[ti] = P.c("sps")
        spq_at[ti] = P.c("spq")
        spv_at[ti] = P.c("spv")

    sp_aux = P.c("sps")
    P.step("vector", lambda e: e.memset(vbuf[0][:, 64:NB * 65:65], 1.0),
           waits=[("dv", dv_pre), ("sps", sp_aux)])
    P.step("vector", lambda e: e.memset(vbuf[1][:, 64:NB * 65:65], 1.0))
    dv_vones = P.c("dv")

    buf_free = {0: None, 1: None}
    pbuf_free = {0: None, 1: None}
    psS_free = {0: None, 1: None}

    conv_tasks = ([("g", i, qq) for i in range(8) for qq in range(4)]
                  + [("u", i, qq) for i in range(8) for qq in range(4)]
                  + [("d", i, qq) for i in range(24) for qq in range(2)])
    conv_guard = {}

    def emit_conv(idx):
        kind, i, qq = conv_tasks[idx]
        par = idx % 2
        src_t, dst_t = {"g": (wgTi, wgb), "u": (wuTi, wub),
                        "d": (wdTi, wdb)}[kind]
        ncols = FF // 4 if kind != "d" else D // 2
        c0c = qq * ncols
        r0 = i * 128
        P.step("sync", (lambda e: e.dma_start(
            out=wcv_i[par][:, 0:ncols],
            in_=src_t[r0:r0 + 128, c0c:c0c + ncols])),
            waits=[("dv", conv_guard.get(("i", par)))],
            dma=("g3a" if par == 0 else "g3b"))
        ld_c = ("g3a" if par == 0 else "g3b",
                P.c("g3a" if par == 0 else "g3b"))
        P.step("vector", (lambda e: e.tensor_copy(
            wcv_o[par][:, 0:ncols], wcv_i[par][:, 0:ncols])),
            waits=[ld_c, ("wcv", conv_guard.get(("o", par)))])
        conv_guard[("i", par)] = P.c("dv")
        P.step("gpsimd", (lambda e: e.dma_start(
            out=dst_t[r0:r0 + 128, c0c:c0c + ncols],
            in_=wcv_o[par][:, 0:ncols])),
            waits=[("dv", conv_guard[("i", par)])], dma="wcv", chain=False)
        conv_guard[("o", par)] = P.c("wcv")

    hts = [(t, h) for t in range(4) for h in range(H)]
    S2 = {"psO_free": None, "ostg_free": None}

    def emit_li(li):
        t, h = hts[li]
        bi = li % 2
        for cidx in (2 * li, 2 * li + 1):
            if cidx < len(conv_tasks):
                emit_conv(cidx)
        nk = (4 * t + 4) * 128
        njt2 = 4 * t + 4
        njt = 2 * njt2

        ksem = "k2a" if bi == 0 else "k2b"
        for dup in range(2):
            P.step("sync", (lambda e, h=h, nk=nk, bi=bi, dup=dup: e.dma_start(
                out=kbufA[bi][dup * 64:dup * 64 + 64, 0:2 * nk]
                    .rearrange("p (s c) -> p s c", s=2),
                in_=kspA[h * 64:h * 64 + 64, :]
                    .rearrange("r (s c) -> r s c", s=2)[:, :, 0:nk])),
                waits=[("sps", sps_at[5 + t]), ("dv", dv_vones),
                       buf_free[bi]], dma=ksem)
        P.step("sync", (lambda e, h=h, nk=nk, bi=bi: e.dma_start(
            out=kbufB[bi][0:68, 0:2 * nk].rearrange("p (s c) -> p s c", s=2),
            in_=kspB[h * 68:h * 68 + 68, :]
                .rearrange("r (s c) -> r s c", s=2)[:, :, 0:nk])),
            waits=[buf_free[bi]], dma=ksem)
        qsem = "v2a" if bi == 0 else "v2b"
        P.step("gpsimd", (lambda e, h=h, njt2=njt2, bi=bi: e.dma_start(
            out=vbuf[bi][:, 0:njt2 * 65]
                .rearrange("p (j c) -> p j c", c=65)[:, :, 0:DH],
            in_=vspill[(h * NB) * 128:(h * NB + njt2) * 128, :]
                .rearrange("(j p) c -> p j c", p=128))),
            waits=[("spv", spv_at[5 + t]), ("dv", dv_vones), buf_free[bi]],
            dma=qsem)
        P.step("gpsimd", (lambda e, h=h, njt2=njt2, bi=bi: e.dma_start(
            out=vbuf[bi][:, njt2 * 65:2 * njt2 * 65]
                .rearrange("p (j c) -> p j c", c=65)[:, :, 0:DH],
            in_=vspill[(h * NB + 16) * 128:(h * NB + 16 + njt2) * 128, :]
                .rearrange("(j p) c -> p j c", p=128))),
            waits=[buf_free[bi]], dma=qsem)
        P.step("gpsimd", (lambda e, h=h, t=t, bi=bi: e.dma_start(
            out=qbufA[bi][:, :],
            in_=qspA[h * 128:h * 128 + 128, t * TT:(t + 1) * TT])),
            waits=[("spq", spq_at[1 + t]), buf_free[bi]], dma=qsem)
        P.step("gpsimd", (lambda e, h=h, t=t, bi=bi: e.dma_start(
            out=qbufB[bi][0:4, :],
            in_=qspB[h * 4:h * 4 + 4, t * TT:(t + 1) * TT])),
            waits=[buf_free[bi]], dma=qsem)
        P.step("gpsimd", (lambda e, h=h, t=t, bi=bi: e.dma_start(
            out=qbufB[bi][4:68, :],
            in_=qspA[h * 128:h * 128 + 64, t * TT:(t + 1) * TT])),
            waits=[buf_free[bi]], dma=qsem)
        ld_set = P.now("k2a" if bi == 0 else "k2b", "v2a" if bi == 0 else "v2b")

        exp_done = {}
        pv_done = {}

        def emit_S(j, bi=bi, t=t, njt2=njt2, ld_set=ld_set):
            sbk = j % 2
            own = j < njt2
            m = j if own else j - njt2
            diag = (m - 4 * t) if (4 * t <= m < 4 * t + 4) else None
            def smm(e, j=j, sbk=sbk, bi=bi):
                e.matmul(pb[sbk][:, :],
                         kbufA[bi][:, j * 128:(j + 1) * 128],
                         qbufA[bi][:, :], start=True, stop=False)
                return e.matmul(pb[sbk][:, :],
                                kbufB[bi][0:68, j * 128:(j + 1) * 128],
                                qbufB[bi][0:68, :], start=False, stop=True)
            P.step("tensor", smm,
                waits=ld_set +
                      [("ac", exp_done.get(j - 2)) if j >= 2 else psS_free[sbk]])
            pe_s = P.c("pe")
            dv_mask = None
            npre = min(m - 4 * t, 4) if m > 4 * t else 0
            if npre > 0:
                P.step("vector", (lambda e, sbk=sbk, npre=npre: e.memset(
                    pb[sbk][:, 0:npre * 128], NEG)),
                    waits=[("pe", pe_s)])
                dv_mask = P.c("dv")
            if diag is not None:
                mk = maskA if own else maskB
                P.step("vector", (lambda e, sbk=sbk, diag=diag, mk=mk:
                       e.tensor_add(pb[sbk][:, diag * 128:(diag + 1) * 128],
                                    pb[sbk][:, diag * 128:(diag + 1) * 128],
                                    mk[:])),
                       waits=[("pe", pe_s), ("pre", ld_pre)])
                dv_mask = P.c("dv")
            P.step("scalar", (lambda e, sbk=sbk: e.activation(
                pbuf[sbk][:], pb[sbk][:], AF.Exp, bias=consts[:, 1:2])),
                waits=[("pe", pe_s), ("dv", dv_pre), ("dv", dv_mask),
                       ("pe", pv_done.get(j - 2)) if j >= 2 else pbuf_free[sbk]])
            exp_done[j] = P.c("ac")

        def emit_PV(j, bi=bi, njt=njt):
            sbk = j % 2
            P.step("tensor", (lambda e, j=j, sbk=sbk, bi=bi, njt=njt: e.matmul(
                pb[2][0:65, :], vbuf[bi][:, j * 65:j * 65 + 65], pbuf[sbk][:],
                start=(j == 0), stop=(j == njt - 1))),
                waits=[("ac", exp_done[j]),
                       S2["psO_free"] if j == 0 else None])
            pv_done[j] = P.c("pe")

        emit_S(0)
        for j in range(1, njt):
            emit_S(j)
            emit_PV(j - 1)
        emit_PV(njt - 1)

        P.step("vector", (lambda e: e.reciprocal(rs_s[64:65, :],
                                                 pb[2][64:65, :])),
               waits=[("pe", pv_done[njt - 1]), S2["ostg_free"]])
        dv_rc = P.c("dv")
        P.step("tensor", (lambda e: e.matmul(
            pb[3][0:64, :], ones128[64:65, 0:64], rs_s[64:65, :],
            start=True, stop=True)), waits=[("dv", dv_rc)])
        pe_bc = P.c("pe")
        P.step("vector", lambda e: e.tensor_copy(bc_s[:], pb[3][0:64, :]),
               waits=[("pe", pe_bc)])
        P.step("vector", lambda e: e.tensor_mul(ostg[:], pb[2][0:64, :],
                                                bc_s[:]))
        dv_div = P.c("dv")
        S2["psO_free"] = ("dv", dv_div)
        P.step("scalar", (lambda e, h=h, t=t: e.dma_start(
            out=ospill[h * 64:(h + 1) * 64, t * TT:(t + 1) * TT],
            in_=ostg[:])), waits=[("dv", dv_div)], dma="spo")
        S2["ostg_free"] = ("spo", P.c("spo"))

        buf_free[bi] = ("pe", pv_done[njt - 1])
        pbuf_free[0] = ("pe", pv_done[njt - 2])
        pbuf_free[1] = ("pe", pv_done[njt - 1])
        psS_free[0] = ("ac", exp_done[njt - 2])
        psS_free[1] = ("ac", exp_done[njt - 1])

    order = [(0, None), (1, None), (5, []), (2, range(0, 4)),
             (6, range(4, 8)), (3, range(8, 12)), (7, range(12, 16)),
             (4, range(16, 24)), (8, range(24, 32))]
    pos = 0
    for ti, lis in order:
        emit_tile(ti, pos)
        pos += 1
        if lis:
            for li in lis:
                emit_li(li)
    for li in range(32, 64):
        emit_li(li)

    ph2_spo = P.c("spo")
    conv_total = P.c("wcv")

    state = {"statsv": None}

    def ln_stats(src_s, waits0):
        pe_sq2 = None
        for cc in range(8):
            P.step("scalar", (lambda e, cc=cc, src_s=src_s: e.activation(
                sqc3[:], src_s[:, cc * TT:(cc + 1) * TT], AF.Square)),
                waits=list(waits0) + [("pe", pe_sq2)])
            ac_sq2 = P.c("ac")
            P.step("tensor", (lambda e, cc=cc: e.matmul(
                pb[3][:], ones128[:], sqc3[:],
                start=(cc == 0), stop=(cc == 7))),
                waits=[("ac", ac_sq2),
                       ("dv", state["statsv"]) if cc == 0 else None])
            pe_sq2 = P.c("pe")
        P.step("tensor", (lambda e, src_s=src_s: [
            e.matmul(pb[2][:], ones128[:], src_s[:, cc * TT:(cc + 1) * TT],
                     start=(cc == 0), stop=(cc == 7)) for cc in range(8)][-1]),
            waits=list(waits0) + [("dv", state["statsv"]), ("dv", dv_pre)])
        pe_cs2 = P.c("pe")
        P.step("vector", lambda e: e.tensor_scalar(
            mean2[:], pb[2][:], 1.0 / D, None, ALU.mult),
            waits=[("pe", pe_cs2), ("pe", pe_sq2)])
        P.step("vector", lambda e: e.tensor_scalar(
            tmp2[:], pb[2][:], 1.0 / D, None, ALU.mult))
        P.step("vector", lambda e: e.tensor_mul(tmp2[:], mean2[:], tmp2[:]))
        P.step("vector", lambda e: e.scalar_tensor_tensor(
            rstd2[:], pb[3][:], 1.0 / D, tmp2[:], ALU.mult, ALU.subtract))
        P.step("vector", lambda e: e.tensor_scalar(
            rstd2[:], rstd2[:], EPS, None, ALU.add))
        P.step("vector", lambda e: e.reciprocal(rstd2[:], rstd2[:]))
        dv_sv = P.c("dv")
        state["statsv"] = dv_sv
        P.step("scalar", lambda e: e.activation(rstd2[:], rstd2[:], AF.Sqrt),
               waits=[("dv", dv_sv)])
        return P.c("ac")

    prev_pe_op = None
    prev_dv_t1 = None
    prev_out = None
    prev_ffn_pp = []
    prev_wg_mm = {}
    prev_ds = None

    for tt_i in range(4):
        c0 = tt_i * TT

        P.step("sync", (lambda e, c0=c0: e.dma_start(
            out=ot_s[:].rearrange("p (k c) -> p k c", k=8),
            in_=ospill[:, c0:c0 + TT].rearrange("(k p) c -> p k c", p=128))),
            waits=[("spo", ph2_spo), ("pe", prev_pe_op)], dma="o3")
        P.step("sync", (lambda e, c0=c0: e.dma_start(
            out=xo_s[:].rearrange("p (k c) -> p k c", k=8),
            in_=xT[:, c0:c0 + TT].rearrange("(k p) c -> p k c", p=128))),
            waits=[("dv", prev_dv_t1)], dma="o3")
        ld_o = ("o3", P.c("o3"))

        po_ev = {}
        pe_o2 = {}
        ld_wo = {}
        for oc in range(8):
            P.step("sync", (lambda e, oc=oc: e.dma_start(
                out=wo_s[oc % 2][:].rearrange("p (k c) -> p k c", k=8),
                in_=woutT[:, oc * 128:(oc + 1) * 128]
                    .rearrange("(k p) c -> p k c", p=128))),
                waits=[("pe", pe_o2.get(oc - 2, prev_pe_op))],
                dma=("w3a" if oc % 2 == 0 else "w3b"))
            ld_wo[oc] = ("w3a" if oc % 2 == 0 else "w3b", P.c("w3a" if oc % 2 == 0 else "w3b"))
            P.step("tensor", (lambda e, oc=oc: [
                e.matmul(pb[oc % 2][:],
                         wo_s[oc % 2][:, kc * 128:(kc + 1) * 128],
                         ot_s[:, kc * TT:(kc + 1) * TT],
                         start=(kc == 0), stop=(kc == 7))
                for kc in range(8)][-1]),
                waits=[ld_o, ld_wo[oc],
                       ("dv", po_ev.get(oc - 2, prev_dv_t1))] + prev_ffn_pp)
            pe_o2[oc] = P.c("pe")
            P.step("vector", (lambda e, oc=oc: e.tensor_copy(
                po_s[:, oc * TT:(oc + 1) * TT], pb[oc % 2][:])),
                waits=[("pe", pe_o2[oc])])
            po_ev[oc] = P.c("dv")
        prev_pe_op = pe_o2[7]

        ac_r2 = ln_stats(po_s, [("dv", po_ev[7]), ("out", prev_out)])
        for cc in range(8):
            P.step("vector", (lambda e, cc=cc: e.tensor_sub(
                t1_s[:, cc * TT:(cc + 1) * TT], po_s[:, cc * TT:(cc + 1) * TT],
                mean2[:])), waits=[("ac", ac_r2)])
            P.step("vector", (lambda e, cc=cc: e.tensor_mul(
                t1_s[:, cc * TT:(cc + 1) * TT], t1_s[:, cc * TT:(cc + 1) * TT],
                rstd2[:])))
            P.step("vector", (lambda e, cc=cc: e.tensor_scalar(
                t1_s[:, cc * TT:(cc + 1) * TT], t1_s[:, cc * TT:(cc + 1) * TT],
                g2s[:, cc:cc + 1], b2s[:, cc:cc + 1], ALU.mult, ALU.add)))
            P.step("vector", (lambda e, cc=cc: e.tensor_add(
                t1_s[:, cc * TT:(cc + 1) * TT], t1_s[:, cc * TT:(cc + 1) * TT],
                xo_s[:, cc * TT:(cc + 1) * TT])), waits=[ld_o])
        dv_t1 = P.c("dv")
        prev_dv_t1 = dv_t1

        ac_r3 = ln_stats(t1_s, [("dv", dv_t1)])
        for cc in range(8):
            P.step("vector", (lambda e, cc=cc: e.tensor_sub(
                po_s[:, cc * TT:(cc + 1) * TT], t1_s[:, cc * TT:(cc + 1) * TT],
                mean2[:])), waits=[("ac", ac_r3)])
            P.step("vector", (lambda e, cc=cc: e.tensor_mul(
                po_s[:, cc * TT:(cc + 1) * TT], po_s[:, cc * TT:(cc + 1) * TT],
                rstd2[:])))
        dv_x1 = P.c("dv")

        for cc in range(8):
            P.step("vector", (lambda e, cc=cc: e.tensor_copy(
                x1h[:, cc * TT:(cc + 1) * TT],
                po_s[:, cc * TT:(cc + 1) * TT])), waits=[("dv", dv_x1)])
            P.step("vector", (lambda e, cc=cc: e.tensor_sub(
                x1l[:, cc * TT:(cc + 1) * TT], po_s[:, cc * TT:(cc + 1) * TT],
                x1h[:, cc * TT:(cc + 1) * TT])))
        dv_x1s = P.c("dv")

        umm_done = {}
        sil_done = {}
        hmul_done = {}
        wg_mm = {}
        ds_ev_last = None
        ds_add_last = None
        for half in range(2):
            for ci in range(12):
                hc = half * 12 + ci
                wb = hc % 2
                wprev = wg_mm.get(hc - 2, prev_wg_mm.get(22 + hc))
                P.step("sync", (lambda e, hc=hc, wb=wb: e.dma_start(
                    out=wgb_s[wb][:].rearrange("p (k c) -> p k c", k=8),
                    in_=wgb[:, hc * 128:(hc + 1) * 128]
                        .rearrange("(k p) c -> p k c", p=128))),
                    waits=[("pe", wprev), ("wcv", conv_total)],
                    dma=("g3a" if wb == 0 else "g3b"))
                P.step("sync", (lambda e, hc=hc, wb=wb: e.dma_start(
                    out=wub_s[wb][:].rearrange("p (k c) -> p k c", k=8),
                    in_=wub[:, hc * 128:(hc + 1) * 128]
                        .rearrange("(k p) c -> p k c", p=128))),
                    dma=("g3a" if wb == 0 else "g3b"))
                ld_gu = ("g3a" if wb == 0 else "g3b",
                         P.c("g3a" if wb == 0 else "g3b"))

                P.step("tensor", (lambda e, wb=wb: [
                    e.matmul(pb[0][:], wgb_s[wb][:, kc * 128:(kc + 1) * 128],
                             xs_[:, kc * TT:(kc + 1) * TT],
                             start=(pi == 0 and kc == 0),
                             stop=(pi == 1 and kc == 7))
                    for pi, xs_ in enumerate((x1h, x1l)) for kc in range(8)][-1]),
                    waits=[ld_gu, ("dv", dv_x1s),
                           ("ac", sil_done.get(hc - 1)),
                           ("dv", po_ev[7] if hc == 0 else None)] +
                          (prev_ffn_pp if hc == 0 else []))
                pe_g = P.c("pe")
                P.step("tensor", (lambda e, wb=wb: [
                    e.matmul(pb[1][:], wub_s[wb][:, kc * 128:(kc + 1) * 128],
                             xs_[:, kc * TT:(kc + 1) * TT],
                             start=(pi == 0 and kc == 0),
                             stop=(pi == 1 and kc == 7))
                    for pi, xs_ in enumerate((x1h, x1l)) for kc in range(8)][-1]),
                    waits=[ld_gu, ("dv", dv_x1s), ("dv", hmul_done.get(hc - 1))])
                umm_done[hc] = P.c("pe")
                wg_mm[hc] = umm_done[hc]

                P.step("scalar", (lambda e: e.activation(
                    gs_s[:], pb[0][:], AF.Silu, scale=1.0 / 127.0)),
                    waits=[("pe", pe_g), ("dv", hmul_done.get(hc - 1))])
                sil_done[hc] = P.c("ac")
                P.step("vector", (lambda e: e.tensor_mul(
                    htmp[:], pb[1][:], gs_s[:])),
                    waits=[("ac", sil_done[hc]), ("pe", umm_done[hc])])
                P.step("vector", (lambda e, ci=ci: e.tensor_copy(
                    hh_s[:, ci * TT:(ci + 1) * TT], htmp[:])))
                P.step("vector", (lambda e, ci=ci: e.tensor_sub(
                    hl_s[:, ci * TT:(ci + 1) * TT], htmp[:],
                    hh_s[:, ci * TT:(ci + 1) * TT])))
                hmul_done[hc] = P.c("dv")

            for dcg in range(2):
                dmm_last = state.get("dmm_prev")
                for ci in range(12):
                    hc = half * 12 + ci
                    wb = hc % 2
                    P.step("sync", (lambda e, hc=hc, wb=wb, dcg=dcg:
                           e.dma_start(
                               out=wdb_s[wb][:],
                               in_=wdb[hc * 128:(hc + 1) * 128,
                                       dcg * TT:(dcg + 1) * TT])),
                           waits=[("pe", dmm_last), ("wcv", conv_total)],
                           dma=("d3a" if wb == 0 else "d3b"))
                    ld_d = ("d3a" if wb == 0 else "d3b",
                            P.c("d3a" if wb == 0 else "d3b"))

                    P.step("tensor", (lambda e, ci=ci, wb=wb: [
                        e.matmul(pb[4 + dq][:],
                                 wdb_s[wb][:, dq * 128:(dq + 1) * 128],
                                 hs_[:, ci * TT:(ci + 1) * TT],
                                 start=(ci == 0 and pi == 0),
                                 stop=(ci == 11 and pi == 1))
                        for dq in range(4) for pi, hs_ in
                        enumerate((hh_s, hl_s))][-1]),
                        waits=[ld_d,
                               ("dv", hmul_done[half * 12 + 11]),
                               ("ac", ds_ev_last) if (ci == 0 and half == 0 and dcg == 1) else None,
                               ("dv", ds_add_last) if (ci == 0 and half == 1 and dcg == 1) else None,
                               ("dv", prev_ds) if (ci == 0 and half == 0 and dcg == 0) else None,
                               ("ac", ds_ev_last) if (ci == 0 and half == 1 and dcg == 0) else None])
                    dmm_last = P.c("pe")
                state["dmm_prev"] = dmm_last
                if half == 0:
                    for dq in range(4):
                        dc = dcg * 4 + dq
                        P.step("scalar", (lambda e, dq=dq, dc=dc: e.activation(
                            scr_s[:, dc * TT:(dc + 1) * TT], pb[4 + dq][:],
                            AF.Copy, scale=DSCALE)),
                            waits=[("pe", dmm_last), ("out", prev_out)])
                    ds_ev_last = P.c("ac")
                else:
                    for dq in range(4):
                        dc = dcg * 4 + dq
                        P.step("vector", (lambda e, dq=dq, dc=dc:
                               e.scalar_tensor_tensor(
                                   scr_s[:, dc * TT:(dc + 1) * TT],
                                   pb[4 + dq][:], DSCALE,
                                   scr_s[:, dc * TT:(dc + 1) * TT],
                                   ALU.mult, ALU.add)),
                               waits=[("pe", dmm_last), ("ac", ds_ev_last)])
                    ds_add_last = P.c("dv")
        prev_wg_mm = dict(wg_mm)
        prev_ffn_pp = [("ac", sil_done[23]), ("dv", hmul_done[23])]

        for cc in range(8):
            P.step("vector", (lambda e, cc=cc: e.tensor_add(
                t1_s[:, cc * TT:(cc + 1) * TT], po_s[:, cc * TT:(cc + 1) * TT],
                scr_s[:, cc * TT:(cc + 1) * TT])),
                waits=[("dv", ds_add_last), ("dv", dv_x1)])
        dv_t2 = P.c("dv")
        prev_ds = dv_t2

        ac_r4 = ln_stats(t1_s, [("dv", dv_t2)])
        for cc in range(8):
            P.step("vector", (lambda e, cc=cc: e.tensor_sub(
                scr_s[:, cc * TT:(cc + 1) * TT], t1_s[:, cc * TT:(cc + 1) * TT],
                mean2[:])), waits=[("ac", ac_r4)])
            P.step("vector", (lambda e, cc=cc: e.tensor_mul(
                scr_s[:, cc * TT:(cc + 1) * TT], scr_s[:, cc * TT:(cc + 1) * TT],
                rstd2[:])))
        dv_y = P.c("dv")

        P.step("scalar", (lambda e, c0=c0: e.dma_start(
            out=yT[:, c0:c0 + TT].rearrange("(k p) c -> p k c", p=128),
            in_=scr_s[:].rearrange("p (k c) -> p k c", k=8))),
            waits=[("dv", dv_y)], dma="out")
        prev_out = P.c("out")

    out_total = P.c("out")

    def emitter(ops, final_wait=None):
        def run(eng):
            for fn, waits, inc in ops:
                for s, v in waits:
                    eng.wait_ge(SEMS[s], v)
                hh = fn(eng)
                if inc is not None:
                    if isinstance(hh, list):
                        hh = hh[-1]
                    hh.then_inc(SEMS[inc[0]], inc[1])
            if final_wait is not None:
                eng.wait_ge(SEMS[final_wait[0]], final_wait[1])
        return run

    with nc.Block() as block:
        block.sync(emitter(P.q["sync"], final_wait=("out", out_total)))
        block.tensor(emitter(P.q["tensor"]))
        block.vector(emitter(P.q["vector"]))
        block.scalar(emitter(P.q["scalar"]))
        block.gpsimd(emitter(P.q["gpsimd"]))
    for cm in reversed(sem_ctx):
        cm.__exit__(None, None, None)
    return nc



def _core_layout(r):
    own = [2 * s + r for s in range(16)]
    other = [2 * s + (1 - r) for s in range(16)]
    blocks = own + other
    perm = np.concatenate([np.arange(128 * g, 128 * g + 128) for g in blocks])
    apx = np.array([max(128 * g - 1, 0) for g in blocks], dtype=np.int64)
    apx_valid = np.array([1.0 if g > 0 else 0.0 for g in blocks],
                         dtype=np.float32)
    return blocks, perm, apx, apx_valid


def kernel(L0_attn_x, L0_attn_ln1_g, L0_attn_ln1_b, L0_attn_in_w,
           L0_attn_out_w, L0_attn_ln2_g, L0_attn_ln2_b, L0_attn_slopes,
           L0_attn_smear, L0_ffn_w_gate, L0_ffn_w_up, L0_ffn_w_down):
    x = np.ascontiguousarray(np.asarray(L0_attn_x, dtype=np.float32))
    winT = np.ascontiguousarray(np.asarray(L0_attn_in_w, np.float32).T)
    woutT = np.ascontiguousarray(np.asarray(L0_attn_out_w, np.float32).T)
    wgT = np.ascontiguousarray(np.asarray(L0_ffn_w_gate, np.int32).T)
    wuT = np.ascontiguousarray(np.asarray(L0_ffn_w_up, np.int32).T)
    wdT = np.ascontiguousarray(np.asarray(L0_ffn_w_down, np.int32).T)
    g1c = np.ascontiguousarray(np.asarray(L0_attn_ln1_g, np.float32)
                               .reshape(8, 128).T)
    b1c = np.ascontiguousarray(np.asarray(L0_attn_ln1_b, np.float32)
                               .reshape(8, 128).T)
    g2c = np.ascontiguousarray(np.asarray(L0_attn_ln2_g, np.float32)
                               .reshape(8, 128).T)
    b2c = np.ascontiguousarray(np.asarray(L0_attn_ln2_b, np.float32)
                               .reshape(8, 128).T)
    slopes = np.ascontiguousarray(
        np.asarray(L0_attn_slopes, np.float32).reshape(H, 1))
    smear = np.asarray(L0_attn_smear, np.float32)
    smearb = np.ascontiguousarray(np.repeat(smear, 64).reshape(8, 128).T)
    maskA = np.where(np.arange(128)[:, None] > np.arange(128)[None, :],
                     np.float32(NEG), np.float32(0.0)).astype(np.float32)

    nc = build_nc()
    in_maps = []
    metas = []
    for c in range(8):
        b, r = c // 2, c % 2
        blocks, perm, apx, apx_valid = _core_layout(r)
        xTb = np.ascontiguousarray(x[b].T)
        xcore = xTb[:, perm]
        xapx = np.zeros((D, 128), dtype=np.float32)
        xapx[:, 0:NB] = xTb[:, apx]
        xin = np.ascontiguousarray(np.concatenate([xcore, xapx], axis=1))
        kpos = np.ascontiguousarray(
            np.broadcast_to(perm.astype(np.float32)[None, :], (16, L)))
        maskB = (np.zeros((128, 128), np.float32) if r == 1
                 else np.full((128, 128), NEG, np.float32))
        bmask = np.ascontiguousarray(
            np.broadcast_to(apx_valid[None, :], (128, NB)).copy())
        in_maps.append({
            "xT": xin, "winT": winT, "woutT": woutT,
            "wgTi": wgT, "wuTi": wuT, "wdTi": wdT,
            "g1i": g1c, "b1i": b1c, "g2i": g2c, "b2i": b2c,
            "slopesi": slopes, "smeari": smearb, "kposi": kpos,
            "maskAi": maskA, "maskBi": maskB, "bmaski": bmask,
        })
        metas.append((b, perm))

    res = run_bass_kernel_spmd(nc, in_maps, core_ids=list(range(8)))

    out = np.empty((4, L, D), dtype=np.float32)
    for c in range(8):
        b, perm = metas[c]
        yTc = np.asarray(res.results[c]["yT"])
        out[b, perm[:NQ], :] = yTc.T
    return out


# version 12
# speedup vs baseline: 1.1451x; 1.1451x over previous
import numpy as np

import concourse.bass as bass
import concourse.mybir as mybir
from concourse.bass_utils import run_bass_kernel_spmd

F32 = mybir.dt.float32
I32 = mybir.dt.int32
AF = mybir.ActivationFunctionType
ALU = mybir.AluOpType

D = 1024
H = 16
DH = 64
FF = 3072
L = 4096
NB = 32
TT = 512
NT = 8
NQ = 2048
NEG = -1.0e9
CSHIFT = -4.0
EPS = 1e-5
DSCALE = 1.0 / (127.0 * 127.0)
ENGS = ("sync", "tensor", "vector", "scalar", "gpsimd")
OWN = {"tensor": "pe", "vector": "dv", "scalar": "ac", "gpsimd": "gp"}
ALL_SEMS = ("pre", "x0", "x1", "k2a", "k2b", "v2a", "v2b", "o3",
            "w3a", "w3b", "g3a", "g3b", "d3a", "d3b",
            "sps", "spv", "spo", "wcv", "out", "pe", "dv", "ac", "gp")


class Plan:

    def __init__(self):
        self.cnt = {}
        self.q = {e: [] for e in ENGS}

    def c(self, s):
        return self.cnt.get(s, 0)

    ENG_SEMS = ("pe", "dv", "ac", "gp")

    def step(self, eng, fn, waits=(), dma=None, chain=True):
        ws = {}
        for item in waits:
            if item is None:
                continue
            s, v = item
            if v:
                ws[s] = max(ws.get(s, 0), v)
        inc = None
        if dma is not None:
            prior = self.cnt.get(dma, 0)
            if prior:
                ws[dma] = max(ws.get(dma, 0), prior)
            inc = (dma, 16)
        elif eng in OWN:
            own = OWN[eng]
            if chain and eng in ("vector", "scalar", "gpsimd"):
                prev = self.cnt.get(own, 0)
                if prev:
                    ws[own] = max(ws.get(own, 0), prev)
            inc = (own, 1)
        if inc:
            self.cnt[inc[0]] = self.cnt.get(inc[0], 0) + inc[1]
        self.q[eng].append((fn, sorted(ws.items()), inc))
        return self.cnt.get(inc[0], 0) if inc else 0

    def now(self, *sems):
        return [(s, self.cnt.get(s, 0)) for s in sems]


def build_nc(debug=False):
    nc = bass.Bass("TRN2", target_bir_lowering=False)

    xT = nc.dram_tensor("xT", [D, L + 128], F32, kind="ExternalInput").ap()
    winT = nc.dram_tensor("winT", [D, 3 * D], F32, kind="ExternalInput").ap()
    woutT = nc.dram_tensor("woutT", [D, D], F32, kind="ExternalInput").ap()
    wgTi = nc.dram_tensor("wgTi", [D, FF], I32, kind="ExternalInput").ap()
    wuTi = nc.dram_tensor("wuTi", [D, FF], I32, kind="ExternalInput").ap()
    wdTi = nc.dram_tensor("wdTi", [FF, D], I32, kind="ExternalInput").ap()
    g1i = nc.dram_tensor("g1i", [128, 8], F32, kind="ExternalInput").ap()
    b1i = nc.dram_tensor("b1i", [128, 8], F32, kind="ExternalInput").ap()
    g2i = nc.dram_tensor("g2i", [128, 8], F32, kind="ExternalInput").ap()
    b2i = nc.dram_tensor("b2i", [128, 8], F32, kind="ExternalInput").ap()
    slopesi = nc.dram_tensor("slopesi", [16, 1], F32, kind="ExternalInput").ap()
    smeari = nc.dram_tensor("smeari", [128, 8], F32, kind="ExternalInput").ap()
    kposi = nc.dram_tensor("kposi", [16, L], F32, kind="ExternalInput").ap()
    maskAi = nc.dram_tensor("maskAi", [128, 128], F32, kind="ExternalInput").ap()
    maskBi = nc.dram_tensor("maskBi", [128, 128], F32, kind="ExternalInput").ap()
    bmaski = nc.dram_tensor("bmaski", [128, NB], F32, kind="ExternalInput").ap()
    identi = nc.inline_tensor(np.eye(128, dtype=np.float32), name="identi").ap()
    yT = nc.dram_tensor("yT", [D, NQ], F32, kind="ExternalOutput").ap()

    spk = "ExternalOutput" if debug else "Internal"
    kspill = nc.dram_tensor("kspill", [H * 66, L], F32, kind=spk).ap()
    vspill = nc.dram_tensor("vspill", [H * NB * 128, DH], F32, kind=spk).ap()
    qspill = nc.dram_tensor("qspill", [H * 66, NQ], F32, kind=spk).ap()
    ospill = nc.dram_tensor("ospill", [D, NQ], F32, kind=spk).ap()
    BF16 = mybir.dt.bfloat16
    wgb = nc.dram_tensor("wgb", [D, FF], BF16, kind="Internal").ap()
    wub = nc.dram_tensor("wub", [D, FF], BF16, kind="Internal").ap()
    wdb = nc.dram_tensor("wdb", [FF, D], BF16, kind="Internal").ap()

    ARENA = 206 * 1024
    arena = nc.alloc_sbuf_tensor("arena", [128, ARENA], mybir.dt.uint8)
    base0 = nc.lookup_mloc(arena).addr
    cur = [base0]

    def place(name, shape, dtype=F32):
        nbytes = int(np.prod(shape[1:])) * mybir.dt.size(dtype)
        off = (cur[0] + 31) // 32 * 32
        cur[0] = off + nbytes
        assert cur[0] <= base0 + ARENA, f"arena overflow at {name}"
        return nc.alloc_sbuf_tensor_at(name, shape, dtype, offset=off).ap()

    consts = place("consts", [128, 4])
    ones128 = place("ones128", [128, 128])
    maskA = place("maskA", [128, 128])
    maskB = place("maskB", [128, 128])
    g1s = place("g1s", [128, 8]); b1s = place("b1s", [128, 8])
    g2s = place("g2s", [128, 8]); b2s = place("b2s", [128, 8])
    smear_s = place("smear_s", [128, 8])
    ident = place("ident", [128, 128])
    pbase = cur[0]

    wi_s = [place(f"wi_s{i}", [128, 8 * 128]) for i in range(2)]
    kpos_s = place("kpos_s", [16, L])
    slopes_s = place("slopes_s", [16, 1])
    bmask = place("bmask", [128, NB])
    xt = [place(f"xt{i}", [128, 8 * TT]) for i in range(2)]
    sqc = place("sqc", [128, TT])
    meanb = place("meanb", [128, TT])
    rstdb = place("rstdb", [128, TT])
    tmpb = place("tmpb", [128, TT])
    kstage = place("kstage", [128, TT])
    ksm = place("ksm", [128, TT])
    kprevs = place("kprevs", [128, 8 * NB])
    vstage = place("vstage", [128, TT])
    vtstage = place("vtstage", [128, TT])
    qstage = place("qstage", [128, TT])
    arow = place("arow", [16, L])
    qr64 = place("qr64", [16, NQ])
    qr65 = place("qr65", [16, NQ])

    kbuf = [place(f"kbuf{i}", [128, L]) for i in range(2)]
    vbuf = [place(f"vbuf{i}", [128, NB * 65]) for i in range(2)]
    qbuf = [place(f"qbuf{i}", [128, TT]) for i in range(2)]
    pbuf = [place(f"pbuf{i}", [128, TT]) for i in range(2)]
    rs_s = place("rs_s", [65, TT])
    bc_s = place("bc_s", [64, TT])
    ostg = place("ostg", [64, TT])
    wcv_i = [place(f"wcv_i{i}", [128, FF // 2], I32) for i in range(2)]
    wcv_o = [place(f"wcv_o{i}", [128, FF // 2], BF16) for i in range(2)]

    cur[0] = pbase
    ot_s = place("ot_s", [128, 8 * TT])
    xo_s = place("xo_s", [128, 8 * TT])
    po_s = place("po_s", [128, 8 * TT])
    t1_s = place("t1_s", [128, 8 * TT])
    scr_s = place("scr_s", [128, 8 * TT])
    hh_s = place("hh_s", [128, 12 * TT], BF16)
    hl_s = place("hl_s", [128, 12 * TT], BF16)
    htmp = place("htmp", [128, TT])
    x1h = place("x1h", [128, 8 * TT], BF16)
    x1l = place("x1l", [128, 8 * TT], BF16)
    wo_s = [place(f"wo_s{i}", [128, 8 * 128]) for i in range(2)]
    wgb_s = [place(f"wgb_s{i}", [128, 8 * 128], BF16) for i in range(2)]
    wub_s = [place(f"wub_s{i}", [128, 8 * 128], BF16) for i in range(2)]
    wdb_s = [place(f"wdb_s{i}", [128, 4 * TT], BF16) for i in range(2)]
    gs_s2 = [place(f"gs_s{i}", [128, TT]) for i in range(2)]
    mean2 = place("mean2", [128, TT])
    rstd2 = place("rstd2", [128, TT])
    tmp2 = place("tmp2", [128, TT])
    sqc3 = place("sqc3", [128, TT])

    pb = [nc.alloc_psum_tensor(f"pb{i}", [128, TT], F32).ap() for i in range(8)]

    SEMS = {}
    sem_ctx = []
    for s in ALL_SEMS:
        cm = nc.semaphore("s_" + s)
        SEMS[s] = cm.__enter__()
        sem_ctx.append(cm)

    P = Plan()

    for dst, src in ((g1s, g1i), (b1s, b1i), (g2s, g2i), (b2s, b2i),
                     (slopes_s, slopesi), (smear_s, smeari), (kpos_s, kposi),
                     (maskA, maskAi), (maskB, maskBi), (bmask, bmaski),
                     (ident, identi)):
        P.step("sync", (lambda e, dst=dst, src=src:
                        e.dma_start(out=dst[:], in_=src[:])), dma="pre")
    ld_pre = P.c("pre")

    P.step("vector", lambda e: e.memset(consts[:, 0:1], 1.0))
    P.step("vector", lambda e: e.memset(consts[:, 1:2], CSHIFT))
    P.step("vector", lambda e: e.memset(ones128[:], 1.0))
    dv_pre = P.c("dv")

    P.step("scalar", lambda e: e.activation(smear_s[:], smear_s[:], AF.Sigmoid),
           waits=[("pre", ld_pre)])
    ac_sig = P.c("ac")

    P.step("vector", lambda e: e.memset(qr64[:], 1.0),
           waits=[("pre", ld_pre)])
    P.step("vector", lambda e: e.tensor_copy(qr65[:], kpos_s[:, 0:NQ]),
           waits=[("pre", ld_pre)])
    dv_q45 = P.c("dv")
    P.step("scalar", (lambda e: e.dma_start(
        out=qspill.rearrange("(h r) c -> h r c", r=66)[:, 64:65, :],
        in_=qr64[:].rearrange("h (r c) -> h r c", r=1))),
        waits=[("dv", dv_q45)], dma="sps")
    P.step("scalar", (lambda e: e.dma_start(
        out=qspill.rearrange("(h r) c -> h r c", r=66)[:, 65:66, :],
        in_=qr65[:].rearrange("h (r c) -> h r c", r=1))),
        waits=[("dv", dv_q45)], dma="sps")

    P.step("vector", lambda e: e.tensor_scalar(
        arow[:], kpos_s[:], slopes_s[:, 0:1], None, ALU.mult),
        waits=[("pre", ld_pre)])
    dva = P.c("dv")
    P.step("scalar", (lambda e: e.dma_start(
        out=kspill.rearrange("(h r) c -> h r c", r=66)[:, 64:65, :],
        in_=arow[:].rearrange("h (r c) -> h r c", r=1))),
        waits=[("dv", dva)], dma="sps")
    sp_a1 = P.c("sps")
    P.step("vector", lambda e: e.tensor_scalar(
        arow[:], kpos_s[:], 0.0, 1.0, ALU.mult, ALU.add),
        waits=[("sps", sp_a1)])
    P.step("vector", lambda e: e.tensor_scalar(
        arow[:], arow[:], slopes_s[:, 0:1], -1.0, ALU.mult, ALU.mult))
    dva2 = P.c("dv")
    P.step("scalar", (lambda e: e.dma_start(
        out=kspill.rearrange("(h r) c -> h r c", r=66)[:, 65:66, :],
        in_=arow[:].rearrange("h (r c) -> h r c", r=1))),
        waits=[("dv", dva2)], dma="sps")

    tiles = [("apx", L, 128)] + [("main", t * TT, TT) for t in range(NT)]
    tile_consumed = {}
    bank_free = {4: None, 5: None}
    wi_guard = {}
    sps_at = {}
    spv_at = {}
    S1 = {"kst": None, "vst": None, "qst": None, "statsv": None,
          "vt_evac": None, "sq_pe": None}
    vtt_free = {}

    def emit_tile(ti, pos):
        kind, c0, nw = tiles[ti]
        buf = xt[pos % 2]
        is_own = (kind == "main" and c0 < NQ)
        oc_list = (list(range(8, 16)) if kind == "apx"
                   else (list(range(24)) if is_own else list(range(8, 24))))

        P.step("sync", (lambda e, buf=buf, c0=c0, nw=nw: e.dma_start(
            out=buf[:, 0:8 * nw].rearrange("p (k c) -> p k c", k=8),
            in_=xT[:, c0:c0 + nw].rearrange("(k p) c -> p k c", p=128))),
            waits=[("pe", tile_consumed.get(pos - 2))], dma=("x0" if pos % 2 == 0 else "x1"))
        ld_x = ("x0" if pos % 2 == 0 else "x1", P.c("x0" if pos % 2 == 0 else "x1"))

        P.step("tensor", (lambda e, buf=buf, nw=nw: [
            e.matmul(pb[6][:, 0:nw], ones128[:],
                     buf[:, cc * nw:(cc + 1) * nw],
                     start=(cc == 0), stop=(cc == 7)) for cc in range(8)][-1]),
            waits=[ld_x, ("dv", dv_pre), ("dv", S1["statsv"]),
                   ("dv", S1["vt_evac"])])
        pe_cs = P.c("pe")

        pe_sq = None
        for cc in range(8):
            P.step("scalar", (lambda e, buf=buf, nw=nw, cc=cc: e.activation(
                sqc[:, 0:nw], buf[:, cc * nw:(cc + 1) * nw], AF.Square)),
                waits=[ld_x, ("pe", pe_sq),
                       ("dv", S1["statsv"] if cc == 0 else None)])
            ac_sqd = P.c("ac")
            P.step("tensor", (lambda e, nw=nw, cc=cc: e.matmul(
                pb[7][:, 0:nw], ones128[:], sqc[:, 0:nw],
                start=(cc == 0), stop=(cc == 7))),
                waits=[("ac", ac_sqd), ("pe", pe_cs if cc == 0 else None),
                       ("dv", dv_pre if cc == 0 else None)])
            pe_sq = P.c("pe")
        S1["sq_pe"] = pe_sq

        P.step("vector", (lambda e, nw=nw: e.tensor_scalar(
            meanb[:, 0:nw], pb[6][:, 0:nw], 1.0 / D, None, ALU.mult)),
            waits=[("pe", pe_sq)])
        P.step("vector", (lambda e, nw=nw: e.tensor_scalar(
            tmpb[:, 0:nw], pb[6][:, 0:nw], 1.0 / D, None, ALU.mult)))
        P.step("vector", (lambda e, nw=nw: e.tensor_mul(
            tmpb[:, 0:nw], meanb[:, 0:nw], tmpb[:, 0:nw])))
        P.step("vector", (lambda e, nw=nw: e.scalar_tensor_tensor(
            rstdb[:, 0:nw], pb[7][:, 0:nw], 1.0 / D, tmpb[:, 0:nw],
            ALU.mult, ALU.subtract)))
        P.step("vector", (lambda e, nw=nw: e.tensor_scalar(
            rstdb[:, 0:nw], rstdb[:, 0:nw], EPS, None, ALU.add)))
        P.step("vector", (lambda e, nw=nw: e.reciprocal(
            rstdb[:, 0:nw], rstdb[:, 0:nw])))
        dv_var = P.c("dv")
        P.step("scalar", (lambda e, nw=nw: e.activation(
            rstdb[:, 0:nw], rstdb[:, 0:nw], AF.Sqrt)),
            waits=[("dv", dv_var)])
        ac_rstd = P.c("ac")

        for cc in range(8):
            P.step("vector", (lambda e, buf=buf, nw=nw, cc=cc: e.tensor_sub(
                buf[:, cc * nw:(cc + 1) * nw], buf[:, cc * nw:(cc + 1) * nw],
                meanb[:, 0:nw])), waits=[("ac", ac_rstd)])
            P.step("vector", (lambda e, buf=buf, nw=nw, cc=cc: e.tensor_mul(
                buf[:, cc * nw:(cc + 1) * nw], buf[:, cc * nw:(cc + 1) * nw],
                rstdb[:, 0:nw])))
            P.step("vector", (lambda e, buf=buf, nw=nw, cc=cc: e.tensor_scalar(
                buf[:, cc * nw:(cc + 1) * nw], buf[:, cc * nw:(cc + 1) * nw],
                g1s[:, cc:cc + 1], b1s[:, cc:cc + 1], ALU.mult, ALU.add)))
        dv_std = P.c("dv")
        S1["statsv"] = dv_std

        pe_last = None
        for oi, oc in enumerate(oc_list):
            bank = 4 + oi % 2
            wsl = oi % 2
            wsem = "w3a" if wsl == 0 else "w3b"
            P.step("sync", (lambda e, oc=oc, wsl=wsl: e.dma_start(
                out=wi_s[wsl][:].rearrange("p (k c) -> p k c", k=8),
                in_=winT[:, oc * 128:(oc + 1) * 128]
                    .rearrange("(k p) c -> p k c", p=128))),
                waits=[wi_guard.get(wsl)], dma=wsem)
            ld_wi = (wsem, P.c(wsem))
            P.step("tensor", (lambda e, bank=bank, nw=nw, buf=buf, wsl=wsl: [
                e.matmul(pb[bank][:, 0:nw],
                         wi_s[wsl][:, kc * 128:(kc + 1) * 128],
                         buf[:, kc * nw:(kc + 1) * nw],
                         start=(kc == 0), stop=(kc == 7))
                for kc in range(8)][-1]),
                waits=[("dv", dv_std), bank_free[bank], ld_wi])
            pe_proj = P.c("pe")
            pe_last = pe_proj
            wi_guard[wsl] = ("pe", pe_proj)

            if oc < 8:
                P.step("scalar", (lambda e, bank=bank, nw=nw: e.activation(
                    qstage[:, 0:nw], pb[bank][:, 0:nw], AF.Copy, scale=0.125)),
                    waits=[("pe", pe_proj), S1["qst"]])
                ac_q = P.c("ac")
                bank_free[bank] = ("ac", ac_q)
                P.step("scalar", (lambda e, oc=oc, c0=c0, nw=nw:
                       e.dma_start(
                           out=qspill.rearrange("(h r) c -> h r c", r=66)
                               [2 * oc:2 * oc + 2, 0:64, c0:c0 + nw],
                           in_=qstage[:, 0:nw])),
                       waits=[("ac", ac_q)], dma="sps")
                S1["qst"] = ("sps", P.c("sps"))
            elif oc < 16:
                kc_ = oc - 8
                if kind == "apx":
                    P.step("vector", (lambda e, bank=bank, kc_=kc_:
                           e.tensor_mul(kprevs[:, kc_ * NB:kc_ * NB + NB],
                                        pb[bank][:, 0:NB], bmask[:])),
                           waits=[("pe", pe_proj), ("pre", ld_pre)])
                    bank_free[bank] = ("dv", P.c("dv"))
                else:
                    P.step("vector", (lambda e, bank=bank, nw=nw:
                           e.tensor_copy(kstage[:, 0:nw], pb[bank][:, 0:nw])),
                           waits=[("pe", pe_proj), S1["kst"]])
                    dv_kcp = P.c("dv")
                    bank_free[bank] = ("dv", dv_kcp)
                    P.step("vector", (lambda e, nw=nw: e.tensor_sub(
                        ksm[:, 1:nw], kstage[:, 0:nw - 1], kstage[:, 1:nw])),
                        waits=[("ac", ac_sig)])
                    P.step("vector", (lambda e, kc_=kc_, nw=nw:
                           e.scalar_tensor_tensor(
                               ksm[:, 1:nw], ksm[:, 1:nw],
                               smear_s[:, kc_:kc_ + 1], kstage[:, 1:nw],
                               ALU.mult, ALU.add)))
                    blk0 = c0 // 128
                    for bi2 in range(nw // 128):
                        cb = bi2 * 128
                        ap_ = kc_ * NB + blk0 + bi2
                        P.step("vector", (lambda e, cb=cb, ap_=ap_:
                               e.tensor_sub(ksm[:, cb:cb + 1],
                                            kprevs[:, ap_:ap_ + 1],
                                            kstage[:, cb:cb + 1])))
                        P.step("vector", (lambda e, cb=cb, kc_=kc_:
                               e.scalar_tensor_tensor(
                                   ksm[:, cb:cb + 1], ksm[:, cb:cb + 1],
                                   smear_s[:, kc_:kc_ + 1],
                                   kstage[:, cb:cb + 1],
                                   ALU.mult, ALU.add)))
                    dv_smr = P.c("dv")
                    P.step("scalar", (lambda e, kc_=kc_, c0=c0, nw=nw:
                           e.dma_start(
                               out=kspill.rearrange("(h r) c -> h r c", r=66)
                                   [2 * kc_:2 * kc_ + 2, 0:64, c0:c0 + nw],
                               in_=ksm[:, 0:nw])),
                           waits=[("dv", dv_smr)], dma="sps")
                    S1["kst"] = ("sps", P.c("sps"))
            else:
                vc_ = oc - 16
                if kind == "apx":
                    bank_free[bank] = ("pe", pe_proj)
                else:
                    P.step("vector", (lambda e, bank=bank, nw=nw:
                           e.tensor_copy(vstage[:, 0:nw], pb[bank][:, 0:nw])),
                           waits=[("pe", pe_proj), S1["vst"]])
                    dv_vcp = P.c("dv")
                    bank_free[bank] = ("dv", dv_vcp)
                    vt_dv = []
                    prev_tr_dv = None
                    for bi2 in range(nw // 128):
                        P.step("tensor", (lambda e, bi2=bi2: e.transpose(
                            pb[6][:, 0:128],
                            vstage[:, bi2 * 128:bi2 * 128 + 128], ident[:])),
                            waits=[("dv", dv_vcp), ("pre", ld_pre),
                                   S1["vst"] if bi2 == 0 else None,
                                   ("dv", prev_tr_dv)])
                        pe_tr = P.c("pe")
                        P.step("vector", (lambda e, bi2=bi2: e.tensor_copy(
                            vtstage[:, bi2 * 128:bi2 * 128 + 128],
                            pb[6][:, 0:128])), waits=[("pe", pe_tr)])
                        vt_dv.append(P.c("dv"))
                        prev_tr_dv = vt_dv[-1]
                        S1["vt_evac"] = vt_dv[-1]
                    blk0v = c0 // 128
                    for hp in range(2):
                        hgl = 2 * vc_ + hp
                        P.step("gpsimd", (lambda e, hgl=hgl, hp=hp,
                                          blk0v=blk0v, nw=nw: e.dma_start(
                               out=vspill.rearrange("(r p) c -> p r c", p=128)
                                   [:, hgl * NB + blk0v:
                                    hgl * NB + blk0v + nw // 128, :],
                               in_=vtstage[:, 0:nw]
                                   .rearrange("p (b q) -> p b q", q=128)
                                   [:, :, hp * 64:hp * 64 + 64])),
                               waits=[("dv", vt_dv[-1])], dma="spv")
                    S1["vst"] = ("spv", P.c("spv"))
        tile_consumed[pos] = pe_last
        sps_at[ti] = P.c("sps")
        spv_at[ti] = P.c("spv")

    P.step("vector", lambda e: e.memset(vbuf[0][:, 64:NB * 65:65], 1.0),
           waits=[("dv", dv_pre)])
    P.step("vector", lambda e: e.memset(vbuf[1][:, 64:NB * 65:65], 1.0))
    dv_vones = P.c("dv")

    buf_free = {0: None, 1: None}
    pbuf_free = {0: None, 1: None}
    psS_free = {0: None, 1: None}

    conv_tasks = ([("g", i, half) for i in range(8) for half in range(2)]
                  + [("u", i, half) for i in range(8) for half in range(2)]
                  + [("d", i, 0) for i in range(24)])
    conv_guard = {}

    def emit_conv(idx):
        kind, i, half = conv_tasks[idx]
        par = idx % 2
        src_t, dst_t = {"g": (wgTi, wgb), "u": (wuTi, wub),
                        "d": (wdTi, wdb)}[kind]
        ncols = FF // 2 if kind != "d" else D
        c0c = half * ncols
        r0 = i * 128
        P.step("sync", (lambda e: e.dma_start(
            out=wcv_i[par][:, 0:ncols],
            in_=src_t[r0:r0 + 128, c0c:c0c + ncols])),
            waits=[("dv", conv_guard.get(("i", par)))],
            dma=("g3a" if par == 0 else "g3b"))
        ld_c = ("g3a" if par == 0 else "g3b",
                P.c("g3a" if par == 0 else "g3b"))
        P.step("vector", (lambda e: e.tensor_copy(
            wcv_o[par][:, 0:ncols], wcv_i[par][:, 0:ncols])),
            waits=[ld_c, ("wcv", conv_guard.get(("o", par)))])
        conv_guard[("i", par)] = P.c("dv")
        P.step("gpsimd", (lambda e: e.dma_start(
            out=dst_t[r0:r0 + 128, c0c:c0c + ncols],
            in_=wcv_o[par][:, 0:ncols])),
            waits=[("dv", conv_guard[("i", par)])], dma="wcv", chain=False)
        conv_guard[("o", par)] = P.c("wcv")

    hts = [(t, h) for t in range(4) for h in range(H)]
    S2 = {"psO_free": None, "ostg_free": None}

    def emit_li(li):
        t, h = hts[li]
        bi = li % 2
        if li < len(conv_tasks):
            emit_conv(li)
        nk = (4 * t + 4) * 128
        njt2 = 4 * t + 4
        njt = 2 * njt2

        P.step("sync", (lambda e, h=h, nk=nk, bi=bi: e.dma_start(
            out=kbuf[bi][0:66, 0:nk],
            in_=kspill[h * 66:h * 66 + 66, 0:nk])),
            waits=[("sps", sps_at[5 + t]), ("dv", dv_vones), buf_free[bi]],
            dma=("k2a" if bi == 0 else "k2b"))
        P.step("sync", (lambda e, h=h, nk=nk, bi=bi: e.dma_start(
            out=kbuf[bi][0:66, nk:2 * nk],
            in_=kspill[h * 66:h * 66 + 66, NQ:NQ + nk])),
            waits=[buf_free[bi]], dma=("k2a" if bi == 0 else "k2b"))
        P.step("gpsimd", (lambda e, h=h, njt2=njt2, bi=bi: e.dma_start(
            out=vbuf[bi][:, 0:njt2 * 65]
                .rearrange("p (j c) -> p j c", c=65)[:, :, 0:DH],
            in_=vspill[(h * NB) * 128:(h * NB + njt2) * 128, :]
                .rearrange("(j p) c -> p j c", p=128))),
            waits=[("spv", spv_at[5 + t]), buf_free[bi]],
            dma=("v2a" if bi == 0 else "v2b"))
        P.step("gpsimd", (lambda e, h=h, njt2=njt2, bi=bi: e.dma_start(
            out=vbuf[bi][:, njt2 * 65:2 * njt2 * 65]
                .rearrange("p (j c) -> p j c", c=65)[:, :, 0:DH],
            in_=vspill[(h * NB + 16) * 128:(h * NB + 16 + njt2) * 128, :]
                .rearrange("(j p) c -> p j c", p=128))),
            waits=[buf_free[bi]], dma=("v2a" if bi == 0 else "v2b"))
        P.step("sync", (lambda e, h=h, t=t, bi=bi: e.dma_start(
            out=qbuf[bi][0:66, :],
            in_=qspill[h * 66:h * 66 + 66, t * TT:(t + 1) * TT])),
            waits=[buf_free[bi]], dma=("k2a" if bi == 0 else "k2b"))
        ld_set = P.now("k2a" if bi == 0 else "k2b", "v2a" if bi == 0 else "v2b")

        exp_done = {}
        pv_done = {}

        def emit_S(j, bi=bi, t=t, njt2=njt2, ld_set=ld_set):
            sbk = j % 2
            own = j < njt2
            m = j if own else j - njt2
            diag = (m - 4 * t) if (4 * t <= m < 4 * t + 4) else None
            P.step("tensor", (lambda e, j=j, sbk=sbk, bi=bi: e.matmul(
                pb[sbk][:, :], kbuf[bi][0:66, j * 128:(j + 1) * 128],
                qbuf[bi][0:66, :], start=True, stop=True)),
                waits=ld_set +
                      [("ac", exp_done.get(j - 2)) if j >= 2 else psS_free[sbk]])
            pe_s = P.c("pe")
            dv_mask = None
            npre = min(m - 4 * t, 4) if m > 4 * t else 0
            if npre > 0:
                P.step("vector", (lambda e, sbk=sbk, npre=npre: e.memset(
                    pb[sbk][:, 0:npre * 128], NEG)),
                    waits=[("pe", pe_s)])
                dv_mask = P.c("dv")
            if diag is not None:
                mk = maskA if own else maskB
                P.step("vector", (lambda e, sbk=sbk, diag=diag, mk=mk:
                       e.tensor_add(pb[sbk][:, diag * 128:(diag + 1) * 128],
                                    pb[sbk][:, diag * 128:(diag + 1) * 128],
                                    mk[:])),
                       waits=[("pe", pe_s), ("pre", ld_pre)])
                dv_mask = P.c("dv")
            P.step("scalar", (lambda e, sbk=sbk: e.activation(
                pbuf[sbk][:], pb[sbk][:], AF.Exp, bias=consts[:, 1:2])),
                waits=[("pe", pe_s), ("dv", dv_pre), ("dv", dv_mask),
                       ("pe", pv_done.get(j - 2)) if j >= 2 else pbuf_free[sbk]])
            exp_done[j] = P.c("ac")

        def emit_PV(j, bi=bi, njt=njt):
            sbk = j % 2
            P.step("tensor", (lambda e, j=j, sbk=sbk, bi=bi, njt=njt: e.matmul(
                pb[2][0:65, :], vbuf[bi][:, j * 65:j * 65 + 65], pbuf[sbk][:],
                start=(j == 0), stop=(j == njt - 1))),
                waits=[("ac", exp_done[j]),
                       S2["psO_free"] if j == 0 else None])
            pv_done[j] = P.c("pe")

        emit_S(0)
        for j in range(1, njt):
            emit_S(j)
            emit_PV(j - 1)
        emit_PV(njt - 1)

        P.step("vector", (lambda e: e.reciprocal(rs_s[64:65, :],
                                                 pb[2][64:65, :])),
               waits=[("pe", pv_done[njt - 1]), S2["ostg_free"]])
        dv_rc = P.c("dv")
        P.step("tensor", (lambda e: e.matmul(
            pb[3][0:64, :], ones128[64:65, 0:64], rs_s[64:65, :],
            start=True, stop=True)), waits=[("dv", dv_rc)])
        pe_bc = P.c("pe")
        P.step("vector", lambda e: e.tensor_copy(bc_s[:], pb[3][0:64, :]),
               waits=[("pe", pe_bc)])
        P.step("vector", lambda e: e.tensor_mul(ostg[:], pb[2][0:64, :],
                                                bc_s[:]))
        dv_div = P.c("dv")
        S2["psO_free"] = ("dv", dv_div)
        P.step("scalar", (lambda e, h=h, t=t: e.dma_start(
            out=ospill[h * 64:(h + 1) * 64, t * TT:(t + 1) * TT],
            in_=ostg[:])), waits=[("dv", dv_div)], dma="spo")
        S2["ostg_free"] = ("spo", P.c("spo"))

        buf_free[bi] = ("pe", pv_done[njt - 1])
        pbuf_free[0] = ("pe", pv_done[njt - 2])
        pbuf_free[1] = ("pe", pv_done[njt - 1])
        psS_free[0] = ("ac", exp_done[njt - 2])
        psS_free[1] = ("ac", exp_done[njt - 1])

    order = [(0, None), (1, None), (5, []), (2, range(0, 4)),
             (6, range(4, 8)), (3, range(8, 12)), (7, range(12, 16)),
             (4, range(16, 24)), (8, range(24, 32))]
    pos = 0
    for ti, lis in order:
        emit_tile(ti, pos)
        pos += 1
        if lis:
            for li in lis:
                emit_li(li)
    for li in range(32, 64):
        emit_li(li)

    ph2_spo = P.c("spo")
    conv_total = P.c("wcv")

    state = {"statsv": None}

    def ln_stats(src_s, waits0):
        pe_sq2 = None
        for cc in range(8):
            P.step("scalar", (lambda e, cc=cc, src_s=src_s: e.activation(
                sqc3[:], src_s[:, cc * TT:(cc + 1) * TT], AF.Square)),
                waits=list(waits0) + [("pe", pe_sq2)])
            ac_sq2 = P.c("ac")
            P.step("tensor", (lambda e, cc=cc: e.matmul(
                pb[3][:], ones128[:], sqc3[:],
                start=(cc == 0), stop=(cc == 7))),
                waits=[("ac", ac_sq2),
                       ("dv", state["statsv"]) if cc == 0 else None])
            pe_sq2 = P.c("pe")
        P.step("tensor", (lambda e, src_s=src_s: [
            e.matmul(pb[2][:], ones128[:], src_s[:, cc * TT:(cc + 1) * TT],
                     start=(cc == 0), stop=(cc == 7)) for cc in range(8)][-1]),
            waits=list(waits0) + [("dv", state["statsv"]), ("dv", dv_pre)])
        pe_cs2 = P.c("pe")
        P.step("vector", lambda e: e.tensor_scalar(
            mean2[:], pb[2][:], 1.0 / D, None, ALU.mult),
            waits=[("pe", pe_cs2), ("pe", pe_sq2)])
        P.step("vector", lambda e: e.tensor_scalar(
            tmp2[:], pb[2][:], 1.0 / D, None, ALU.mult))
        P.step("vector", lambda e: e.tensor_mul(tmp2[:], mean2[:], tmp2[:]))
        P.step("vector", lambda e: e.scalar_tensor_tensor(
            rstd2[:], pb[3][:], 1.0 / D, tmp2[:], ALU.mult, ALU.subtract))
        P.step("vector", lambda e: e.tensor_scalar(
            rstd2[:], rstd2[:], EPS, None, ALU.add))
        P.step("vector", lambda e: e.reciprocal(rstd2[:], rstd2[:]))
        dv_sv = P.c("dv")
        state["statsv"] = dv_sv
        P.step("scalar", lambda e: e.activation(rstd2[:], rstd2[:], AF.Sqrt),
               waits=[("dv", dv_sv)])
        return P.c("ac")

    prev_pe_op = None
    prev_dv_t1 = None
    prev_out = None
    prev_ffn_pp = []
    prev_wg_mm = {}
    prev_ds = None

    for tt_i in range(4):
        c0 = tt_i * TT

        P.step("sync", (lambda e, c0=c0: e.dma_start(
            out=ot_s[:].rearrange("p (k c) -> p k c", k=8),
            in_=ospill[:, c0:c0 + TT].rearrange("(k p) c -> p k c", p=128))),
            waits=[("spo", ph2_spo), ("pe", prev_pe_op)], dma="o3")
        P.step("sync", (lambda e, c0=c0: e.dma_start(
            out=xo_s[:].rearrange("p (k c) -> p k c", k=8),
            in_=xT[:, c0:c0 + TT].rearrange("(k p) c -> p k c", p=128))),
            waits=[("dv", prev_dv_t1)], dma="o3")
        ld_o = ("o3", P.c("o3"))

        po_ev = {}
        pe_o2 = {}
        ld_wo = {}
        for oc in range(8):
            P.step("sync", (lambda e, oc=oc: e.dma_start(
                out=wo_s[oc % 2][:].rearrange("p (k c) -> p k c", k=8),
                in_=woutT[:, oc * 128:(oc + 1) * 128]
                    .rearrange("(k p) c -> p k c", p=128))),
                waits=[("pe", pe_o2.get(oc - 2, prev_pe_op))],
                dma=("w3a" if oc % 2 == 0 else "w3b"))
            ld_wo[oc] = ("w3a" if oc % 2 == 0 else "w3b", P.c("w3a" if oc % 2 == 0 else "w3b"))
            P.step("tensor", (lambda e, oc=oc: [
                e.matmul(pb[oc % 2][:],
                         wo_s[oc % 2][:, kc * 128:(kc + 1) * 128],
                         ot_s[:, kc * TT:(kc + 1) * TT],
                         start=(kc == 0), stop=(kc == 7))
                for kc in range(8)][-1]),
                waits=[ld_o, ld_wo[oc],
                       ("dv", po_ev.get(oc - 2, prev_dv_t1))] + prev_ffn_pp)
            pe_o2[oc] = P.c("pe")
            P.step("vector", (lambda e, oc=oc: e.tensor_copy(
                po_s[:, oc * TT:(oc + 1) * TT], pb[oc % 2][:])),
                waits=[("pe", pe_o2[oc])])
            po_ev[oc] = P.c("dv")
        prev_pe_op = pe_o2[7]

        ac_r2 = ln_stats(po_s, [("dv", po_ev[7]), ("out", prev_out)])
        for cc in range(8):
            P.step("vector", (lambda e, cc=cc: e.tensor_sub(
                t1_s[:, cc * TT:(cc + 1) * TT], po_s[:, cc * TT:(cc + 1) * TT],
                mean2[:])), waits=[("ac", ac_r2)])
            P.step("vector", (lambda e, cc=cc: e.tensor_mul(
                t1_s[:, cc * TT:(cc + 1) * TT], t1_s[:, cc * TT:(cc + 1) * TT],
                rstd2[:])))
            P.step("vector", (lambda e, cc=cc: e.tensor_scalar(
                t1_s[:, cc * TT:(cc + 1) * TT], t1_s[:, cc * TT:(cc + 1) * TT],
                g2s[:, cc:cc + 1], b2s[:, cc:cc + 1], ALU.mult, ALU.add)))
            P.step("vector", (lambda e, cc=cc: e.tensor_add(
                t1_s[:, cc * TT:(cc + 1) * TT], t1_s[:, cc * TT:(cc + 1) * TT],
                xo_s[:, cc * TT:(cc + 1) * TT])), waits=[ld_o])
        dv_t1 = P.c("dv")
        prev_dv_t1 = dv_t1

        ac_r3 = ln_stats(t1_s, [("dv", dv_t1)])
        for cc in range(8):
            P.step("vector", (lambda e, cc=cc: e.tensor_sub(
                po_s[:, cc * TT:(cc + 1) * TT], t1_s[:, cc * TT:(cc + 1) * TT],
                mean2[:])), waits=[("ac", ac_r3)])
            P.step("vector", (lambda e, cc=cc: e.tensor_mul(
                po_s[:, cc * TT:(cc + 1) * TT], po_s[:, cc * TT:(cc + 1) * TT],
                rstd2[:])))
        dv_x1 = P.c("dv")

        for cc in range(8):
            P.step("vector", (lambda e, cc=cc: e.tensor_copy(
                x1h[:, cc * TT:(cc + 1) * TT],
                po_s[:, cc * TT:(cc + 1) * TT])), waits=[("dv", dv_x1)])
            P.step("vector", (lambda e, cc=cc: e.tensor_sub(
                x1l[:, cc * TT:(cc + 1) * TT], po_s[:, cc * TT:(cc + 1) * TT],
                x1h[:, cc * TT:(cc + 1) * TT])))
        dv_x1s = P.c("dv")

        umm_done = {}
        sil_done = {}
        hmul_done = {}
        wg_mm = {}
        ds_ev_last = None
        ds_add_last = None
        for half in range(2):
            for ci in range(12):
                hc = half * 12 + ci
                wb = hc % 2
                wprev = wg_mm.get(hc - 2, prev_wg_mm.get(22 + hc))
                P.step("sync", (lambda e, hc=hc, wb=wb: e.dma_start(
                    out=wgb_s[wb][:].rearrange("p (k c) -> p k c", k=8),
                    in_=wgb[:, hc * 128:(hc + 1) * 128]
                        .rearrange("(k p) c -> p k c", p=128))),
                    waits=[("pe", wprev), ("wcv", conv_total)],
                    dma=("g3a" if wb == 0 else "g3b"))
                P.step("sync", (lambda e, hc=hc, wb=wb: e.dma_start(
                    out=wub_s[wb][:].rearrange("p (k c) -> p k c", k=8),
                    in_=wub[:, hc * 128:(hc + 1) * 128]
                        .rearrange("(k p) c -> p k c", p=128))),
                    dma=("g3a" if wb == 0 else "g3b"))
                ld_gu = ("g3a" if wb == 0 else "g3b",
                         P.c("g3a" if wb == 0 else "g3b"))

                gbk = pb[0] if hc % 2 == 0 else pb[2]
                ubk = pb[1] if hc % 2 == 0 else pb[3]
                gsb = gs_s2[hc % 2]
                P.step("tensor", (lambda e, wb=wb, gbk=gbk: [
                    e.matmul(gbk[:], wgb_s[wb][:, kc * 128:(kc + 1) * 128],
                             xs_[:, kc * TT:(kc + 1) * TT],
                             start=(pi == 0 and kc == 0),
                             stop=(pi == 1 and kc == 7))
                    for pi, xs_ in enumerate((x1h, x1l)) for kc in range(8)][-1]),
                    waits=[ld_gu, ("dv", dv_x1s),
                           ("ac", sil_done.get(hc - 2)),
                           ("dv", po_ev[7] if hc < 2 else None)] +
                          (prev_ffn_pp if hc < 2 else []))
                pe_g = P.c("pe")
                P.step("tensor", (lambda e, wb=wb, ubk=ubk: [
                    e.matmul(ubk[:], wub_s[wb][:, kc * 128:(kc + 1) * 128],
                             xs_[:, kc * TT:(kc + 1) * TT],
                             start=(pi == 0 and kc == 0),
                             stop=(pi == 1 and kc == 7))
                    for pi, xs_ in enumerate((x1h, x1l)) for kc in range(8)][-1]),
                    waits=[ld_gu, ("dv", dv_x1s), ("dv", hmul_done.get(hc - 2)),
                           ("dv", po_ev[7] if hc < 2 else None)] +
                          (prev_ffn_pp if hc < 2 else []))
                umm_done[hc] = P.c("pe")
                wg_mm[hc] = umm_done[hc]

                P.step("scalar", (lambda e, gbk=gbk, gsb=gsb: e.activation(
                    gsb[:], gbk[:], AF.Silu, scale=1.0 / 127.0)),
                    waits=[("pe", pe_g), ("dv", hmul_done.get(hc - 2))])
                sil_done[hc] = P.c("ac")
                P.step("vector", (lambda e, ubk=ubk, gsb=gsb: e.tensor_mul(
                    htmp[:], ubk[:], gsb[:])),
                    waits=[("ac", sil_done[hc]), ("pe", umm_done[hc])])
                P.step("vector", (lambda e, ci=ci: e.tensor_copy(
                    hh_s[:, ci * TT:(ci + 1) * TT], htmp[:])))
                P.step("vector", (lambda e, ci=ci: e.tensor_sub(
                    hl_s[:, ci * TT:(ci + 1) * TT], htmp[:],
                    hh_s[:, ci * TT:(ci + 1) * TT])))
                hmul_done[hc] = P.c("dv")

            for dcg in range(2):
                dmm_last = state.get("dmm_prev")
                for hcg in range(3):
                    wb = hcg % 2
                    r0w = (half * 12 + hcg * 4) * 128
                    P.step("sync", (lambda e, r0w=r0w, wb=wb, dcg=dcg:
                           e.dma_start(
                               out=wdb_s[wb][:]
                                   .rearrange("p (q c) -> p q c", q=4),
                               in_=wdb[r0w:r0w + 512, dcg * TT:(dcg + 1) * TT]
                                   .rearrange("(q p) c -> p q c", p=128))),
                           waits=[("pe", dmm_last), ("wcv", conv_total)],
                           dma=("d3a" if wb == 0 else "d3b"))
                    ld_d = ("d3a" if wb == 0 else "d3b",
                            P.c("d3a" if wb == 0 else "d3b"))
                    for cq in range(4):
                        ci = hcg * 4 + cq
                        P.step("tensor", (lambda e, ci=ci, cq=cq, wb=wb: [
                            e.matmul(pb[4 + dq][:],
                                     wdb_s[wb][:, cq * TT + dq * 128:
                                               cq * TT + (dq + 1) * 128],
                                     hs_[:, ci * TT:(ci + 1) * TT],
                                     start=(ci == 0 and pi == 0),
                                     stop=(ci == 11 and pi == 1))
                            for dq in range(4) for pi, hs_ in
                            enumerate((hh_s, hl_s))][-1]),
                            waits=[ld_d,
                                   ("dv", hmul_done[half * 12 + 11]),
                                   ("ac", ds_ev_last) if (ci == 0 and half == 0 and dcg == 1) else None,
                                   ("dv", ds_add_last) if (ci == 0 and half == 1 and dcg == 1) else None,
                                   ("dv", prev_ds) if (ci == 0 and half == 0 and dcg == 0) else None,
                                   ("ac", ds_ev_last) if (ci == 0 and half == 1 and dcg == 0) else None])
                        dmm_last = P.c("pe")
                state["dmm_prev"] = dmm_last
                if half == 0:
                    for dq in range(4):
                        dc = dcg * 4 + dq
                        P.step("scalar", (lambda e, dq=dq, dc=dc: e.activation(
                            scr_s[:, dc * TT:(dc + 1) * TT], pb[4 + dq][:],
                            AF.Copy, scale=DSCALE)),
                            waits=[("pe", dmm_last), ("out", prev_out)])
                    ds_ev_last = P.c("ac")
                else:
                    for dq in range(4):
                        dc = dcg * 4 + dq
                        P.step("vector", (lambda e, dq=dq, dc=dc:
                               e.scalar_tensor_tensor(
                                   scr_s[:, dc * TT:(dc + 1) * TT],
                                   pb[4 + dq][:], DSCALE,
                                   scr_s[:, dc * TT:(dc + 1) * TT],
                                   ALU.mult, ALU.add)),
                               waits=[("pe", dmm_last), ("ac", ds_ev_last)])
                    ds_add_last = P.c("dv")
        prev_wg_mm = dict(wg_mm)
        prev_ffn_pp = [("ac", sil_done[23]), ("dv", hmul_done[23])]

        for cc in range(8):
            P.step("vector", (lambda e, cc=cc: e.tensor_add(
                t1_s[:, cc * TT:(cc + 1) * TT], po_s[:, cc * TT:(cc + 1) * TT],
                scr_s[:, cc * TT:(cc + 1) * TT])),
                waits=[("dv", ds_add_last), ("dv", dv_x1)])
        dv_t2 = P.c("dv")
        prev_ds = dv_t2

        ac_r4 = ln_stats(t1_s, [("dv", dv_t2)])
        for cc in range(8):
            P.step("vector", (lambda e, cc=cc: e.tensor_sub(
                scr_s[:, cc * TT:(cc + 1) * TT], t1_s[:, cc * TT:(cc + 1) * TT],
                mean2[:])), waits=[("ac", ac_r4)])
            P.step("vector", (lambda e, cc=cc: e.tensor_mul(
                scr_s[:, cc * TT:(cc + 1) * TT], scr_s[:, cc * TT:(cc + 1) * TT],
                rstd2[:])))
        dv_y = P.c("dv")

        P.step("scalar", (lambda e, c0=c0: e.dma_start(
            out=yT[:, c0:c0 + TT].rearrange("(k p) c -> p k c", p=128),
            in_=scr_s[:].rearrange("p (k c) -> p k c", k=8))),
            waits=[("dv", dv_y)], dma="out")
        prev_out = P.c("out")

    out_total = P.c("out")

    def emitter(ops, final_wait=None):
        def run(eng):
            for fn, waits, inc in ops:
                for s, v in waits:
                    eng.wait_ge(SEMS[s], v)
                hh = fn(eng)
                if inc is not None:
                    if isinstance(hh, list):
                        hh = hh[-1]
                    hh.then_inc(SEMS[inc[0]], inc[1])
            if final_wait is not None:
                eng.wait_ge(SEMS[final_wait[0]], final_wait[1])
        return run

    with nc.Block() as block:
        block.sync(emitter(P.q["sync"], final_wait=("out", out_total)))
        block.tensor(emitter(P.q["tensor"]))
        block.vector(emitter(P.q["vector"]))
        block.scalar(emitter(P.q["scalar"]))
        block.gpsimd(emitter(P.q["gpsimd"]))
    for cm in reversed(sem_ctx):
        cm.__exit__(None, None, None)
    return nc



def _core_layout(r):
    own = [2 * s + r for s in range(16)]
    other = [2 * s + (1 - r) for s in range(16)]
    blocks = own + other
    perm = np.concatenate([np.arange(128 * g, 128 * g + 128) for g in blocks])
    apx = np.array([max(128 * g - 1, 0) for g in blocks], dtype=np.int64)
    apx_valid = np.array([1.0 if g > 0 else 0.0 for g in blocks],
                         dtype=np.float32)
    return blocks, perm, apx, apx_valid


def kernel(L0_attn_x, L0_attn_ln1_g, L0_attn_ln1_b, L0_attn_in_w,
           L0_attn_out_w, L0_attn_ln2_g, L0_attn_ln2_b, L0_attn_slopes,
           L0_attn_smear, L0_ffn_w_gate, L0_ffn_w_up, L0_ffn_w_down):
    x = np.ascontiguousarray(np.asarray(L0_attn_x, dtype=np.float32))
    winT = np.ascontiguousarray(np.asarray(L0_attn_in_w, np.float32).T)
    woutT = np.ascontiguousarray(np.asarray(L0_attn_out_w, np.float32).T)
    wgT = np.ascontiguousarray(np.asarray(L0_ffn_w_gate, np.int32).T)
    wuT = np.ascontiguousarray(np.asarray(L0_ffn_w_up, np.int32).T)
    wdT = np.ascontiguousarray(np.asarray(L0_ffn_w_down, np.int32).T)
    g1c = np.ascontiguousarray(np.asarray(L0_attn_ln1_g, np.float32)
                               .reshape(8, 128).T)
    b1c = np.ascontiguousarray(np.asarray(L0_attn_ln1_b, np.float32)
                               .reshape(8, 128).T)
    g2c = np.ascontiguousarray(np.asarray(L0_attn_ln2_g, np.float32)
                               .reshape(8, 128).T)
    b2c = np.ascontiguousarray(np.asarray(L0_attn_ln2_b, np.float32)
                               .reshape(8, 128).T)
    slopes = np.ascontiguousarray(
        np.asarray(L0_attn_slopes, np.float32).reshape(H, 1))
    smear = np.asarray(L0_attn_smear, np.float32)
    smearb = np.ascontiguousarray(np.repeat(smear, 64).reshape(8, 128).T)
    maskA = np.where(np.arange(128)[:, None] > np.arange(128)[None, :],
                     np.float32(NEG), np.float32(0.0)).astype(np.float32)

    nc = build_nc()
    in_maps = []
    metas = []
    for c in range(8):
        b, r = c // 2, c % 2
        blocks, perm, apx, apx_valid = _core_layout(r)
        xTb = np.ascontiguousarray(x[b].T)
        xcore = xTb[:, perm]
        xapx = np.zeros((D, 128), dtype=np.float32)
        xapx[:, 0:NB] = xTb[:, apx]
        xin = np.ascontiguousarray(np.concatenate([xcore, xapx], axis=1))
        kpos = np.ascontiguousarray(
            np.broadcast_to(perm.astype(np.float32)[None, :], (16, L)))
        maskB = (np.zeros((128, 128), np.float32) if r == 1
                 else np.full((128, 128), NEG, np.float32))
        bmask = np.ascontiguousarray(
            np.broadcast_to(apx_valid[None, :], (128, NB)).copy())
        in_maps.append({
            "xT": xin, "winT": winT, "woutT": woutT,
            "wgTi": wgT, "wuTi": wuT, "wdTi": wdT,
            "g1i": g1c, "b1i": b1c, "g2i": g2c, "b2i": b2c,
            "slopesi": slopes, "smeari": smearb, "kposi": kpos,
            "maskAi": maskA, "maskBi": maskB, "bmaski": bmask,
        })
        metas.append((b, perm))

    res = run_bass_kernel_spmd(nc, in_maps, core_ids=list(range(8)))

    out = np.empty((4, L, D), dtype=np.float32)
    for c in range(8):
        b, perm = metas[c]
        yTc = np.asarray(res.results[c]["yT"])
        out[b, perm[:NQ], :] = yTc.T
    return out


# version 13 (final)
# speedup vs baseline: 1.1933x; 1.0421x over previous
import numpy as np

import concourse.bass as bass
import concourse.mybir as mybir
from concourse.bass_utils import run_bass_kernel_spmd

F32 = mybir.dt.float32
I32 = mybir.dt.int32
AF = mybir.ActivationFunctionType
ALU = mybir.AluOpType

D = 1024
H = 16
DH = 64
FF = 3072
L = 4096
NB = 32
TT = 512
NT = 8
NQ = 2048
NEG = -1.0e9
CSHIFT = -4.0
EPS = 1e-5
DSCALE = 1.0 / (127.0 * 127.0)
ENGS = ("sync", "tensor", "vector", "scalar", "gpsimd")
OWN = {"tensor": "pe", "vector": "dv", "scalar": "ac", "gpsimd": "gp"}
ALL_SEMS = ("pre", "x0", "x1", "k2a", "k2b", "v2a", "v2b", "o3",
            "w3a", "w3b", "g3a", "g3b", "d3a", "d3b",
            "sps", "spv", "spo", "wcv", "out", "pe", "dv", "ac", "gp")


class Plan:

    def __init__(self):
        self.cnt = {}
        self.q = {e: [] for e in ENGS}

    def c(self, s):
        return self.cnt.get(s, 0)

    ENG_SEMS = ("pe", "dv", "ac", "gp")

    def step(self, eng, fn, waits=(), dma=None, chain=True):
        ws = {}
        for item in waits:
            if item is None:
                continue
            s, v = item
            if v:
                ws[s] = max(ws.get(s, 0), v)
        inc = None
        if dma is not None:
            prior = self.cnt.get(dma, 0)
            if prior:
                ws[dma] = max(ws.get(dma, 0), prior)
            inc = (dma, 16)
        elif eng in OWN:
            own = OWN[eng]
            if chain and eng in ("vector", "scalar", "gpsimd"):
                prev = self.cnt.get(own, 0)
                if prev:
                    ws[own] = max(ws.get(own, 0), prev)
            inc = (own, 1)
        if inc:
            self.cnt[inc[0]] = self.cnt.get(inc[0], 0) + inc[1]
        self.q[eng].append((fn, sorted(ws.items()), inc))
        return self.cnt.get(inc[0], 0) if inc else 0

    def now(self, *sems):
        return [(s, self.cnt.get(s, 0)) for s in sems]


def build_nc(debug=False):
    nc = bass.Bass("TRN2", target_bir_lowering=False)

    xT = nc.dram_tensor("xT", [D, L + 128], F32, kind="ExternalInput").ap()
    winT = nc.dram_tensor("winT", [D, 3 * D], F32, kind="ExternalInput").ap()
    woutT = nc.dram_tensor("woutT", [D, D], F32, kind="ExternalInput").ap()
    wgTi = nc.dram_tensor("wgTi", [D, FF], I32, kind="ExternalInput").ap()
    wuTi = nc.dram_tensor("wuTi", [D, FF], I32, kind="ExternalInput").ap()
    wdTi = nc.dram_tensor("wdTi", [FF, D], I32, kind="ExternalInput").ap()
    g1i = nc.dram_tensor("g1i", [128, 8], F32, kind="ExternalInput").ap()
    b1i = nc.dram_tensor("b1i", [128, 8], F32, kind="ExternalInput").ap()
    g2i = nc.dram_tensor("g2i", [128, 8], F32, kind="ExternalInput").ap()
    b2i = nc.dram_tensor("b2i", [128, 8], F32, kind="ExternalInput").ap()
    slopesi = nc.dram_tensor("slopesi", [16, 1], F32, kind="ExternalInput").ap()
    smeari = nc.dram_tensor("smeari", [128, 8], F32, kind="ExternalInput").ap()
    kposi = nc.dram_tensor("kposi", [16, L], F32, kind="ExternalInput").ap()
    maskAi = nc.dram_tensor("maskAi", [128, 128], F32, kind="ExternalInput").ap()
    maskBi = nc.dram_tensor("maskBi", [128, 128], F32, kind="ExternalInput").ap()
    bmaski = nc.dram_tensor("bmaski", [128, NB], F32, kind="ExternalInput").ap()
    identi = nc.inline_tensor(np.eye(128, dtype=np.float32), name="identi").ap()
    yT = nc.dram_tensor("yT", [D, NQ], F32, kind="ExternalOutput").ap()

    spk = "ExternalOutput" if debug else "Internal"
    kspill = nc.dram_tensor("kspill", [H * 66, L], F32, kind=spk).ap()
    vspill = nc.dram_tensor("vspill", [H * NB * 128, DH], F32, kind=spk).ap()
    qspill = nc.dram_tensor("qspill", [H * 66, NQ], F32, kind=spk).ap()
    ospill = nc.dram_tensor("ospill", [D, NQ], F32, kind=spk).ap()
    BF16 = mybir.dt.bfloat16
    wgb = nc.dram_tensor("wgb", [D, FF], BF16, kind="Internal").ap()
    wub = nc.dram_tensor("wub", [D, FF], BF16, kind="Internal").ap()
    wdb = nc.dram_tensor("wdb", [FF, D], BF16, kind="Internal").ap()

    ARENA = 206 * 1024
    arena = nc.alloc_sbuf_tensor("arena", [128, ARENA], mybir.dt.uint8)
    base0 = nc.lookup_mloc(arena).addr
    cur = [base0]

    def place(name, shape, dtype=F32):
        nbytes = int(np.prod(shape[1:])) * mybir.dt.size(dtype)
        off = (cur[0] + 31) // 32 * 32
        cur[0] = off + nbytes
        assert cur[0] <= base0 + ARENA, f"arena overflow at {name}"
        return nc.alloc_sbuf_tensor_at(name, shape, dtype, offset=off).ap()

    consts = place("consts", [128, 4])
    ones128 = place("ones128", [128, 128])
    maskA = place("maskA", [128, 128])
    maskB = place("maskB", [128, 128])
    g1s = place("g1s", [128, 8]); b1s = place("b1s", [128, 8])
    g2s = place("g2s", [128, 8]); b2s = place("b2s", [128, 8])
    smear_s = place("smear_s", [128, 8])
    ident = place("ident", [128, 128])
    pbase = cur[0]

    wi_s = [place(f"wi_s{i}", [128, 8 * 128]) for i in range(2)]
    kpos_s = place("kpos_s", [16, L])
    slopes_s = place("slopes_s", [16, 1])
    bmask = place("bmask", [128, NB])
    xt = [place(f"xt{i}", [128, 8 * TT]) for i in range(2)]
    sqc = place("sqc", [128, TT])
    meanb = place("meanb", [128, TT])
    rstdb = place("rstdb", [128, TT])
    tmpb = place("tmpb", [128, TT])
    kstage = place("kstage", [128, TT])
    ksm = place("ksm", [128, TT])
    kprevs = place("kprevs", [128, 8 * NB])
    vstage = place("vstage", [128, TT])
    vtstage = place("vtstage", [128, TT])
    qstage = place("qstage", [128, TT])
    arow = place("arow", [16, L])
    qr64 = place("qr64", [16, NQ])
    qr65 = place("qr65", [16, NQ])

    kbuf = [place(f"kbuf{i}", [128, L]) for i in range(2)]
    vbuf = [place(f"vbuf{i}", [128, NB * 65]) for i in range(2)]
    qbuf = [place(f"qbuf{i}", [128, TT]) for i in range(2)]
    pbuf = [place(f"pbuf{i}", [128, TT]) for i in range(2)]
    rs_s = place("rs_s", [65, TT])
    bc_s = place("bc_s", [64, TT])
    ostg = place("ostg", [64, TT])
    wcv_i = [place(f"wcv_i{i}", [128, FF // 2], I32) for i in range(2)]
    wcv_o = [place(f"wcv_o{i}", [128, FF // 2], BF16) for i in range(2)]

    cur[0] = pbase
    ot_s = place("ot_s", [128, 8 * TT])
    xo_s = place("xo_s", [128, 8 * TT])
    po_s = place("po_s", [128, 8 * TT])
    t1_s = place("t1_s", [128, 8 * TT])
    scr_s = place("scr_s", [128, 8 * TT])
    hh_s = place("hh_s", [128, 12 * TT], BF16)
    hl_s = place("hl_s", [128, 12 * TT], BF16)
    htmp = place("htmp", [128, TT])
    x1h = place("x1h", [128, 8 * TT], BF16)
    x1l = place("x1l", [128, 8 * TT], BF16)
    wo_s = [place(f"wo_s{i}", [128, 8 * 128]) for i in range(2)]
    wgb_s = [place(f"wgb_s{i}", [128, 8 * 128], BF16) for i in range(2)]
    wub_s = [place(f"wub_s{i}", [128, 8 * 128], BF16) for i in range(2)]
    wdb_s = [place(f"wdb_s{i}", [128, 4 * TT], BF16) for i in range(2)]
    gs_s2 = [place(f"gs_s{i}", [128, TT]) for i in range(2)]
    mean2 = place("mean2", [128, TT])
    rstd2 = place("rstd2", [128, TT])
    tmp2 = place("tmp2", [128, TT])
    sqc3 = [place(f"sqc3_{i}", [128, TT]) for i in range(2)]

    pb = [nc.alloc_psum_tensor(f"pb{i}", [128, TT], F32).ap() for i in range(8)]

    SEMS = {}
    sem_ctx = []
    for s in ALL_SEMS:
        cm = nc.semaphore("s_" + s)
        SEMS[s] = cm.__enter__()
        sem_ctx.append(cm)

    P = Plan()

    for dst, src in ((g1s, g1i), (b1s, b1i), (g2s, g2i), (b2s, b2i),
                     (slopes_s, slopesi), (smear_s, smeari), (kpos_s, kposi),
                     (maskA, maskAi), (maskB, maskBi), (bmask, bmaski),
                     (ident, identi)):
        P.step("sync", (lambda e, dst=dst, src=src:
                        e.dma_start(out=dst[:], in_=src[:])), dma="pre")
    ld_pre = P.c("pre")

    P.step("vector", lambda e: e.memset(consts[:, 0:1], 1.0))
    P.step("vector", lambda e: e.memset(consts[:, 1:2], CSHIFT))
    P.step("vector", lambda e: e.memset(ones128[:], 1.0))
    dv_pre = P.c("dv")

    P.step("scalar", lambda e: e.activation(smear_s[:], smear_s[:], AF.Sigmoid),
           waits=[("pre", ld_pre)])
    ac_sig = P.c("ac")

    P.step("vector", lambda e: e.memset(qr64[:], 1.0),
           waits=[("pre", ld_pre)])
    P.step("vector", lambda e: e.tensor_copy(qr65[:], kpos_s[:, 0:NQ]),
           waits=[("pre", ld_pre)])
    dv_q45 = P.c("dv")
    P.step("scalar", (lambda e: e.dma_start(
        out=qspill.rearrange("(h r) c -> h r c", r=66)[:, 64:65, :],
        in_=qr64[:].rearrange("h (r c) -> h r c", r=1))),
        waits=[("dv", dv_q45)], dma="sps")
    P.step("scalar", (lambda e: e.dma_start(
        out=qspill.rearrange("(h r) c -> h r c", r=66)[:, 65:66, :],
        in_=qr65[:].rearrange("h (r c) -> h r c", r=1))),
        waits=[("dv", dv_q45)], dma="sps")

    P.step("vector", lambda e: e.tensor_scalar(
        arow[:], kpos_s[:], slopes_s[:, 0:1], None, ALU.mult),
        waits=[("pre", ld_pre)])
    dva = P.c("dv")
    P.step("scalar", (lambda e: e.dma_start(
        out=kspill.rearrange("(h r) c -> h r c", r=66)[:, 64:65, :],
        in_=arow[:].rearrange("h (r c) -> h r c", r=1))),
        waits=[("dv", dva)], dma="sps")
    sp_a1 = P.c("sps")
    P.step("vector", lambda e: e.tensor_scalar(
        arow[:], kpos_s[:], 0.0, 1.0, ALU.mult, ALU.add),
        waits=[("sps", sp_a1)])
    P.step("vector", lambda e: e.tensor_scalar(
        arow[:], arow[:], slopes_s[:, 0:1], -1.0, ALU.mult, ALU.mult))
    dva2 = P.c("dv")
    P.step("scalar", (lambda e: e.dma_start(
        out=kspill.rearrange("(h r) c -> h r c", r=66)[:, 65:66, :],
        in_=arow[:].rearrange("h (r c) -> h r c", r=1))),
        waits=[("dv", dva2)], dma="sps")

    tiles = [("apx", L, 128)] + [("main", t * TT, TT) for t in range(NT)]
    tile_consumed = {}
    bank_free = {4: None, 5: None}
    wi_guard = {}
    sps_at = {}
    spv_at = {}
    S1 = {"kst": None, "vst": None, "qst": None, "statsv": None,
          "vt_evac": None, "sq_pe": None}
    vtt_free = {}

    def emit_tile(ti, pos):
        kind, c0, nw = tiles[ti]
        buf = xt[pos % 2]
        is_own = (kind == "main" and c0 < NQ)
        oc_list = (list(range(8, 16)) if kind == "apx"
                   else (list(range(24)) if is_own else list(range(8, 24))))

        P.step("sync", (lambda e, buf=buf, c0=c0, nw=nw: e.dma_start(
            out=buf[:, 0:8 * nw].rearrange("p (k c) -> p k c", k=8),
            in_=xT[:, c0:c0 + nw].rearrange("(k p) c -> p k c", p=128))),
            waits=[("pe", tile_consumed.get(pos - 2))], dma=("x0" if pos % 2 == 0 else "x1"))
        ld_x = ("x0" if pos % 2 == 0 else "x1", P.c("x0" if pos % 2 == 0 else "x1"))

        P.step("tensor", (lambda e, buf=buf, nw=nw: [
            e.matmul(pb[6][:, 0:nw], ones128[:],
                     buf[:, cc * nw:(cc + 1) * nw],
                     start=(cc == 0), stop=(cc == 7)) for cc in range(8)][-1]),
            waits=[ld_x, ("dv", dv_pre), ("dv", S1["statsv"]),
                   ("dv", S1["vt_evac"])])
        pe_cs = P.c("pe")

        pe_sq = None
        for cc in range(8):
            P.step("scalar", (lambda e, buf=buf, nw=nw, cc=cc: e.activation(
                sqc[:, 0:nw], buf[:, cc * nw:(cc + 1) * nw], AF.Square)),
                waits=[ld_x, ("pe", pe_sq),
                       ("dv", S1["statsv"] if cc == 0 else None)])
            ac_sqd = P.c("ac")
            P.step("tensor", (lambda e, nw=nw, cc=cc: e.matmul(
                pb[7][:, 0:nw], ones128[:], sqc[:, 0:nw],
                start=(cc == 0), stop=(cc == 7))),
                waits=[("ac", ac_sqd), ("pe", pe_cs if cc == 0 else None),
                       ("dv", dv_pre if cc == 0 else None)])
            pe_sq = P.c("pe")
        S1["sq_pe"] = pe_sq

        P.step("vector", (lambda e, nw=nw: e.tensor_scalar(
            meanb[:, 0:nw], pb[6][:, 0:nw], 1.0 / D, None, ALU.mult)),
            waits=[("pe", pe_sq)])
        P.step("vector", (lambda e, nw=nw: e.tensor_scalar(
            tmpb[:, 0:nw], pb[6][:, 0:nw], 1.0 / D, None, ALU.mult)))
        P.step("vector", (lambda e, nw=nw: e.tensor_mul(
            tmpb[:, 0:nw], meanb[:, 0:nw], tmpb[:, 0:nw])))
        P.step("vector", (lambda e, nw=nw: e.scalar_tensor_tensor(
            rstdb[:, 0:nw], pb[7][:, 0:nw], 1.0 / D, tmpb[:, 0:nw],
            ALU.mult, ALU.subtract)))
        P.step("vector", (lambda e, nw=nw: e.tensor_scalar(
            rstdb[:, 0:nw], rstdb[:, 0:nw], EPS, None, ALU.add)))
        P.step("vector", (lambda e, nw=nw: e.reciprocal(
            rstdb[:, 0:nw], rstdb[:, 0:nw])))
        dv_var = P.c("dv")
        P.step("scalar", (lambda e, nw=nw: e.activation(
            rstdb[:, 0:nw], rstdb[:, 0:nw], AF.Sqrt)),
            waits=[("dv", dv_var)])
        ac_rstd = P.c("ac")

        for cc in range(8):
            P.step("vector", (lambda e, buf=buf, nw=nw, cc=cc: e.tensor_sub(
                buf[:, cc * nw:(cc + 1) * nw], buf[:, cc * nw:(cc + 1) * nw],
                meanb[:, 0:nw])), waits=[("ac", ac_rstd)])
            P.step("vector", (lambda e, buf=buf, nw=nw, cc=cc: e.tensor_mul(
                buf[:, cc * nw:(cc + 1) * nw], buf[:, cc * nw:(cc + 1) * nw],
                rstdb[:, 0:nw])))
            P.step("vector", (lambda e, buf=buf, nw=nw, cc=cc: e.tensor_scalar(
                buf[:, cc * nw:(cc + 1) * nw], buf[:, cc * nw:(cc + 1) * nw],
                g1s[:, cc:cc + 1], b1s[:, cc:cc + 1], ALU.mult, ALU.add)))
        dv_std = P.c("dv")
        S1["statsv"] = dv_std

        pe_last = None
        for oi, oc in enumerate(oc_list):
            bank = 4 + oi % 2
            wsl = oi % 2
            wsem = "w3a" if wsl == 0 else "w3b"
            P.step("sync", (lambda e, oc=oc, wsl=wsl: e.dma_start(
                out=wi_s[wsl][:].rearrange("p (k c) -> p k c", k=8),
                in_=winT[:, oc * 128:(oc + 1) * 128]
                    .rearrange("(k p) c -> p k c", p=128))),
                waits=[wi_guard.get(wsl)], dma=wsem)
            ld_wi = (wsem, P.c(wsem))
            P.step("tensor", (lambda e, bank=bank, nw=nw, buf=buf, wsl=wsl: [
                e.matmul(pb[bank][:, 0:nw],
                         wi_s[wsl][:, kc * 128:(kc + 1) * 128],
                         buf[:, kc * nw:(kc + 1) * nw],
                         start=(kc == 0), stop=(kc == 7))
                for kc in range(8)][-1]),
                waits=[("dv", dv_std), bank_free[bank], ld_wi])
            pe_proj = P.c("pe")
            pe_last = pe_proj
            wi_guard[wsl] = ("pe", pe_proj)

            if oc < 8:
                P.step("scalar", (lambda e, bank=bank, nw=nw: e.activation(
                    qstage[:, 0:nw], pb[bank][:, 0:nw], AF.Copy, scale=0.125)),
                    waits=[("pe", pe_proj), S1["qst"]])
                ac_q = P.c("ac")
                bank_free[bank] = ("ac", ac_q)
                P.step("scalar", (lambda e, oc=oc, c0=c0, nw=nw:
                       e.dma_start(
                           out=qspill.rearrange("(h r) c -> h r c", r=66)
                               [2 * oc:2 * oc + 2, 0:64, c0:c0 + nw],
                           in_=qstage[:, 0:nw])),
                       waits=[("ac", ac_q)], dma="sps")
                S1["qst"] = ("sps", P.c("sps"))
            elif oc < 16:
                kc_ = oc - 8
                if kind == "apx":
                    P.step("vector", (lambda e, bank=bank, kc_=kc_:
                           e.tensor_mul(kprevs[:, kc_ * NB:kc_ * NB + NB],
                                        pb[bank][:, 0:NB], bmask[:])),
                           waits=[("pe", pe_proj), ("pre", ld_pre)])
                    bank_free[bank] = ("dv", P.c("dv"))
                else:
                    P.step("vector", (lambda e, bank=bank, nw=nw:
                           e.tensor_copy(kstage[:, 0:nw], pb[bank][:, 0:nw])),
                           waits=[("pe", pe_proj), S1["kst"]])
                    dv_kcp = P.c("dv")
                    bank_free[bank] = ("dv", dv_kcp)
                    P.step("vector", (lambda e, nw=nw: e.tensor_sub(
                        ksm[:, 1:nw], kstage[:, 0:nw - 1], kstage[:, 1:nw])),
                        waits=[("ac", ac_sig)])
                    P.step("vector", (lambda e, kc_=kc_, nw=nw:
                           e.scalar_tensor_tensor(
                               ksm[:, 1:nw], ksm[:, 1:nw],
                               smear_s[:, kc_:kc_ + 1], kstage[:, 1:nw],
                               ALU.mult, ALU.add)))
                    blk0 = c0 // 128
                    for bi2 in range(nw // 128):
                        cb = bi2 * 128
                        ap_ = kc_ * NB + blk0 + bi2
                        P.step("vector", (lambda e, cb=cb, ap_=ap_:
                               e.tensor_sub(ksm[:, cb:cb + 1],
                                            kprevs[:, ap_:ap_ + 1],
                                            kstage[:, cb:cb + 1])))
                        P.step("vector", (lambda e, cb=cb, kc_=kc_:
                               e.scalar_tensor_tensor(
                                   ksm[:, cb:cb + 1], ksm[:, cb:cb + 1],
                                   smear_s[:, kc_:kc_ + 1],
                                   kstage[:, cb:cb + 1],
                                   ALU.mult, ALU.add)))
                    dv_smr = P.c("dv")
                    P.step("scalar", (lambda e, kc_=kc_, c0=c0, nw=nw:
                           e.dma_start(
                               out=kspill.rearrange("(h r) c -> h r c", r=66)
                                   [2 * kc_:2 * kc_ + 2, 0:64, c0:c0 + nw],
                               in_=ksm[:, 0:nw])),
                           waits=[("dv", dv_smr)], dma="sps")
                    S1["kst"] = ("sps", P.c("sps"))
            else:
                vc_ = oc - 16
                if kind == "apx":
                    bank_free[bank] = ("pe", pe_proj)
                else:
                    P.step("vector", (lambda e, bank=bank, nw=nw:
                           e.tensor_copy(vstage[:, 0:nw], pb[bank][:, 0:nw])),
                           waits=[("pe", pe_proj), S1["vst"]])
                    dv_vcp = P.c("dv")
                    bank_free[bank] = ("dv", dv_vcp)
                    vt_dv = []
                    prev_tr_dv = None
                    for bi2 in range(nw // 128):
                        P.step("tensor", (lambda e, bi2=bi2: e.transpose(
                            pb[6][:, 0:128],
                            vstage[:, bi2 * 128:bi2 * 128 + 128], ident[:])),
                            waits=[("dv", dv_vcp), ("pre", ld_pre),
                                   S1["vst"] if bi2 == 0 else None,
                                   ("dv", prev_tr_dv)])
                        pe_tr = P.c("pe")
                        P.step("vector", (lambda e, bi2=bi2: e.tensor_copy(
                            vtstage[:, bi2 * 128:bi2 * 128 + 128],
                            pb[6][:, 0:128])), waits=[("pe", pe_tr)])
                        vt_dv.append(P.c("dv"))
                        prev_tr_dv = vt_dv[-1]
                        S1["vt_evac"] = vt_dv[-1]
                    blk0v = c0 // 128
                    for hp in range(2):
                        hgl = 2 * vc_ + hp
                        P.step("gpsimd", (lambda e, hgl=hgl, hp=hp,
                                          blk0v=blk0v, nw=nw: e.dma_start(
                               out=vspill.rearrange("(r p) c -> p r c", p=128)
                                   [:, hgl * NB + blk0v:
                                    hgl * NB + blk0v + nw // 128, :],
                               in_=vtstage[:, 0:nw]
                                   .rearrange("p (b q) -> p b q", q=128)
                                   [:, :, hp * 64:hp * 64 + 64])),
                               waits=[("dv", vt_dv[-1])], dma="spv")
                    S1["vst"] = ("spv", P.c("spv"))
        tile_consumed[pos] = pe_last
        sps_at[ti] = P.c("sps")
        spv_at[ti] = P.c("spv")

    P.step("vector", lambda e: e.memset(vbuf[0][:, 64:NB * 65:65], 1.0),
           waits=[("dv", dv_pre)])
    P.step("vector", lambda e: e.memset(vbuf[1][:, 64:NB * 65:65], 1.0))
    dv_vones = P.c("dv")

    buf_free = {0: None, 1: None}
    pbuf_free = {0: None, 1: None}
    psS_free = {0: None, 1: None}

    conv_tasks = ([("g", i, half) for i in range(8) for half in range(2)]
                  + [("u", i, half) for i in range(8) for half in range(2)]
                  + [("d", i, 0) for i in range(24)])
    conv_guard = {}

    def emit_conv(idx):
        kind, i, half = conv_tasks[idx]
        par = idx % 2
        src_t, dst_t = {"g": (wgTi, wgb), "u": (wuTi, wub),
                        "d": (wdTi, wdb)}[kind]
        ncols = FF // 2 if kind != "d" else D
        c0c = half * ncols
        r0 = i * 128
        P.step("sync", (lambda e: e.dma_start(
            out=wcv_i[par][:, 0:ncols],
            in_=src_t[r0:r0 + 128, c0c:c0c + ncols])),
            waits=[("dv", conv_guard.get(("i", par)))],
            dma=("g3a" if par == 0 else "g3b"))
        ld_c = ("g3a" if par == 0 else "g3b",
                P.c("g3a" if par == 0 else "g3b"))
        P.step("vector", (lambda e: e.tensor_copy(
            wcv_o[par][:, 0:ncols], wcv_i[par][:, 0:ncols])),
            waits=[ld_c, ("wcv", conv_guard.get(("o", par)))])
        conv_guard[("i", par)] = P.c("dv")
        P.step("gpsimd", (lambda e: e.dma_start(
            out=dst_t[r0:r0 + 128, c0c:c0c + ncols],
            in_=wcv_o[par][:, 0:ncols])),
            waits=[("dv", conv_guard[("i", par)])], dma="wcv", chain=False)
        conv_guard[("o", par)] = P.c("wcv")

    hts = [(t, h) for t in range(4) for h in range(H)]
    S2 = {"psO_free": None, "ostg_free": None}

    def emit_li(li):
        t, h = hts[li]
        bi = li % 2
        if li < len(conv_tasks):
            emit_conv(li)
        nk = (4 * t + 4) * 128
        njt2 = 4 * t + 4
        njt = 2 * njt2

        P.step("sync", (lambda e, h=h, nk=nk, bi=bi: e.dma_start(
            out=kbuf[bi][0:66, 0:nk],
            in_=kspill[h * 66:h * 66 + 66, 0:nk])),
            waits=[("sps", sps_at[5 + t]), ("dv", dv_vones), buf_free[bi]],
            dma=("k2a" if bi == 0 else "k2b"))
        P.step("sync", (lambda e, h=h, nk=nk, bi=bi: e.dma_start(
            out=kbuf[bi][0:66, nk:2 * nk],
            in_=kspill[h * 66:h * 66 + 66, NQ:NQ + nk])),
            waits=[buf_free[bi]], dma=("k2a" if bi == 0 else "k2b"))
        P.step("gpsimd", (lambda e, h=h, njt2=njt2, bi=bi: e.dma_start(
            out=vbuf[bi][:, 0:njt2 * 65]
                .rearrange("p (j c) -> p j c", c=65)[:, :, 0:DH],
            in_=vspill[(h * NB) * 128:(h * NB + njt2) * 128, :]
                .rearrange("(j p) c -> p j c", p=128))),
            waits=[("spv", spv_at[5 + t]), buf_free[bi]],
            dma=("v2a" if bi == 0 else "v2b"))
        P.step("gpsimd", (lambda e, h=h, njt2=njt2, bi=bi: e.dma_start(
            out=vbuf[bi][:, njt2 * 65:2 * njt2 * 65]
                .rearrange("p (j c) -> p j c", c=65)[:, :, 0:DH],
            in_=vspill[(h * NB + 16) * 128:(h * NB + 16 + njt2) * 128, :]
                .rearrange("(j p) c -> p j c", p=128))),
            waits=[buf_free[bi]], dma=("v2a" if bi == 0 else "v2b"))
        P.step("sync", (lambda e, h=h, t=t, bi=bi: e.dma_start(
            out=qbuf[bi][0:66, :],
            in_=qspill[h * 66:h * 66 + 66, t * TT:(t + 1) * TT])),
            waits=[buf_free[bi]], dma=("k2a" if bi == 0 else "k2b"))
        ld_set = P.now("k2a" if bi == 0 else "k2b", "v2a" if bi == 0 else "v2b")

        exp_done = {}
        pv_done = {}

        def emit_S(j, bi=bi, t=t, njt2=njt2, ld_set=ld_set):
            sbk = j % 2
            own = j < njt2
            m = j if own else j - njt2
            diag = (m - 4 * t) if (4 * t <= m < 4 * t + 4) else None
            P.step("tensor", (lambda e, j=j, sbk=sbk, bi=bi: e.matmul(
                pb[sbk][:, :], kbuf[bi][0:66, j * 128:(j + 1) * 128],
                qbuf[bi][0:66, :], start=True, stop=True)),
                waits=ld_set +
                      [("ac", exp_done.get(j - 2)) if j >= 2 else psS_free[sbk]])
            pe_s = P.c("pe")
            dv_mask = None
            npre = min(m - 4 * t, 4) if m > 4 * t else 0
            if npre > 0:
                P.step("vector", (lambda e, sbk=sbk, npre=npre: e.memset(
                    pb[sbk][:, 0:npre * 128], NEG)),
                    waits=[("pe", pe_s)])
                dv_mask = P.c("dv")
            if diag is not None:
                mk = maskA if own else maskB
                P.step("vector", (lambda e, sbk=sbk, diag=diag, mk=mk:
                       e.tensor_add(pb[sbk][:, diag * 128:(diag + 1) * 128],
                                    pb[sbk][:, diag * 128:(diag + 1) * 128],
                                    mk[:])),
                       waits=[("pe", pe_s), ("pre", ld_pre)])
                dv_mask = P.c("dv")
            P.step("scalar", (lambda e, sbk=sbk: e.activation(
                pbuf[sbk][:], pb[sbk][:], AF.Exp, bias=consts[:, 1:2])),
                waits=[("pe", pe_s), ("dv", dv_pre), ("dv", dv_mask),
                       ("pe", pv_done.get(j - 2)) if j >= 2 else pbuf_free[sbk]])
            exp_done[j] = P.c("ac")

        def emit_PV(j, bi=bi, njt=njt):
            sbk = j % 2
            P.step("tensor", (lambda e, j=j, sbk=sbk, bi=bi, njt=njt: e.matmul(
                pb[2][0:65, :], vbuf[bi][:, j * 65:j * 65 + 65], pbuf[sbk][:],
                start=(j == 0), stop=(j == njt - 1))),
                waits=[("ac", exp_done[j]),
                       S2["psO_free"] if j == 0 else None])
            pv_done[j] = P.c("pe")

        emit_S(0)
        for j in range(1, njt):
            emit_S(j)
            emit_PV(j - 1)
        emit_PV(njt - 1)

        P.step("vector", (lambda e: e.reciprocal(rs_s[64:65, :],
                                                 pb[2][64:65, :])),
               waits=[("pe", pv_done[njt - 1]), S2["ostg_free"]])
        dv_rc = P.c("dv")
        P.step("tensor", (lambda e: e.matmul(
            pb[3][0:64, :], ones128[64:65, 0:64], rs_s[64:65, :],
            start=True, stop=True)), waits=[("dv", dv_rc)])
        pe_bc = P.c("pe")
        P.step("vector", lambda e: e.tensor_copy(bc_s[:], pb[3][0:64, :]),
               waits=[("pe", pe_bc)])
        P.step("vector", lambda e: e.tensor_mul(ostg[:], pb[2][0:64, :],
                                                bc_s[:]))
        dv_div = P.c("dv")
        S2["psO_free"] = ("dv", dv_div)
        P.step("scalar", (lambda e, h=h, t=t: e.dma_start(
            out=ospill[h * 64:(h + 1) * 64, t * TT:(t + 1) * TT],
            in_=ostg[:])), waits=[("dv", dv_div)], dma="spo")
        S2["ostg_free"] = ("spo", P.c("spo"))

        buf_free[bi] = ("pe", pv_done[njt - 1])
        pbuf_free[0] = ("pe", pv_done[njt - 2])
        pbuf_free[1] = ("pe", pv_done[njt - 1])
        psS_free[0] = ("ac", exp_done[njt - 2])
        psS_free[1] = ("ac", exp_done[njt - 1])

    order = [(0, None), (1, None), (5, []), (2, range(0, 4)),
             (6, range(4, 8)), (3, range(8, 12)), (7, range(12, 16)),
             (4, range(16, 24)), (8, range(24, 32))]
    pos = 0
    for ti, lis in order:
        emit_tile(ti, pos)
        pos += 1
        if lis:
            for li in lis:
                emit_li(li)
    for li in range(32, 64):
        emit_li(li)

    ph2_spo = P.c("spo")
    conv_total = P.c("wcv")

    state = {"statsv": None}

    def ln_stats(src_s, waits0):
        pe_sq2 = {}
        for cc in range(8):
            P.step("scalar", (lambda e, cc=cc, src_s=src_s: e.activation(
                sqc3[cc % 2][:], src_s[:, cc * TT:(cc + 1) * TT], AF.Square)),
                waits=list(waits0) + [("pe", pe_sq2.get(cc - 2))])
            ac_sq2 = P.c("ac")
            P.step("tensor", (lambda e, cc=cc: e.matmul(
                pb[3][:], ones128[:], sqc3[cc % 2][:],
                start=(cc == 0), stop=(cc == 7))),
                waits=[("ac", ac_sq2),
                       ("dv", state["statsv"]) if cc == 0 else None])
            pe_sq2[cc] = P.c("pe")
        P.step("tensor", (lambda e, src_s=src_s: [
            e.matmul(pb[2][:], ones128[:], src_s[:, cc * TT:(cc + 1) * TT],
                     start=(cc == 0), stop=(cc == 7)) for cc in range(8)][-1]),
            waits=list(waits0) + [("dv", state["statsv"]), ("dv", dv_pre)])
        pe_cs2 = P.c("pe")
        P.step("vector", lambda e: e.tensor_scalar(
            mean2[:], pb[2][:], 1.0 / D, None, ALU.mult),
            waits=[("pe", pe_cs2), ("pe", pe_sq2[7])])
        P.step("vector", lambda e: e.tensor_scalar(
            tmp2[:], pb[2][:], 1.0 / D, None, ALU.mult))
        P.step("vector", lambda e: e.tensor_mul(tmp2[:], mean2[:], tmp2[:]))
        P.step("vector", lambda e: e.scalar_tensor_tensor(
            rstd2[:], pb[3][:], 1.0 / D, tmp2[:], ALU.mult, ALU.subtract))
        P.step("vector", lambda e: e.tensor_scalar(
            rstd2[:], rstd2[:], EPS, None, ALU.add))
        P.step("vector", lambda e: e.reciprocal(rstd2[:], rstd2[:]))
        dv_sv = P.c("dv")
        state["statsv"] = dv_sv
        P.step("scalar", lambda e: e.activation(rstd2[:], rstd2[:], AF.Sqrt),
               waits=[("dv", dv_sv)])
        return P.c("ac")

    prev_pe_op = None
    prev_dv_t1 = None
    prev_out = None
    prev_ffn_pp = []
    prev_wg_mm = {}
    prev_ds = None

    for tt_i in range(4):
        c0 = tt_i * TT

        P.step("sync", (lambda e, c0=c0: e.dma_start(
            out=ot_s[:].rearrange("p (k c) -> p k c", k=8),
            in_=ospill[:, c0:c0 + TT].rearrange("(k p) c -> p k c", p=128))),
            waits=[("spo", ph2_spo), ("pe", prev_pe_op)], dma="o3")
        P.step("sync", (lambda e, c0=c0: e.dma_start(
            out=xo_s[:].rearrange("p (k c) -> p k c", k=8),
            in_=xT[:, c0:c0 + TT].rearrange("(k p) c -> p k c", p=128))),
            waits=[("dv", prev_dv_t1)], dma="o3")
        ld_o = ("o3", P.c("o3"))

        po_ev = {}
        pe_o2 = {}
        ld_wo = {}
        for oc in range(8):
            P.step("sync", (lambda e, oc=oc: e.dma_start(
                out=wo_s[oc % 2][:].rearrange("p (k c) -> p k c", k=8),
                in_=woutT[:, oc * 128:(oc + 1) * 128]
                    .rearrange("(k p) c -> p k c", p=128))),
                waits=[("pe", pe_o2.get(oc - 2, prev_pe_op))],
                dma=("w3a" if oc % 2 == 0 else "w3b"))
            ld_wo[oc] = ("w3a" if oc % 2 == 0 else "w3b", P.c("w3a" if oc % 2 == 0 else "w3b"))
            P.step("tensor", (lambda e, oc=oc: [
                e.matmul(pb[oc % 2][:],
                         wo_s[oc % 2][:, kc * 128:(kc + 1) * 128],
                         ot_s[:, kc * TT:(kc + 1) * TT],
                         start=(kc == 0), stop=(kc == 7))
                for kc in range(8)][-1]),
                waits=[ld_o, ld_wo[oc],
                       ("dv", po_ev.get(oc - 2, prev_dv_t1))] + prev_ffn_pp)
            pe_o2[oc] = P.c("pe")
            P.step("vector", (lambda e, oc=oc: e.tensor_copy(
                po_s[:, oc * TT:(oc + 1) * TT], pb[oc % 2][:])),
                waits=[("pe", pe_o2[oc])])
            po_ev[oc] = P.c("dv")
        prev_pe_op = pe_o2[7]

        ac_r2 = ln_stats(po_s, [("dv", po_ev[7]), ("out", prev_out)])
        for cc in range(8):
            P.step("vector", (lambda e, cc=cc: e.tensor_sub(
                t1_s[:, cc * TT:(cc + 1) * TT], po_s[:, cc * TT:(cc + 1) * TT],
                mean2[:])), waits=[("ac", ac_r2)])
            P.step("vector", (lambda e, cc=cc: e.tensor_mul(
                t1_s[:, cc * TT:(cc + 1) * TT], t1_s[:, cc * TT:(cc + 1) * TT],
                rstd2[:])))
            P.step("vector", (lambda e, cc=cc: e.tensor_scalar(
                t1_s[:, cc * TT:(cc + 1) * TT], t1_s[:, cc * TT:(cc + 1) * TT],
                g2s[:, cc:cc + 1], b2s[:, cc:cc + 1], ALU.mult, ALU.add)))
            P.step("vector", (lambda e, cc=cc: e.tensor_add(
                t1_s[:, cc * TT:(cc + 1) * TT], t1_s[:, cc * TT:(cc + 1) * TT],
                xo_s[:, cc * TT:(cc + 1) * TT])), waits=[ld_o])
        dv_t1 = P.c("dv")
        prev_dv_t1 = dv_t1

        ac_r3 = ln_stats(t1_s, [("dv", dv_t1)])
        for cc in range(8):
            P.step("vector", (lambda e, cc=cc: e.tensor_sub(
                po_s[:, cc * TT:(cc + 1) * TT], t1_s[:, cc * TT:(cc + 1) * TT],
                mean2[:])), waits=[("ac", ac_r3)])
            P.step("vector", (lambda e, cc=cc: e.tensor_mul(
                po_s[:, cc * TT:(cc + 1) * TT], po_s[:, cc * TT:(cc + 1) * TT],
                rstd2[:])))
        dv_x1 = P.c("dv")

        for cc in range(8):
            P.step("vector", (lambda e, cc=cc: e.tensor_copy(
                x1h[:, cc * TT:(cc + 1) * TT],
                po_s[:, cc * TT:(cc + 1) * TT])), waits=[("dv", dv_x1)])
            P.step("vector", (lambda e, cc=cc: e.tensor_sub(
                x1l[:, cc * TT:(cc + 1) * TT], po_s[:, cc * TT:(cc + 1) * TT],
                x1h[:, cc * TT:(cc + 1) * TT])))
        dv_x1s = P.c("dv")

        umm_done = {}
        sil_done = {}
        hmul_done = {}
        wg_mm = {}
        wd_used = dict(state.get("wd_used_prev", {}))
        ds_ev_last = None
        ds_add_last = None
        for half in range(2):
            for ci in range(12):
                hc = half * 12 + ci
                wb = hc % 2
                wprev = wg_mm.get(hc - 2, prev_wg_mm.get(22 + hc))
                P.step("sync", (lambda e, hc=hc, wb=wb: e.dma_start(
                    out=wgb_s[wb][:].rearrange("p (k c) -> p k c", k=8),
                    in_=wgb[:, hc * 128:(hc + 1) * 128]
                        .rearrange("(k p) c -> p k c", p=128))),
                    waits=[("pe", wprev), ("wcv", conv_total)],
                    dma=("g3a" if wb == 0 else "g3b"))
                P.step("sync", (lambda e, hc=hc, wb=wb: e.dma_start(
                    out=wub_s[wb][:].rearrange("p (k c) -> p k c", k=8),
                    in_=wub[:, hc * 128:(hc + 1) * 128]
                        .rearrange("(k p) c -> p k c", p=128))),
                    dma=("g3a" if wb == 0 else "g3b"))
                ld_gu = ("g3a" if wb == 0 else "g3b",
                         P.c("g3a" if wb == 0 else "g3b"))

                gbk = pb[0] if hc % 2 == 0 else pb[2]
                ubk = pb[1] if hc % 2 == 0 else pb[3]
                gsb = gs_s2[hc % 2]
                P.step("tensor", (lambda e, wb=wb, gbk=gbk: [
                    e.matmul(gbk[:], wgb_s[wb][:, kc * 128:(kc + 1) * 128],
                             xs_[:, kc * TT:(kc + 1) * TT],
                             start=(pi == 0 and kc == 0),
                             stop=(pi == 1 and kc == 7))
                    for pi, xs_ in enumerate((x1h, x1l)) for kc in range(8)][-1]),
                    waits=[ld_gu, ("dv", dv_x1s),
                           ("ac", sil_done.get(hc - 2)),
                           ("dv", po_ev[7] if hc < 2 else None)] +
                          (prev_ffn_pp if hc < 2 else []))
                pe_g = P.c("pe")
                P.step("tensor", (lambda e, wb=wb, ubk=ubk: [
                    e.matmul(ubk[:], wub_s[wb][:, kc * 128:(kc + 1) * 128],
                             xs_[:, kc * TT:(kc + 1) * TT],
                             start=(pi == 0 and kc == 0),
                             stop=(pi == 1 and kc == 7))
                    for pi, xs_ in enumerate((x1h, x1l)) for kc in range(8)][-1]),
                    waits=[ld_gu, ("dv", dv_x1s), ("dv", hmul_done.get(hc - 2)),
                           ("dv", po_ev[7] if hc < 2 else None)] +
                          (prev_ffn_pp if hc < 2 else []))
                umm_done[hc] = P.c("pe")
                wg_mm[hc] = umm_done[hc]

                P.step("scalar", (lambda e, gbk=gbk, gsb=gsb: e.activation(
                    gsb[:], gbk[:], AF.Silu, scale=1.0 / 127.0)),
                    waits=[("pe", pe_g), ("dv", hmul_done.get(hc - 2))])
                sil_done[hc] = P.c("ac")
                P.step("vector", (lambda e, ubk=ubk, gsb=gsb: e.tensor_mul(
                    htmp[:], ubk[:], gsb[:])),
                    waits=[("ac", sil_done[hc]), ("pe", umm_done[hc])])
                P.step("vector", (lambda e, ci=ci: e.tensor_copy(
                    hh_s[:, ci * TT:(ci + 1) * TT], htmp[:])))
                P.step("vector", (lambda e, ci=ci: e.tensor_sub(
                    hl_s[:, ci * TT:(ci + 1) * TT], htmp[:],
                    hh_s[:, ci * TT:(ci + 1) * TT])))
                hmul_done[hc] = P.c("dv")

            for dcg in range(2):
                dmm_last = state.get("dmm_prev")
                for hcg in range(3):
                    gidx = half * 6 + dcg * 3 + hcg
                    wb = gidx % 2
                    r0w = (half * 12 + hcg * 4) * 128
                    P.step("sync", (lambda e, r0w=r0w, wb=wb, dcg=dcg:
                           e.dma_start(
                               out=wdb_s[wb][:]
                                   .rearrange("p (q c) -> p q c", q=4),
                               in_=wdb[r0w:r0w + 512, dcg * TT:(dcg + 1) * TT]
                                   .rearrange("(q p) c -> p q c", p=128))),
                           waits=[("pe", wd_used.get(gidx - 2,
                                                     state.get("dmm_prev"))),
                                  ("wcv", conv_total)],
                           dma=("d3a" if wb == 0 else "d3b"))
                    ld_d = ("d3a" if wb == 0 else "d3b",
                            P.c("d3a" if wb == 0 else "d3b"))
                    for cq in range(4):
                        ci = hcg * 4 + cq
                        P.step("tensor", (lambda e, ci=ci, cq=cq, wb=wb: [
                            e.matmul(pb[4 + dq][:],
                                     wdb_s[wb][:, cq * TT + dq * 128:
                                               cq * TT + (dq + 1) * 128],
                                     hs_[:, ci * TT:(ci + 1) * TT],
                                     start=(ci == 0 and pi == 0),
                                     stop=(ci == 11 and pi == 1))
                            for dq in range(4) for pi, hs_ in
                            enumerate((hh_s, hl_s))][-1]),
                            waits=[ld_d,
                                   ("dv", hmul_done[half * 12 + 11]),
                                   ("ac", ds_ev_last) if (ci == 0 and half == 0 and dcg == 1) else None,
                                   ("dv", ds_add_last) if (ci == 0 and half == 1 and dcg == 1) else None,
                                   ("dv", prev_ds) if (ci == 0 and half == 0 and dcg == 0) else None,
                                   ("ac", ds_ev_last) if (ci == 0 and half == 1 and dcg == 0) else None])
                        dmm_last = P.c("pe")
                    wd_used[gidx] = dmm_last
                state["dmm_prev"] = dmm_last
                if half == 0:
                    for dq in range(4):
                        dc = dcg * 4 + dq
                        P.step("scalar", (lambda e, dq=dq, dc=dc: e.activation(
                            scr_s[:, dc * TT:(dc + 1) * TT], pb[4 + dq][:],
                            AF.Copy, scale=DSCALE)),
                            waits=[("pe", dmm_last), ("out", prev_out)])
                    ds_ev_last = P.c("ac")
                else:
                    for dq in range(4):
                        dc = dcg * 4 + dq
                        P.step("vector", (lambda e, dq=dq, dc=dc:
                               e.scalar_tensor_tensor(
                                   scr_s[:, dc * TT:(dc + 1) * TT],
                                   pb[4 + dq][:], DSCALE,
                                   scr_s[:, dc * TT:(dc + 1) * TT],
                                   ALU.mult, ALU.add)),
                               waits=[("pe", dmm_last), ("ac", ds_ev_last)])
                    ds_add_last = P.c("dv")
        prev_wg_mm = dict(wg_mm)
        state["wd_used_prev"] = {k - 12: v for k, v in wd_used.items()}
        prev_ffn_pp = [("ac", sil_done[23]), ("dv", hmul_done[23])]

        for cc in range(8):
            P.step("vector", (lambda e, cc=cc: e.tensor_add(
                t1_s[:, cc * TT:(cc + 1) * TT], po_s[:, cc * TT:(cc + 1) * TT],
                scr_s[:, cc * TT:(cc + 1) * TT])),
                waits=[("dv", ds_add_last), ("dv", dv_x1)])
        dv_t2 = P.c("dv")
        prev_ds = dv_t2

        ac_r4 = ln_stats(t1_s, [("dv", dv_t2)])
        for cc in range(8):
            P.step("vector", (lambda e, cc=cc: e.tensor_sub(
                scr_s[:, cc * TT:(cc + 1) * TT], t1_s[:, cc * TT:(cc + 1) * TT],
                mean2[:])), waits=[("ac", ac_r4)])
            P.step("vector", (lambda e, cc=cc: e.tensor_mul(
                scr_s[:, cc * TT:(cc + 1) * TT], scr_s[:, cc * TT:(cc + 1) * TT],
                rstd2[:])))
        dv_y = P.c("dv")

        P.step("scalar", (lambda e, c0=c0: e.dma_start(
            out=yT[:, c0:c0 + TT].rearrange("(k p) c -> p k c", p=128),
            in_=scr_s[:].rearrange("p (k c) -> p k c", k=8))),
            waits=[("dv", dv_y)], dma="out")
        prev_out = P.c("out")

    out_total = P.c("out")

    def emitter(ops, final_wait=None):
        def run(eng):
            for fn, waits, inc in ops:
                for s, v in waits:
                    eng.wait_ge(SEMS[s], v)
                hh = fn(eng)
                if inc is not None:
                    if isinstance(hh, list):
                        hh = hh[-1]
                    hh.then_inc(SEMS[inc[0]], inc[1])
            if final_wait is not None:
                eng.wait_ge(SEMS[final_wait[0]], final_wait[1])
        return run

    with nc.Block() as block:
        block.sync(emitter(P.q["sync"], final_wait=("out", out_total)))
        block.tensor(emitter(P.q["tensor"]))
        block.vector(emitter(P.q["vector"]))
        block.scalar(emitter(P.q["scalar"]))
        block.gpsimd(emitter(P.q["gpsimd"]))
    for cm in reversed(sem_ctx):
        cm.__exit__(None, None, None)
    return nc



def _core_layout(r):
    own = [2 * s + r for s in range(16)]
    other = [2 * s + (1 - r) for s in range(16)]
    blocks = own + other
    perm = np.concatenate([np.arange(128 * g, 128 * g + 128) for g in blocks])
    apx = np.array([max(128 * g - 1, 0) for g in blocks], dtype=np.int64)
    apx_valid = np.array([1.0 if g > 0 else 0.0 for g in blocks],
                         dtype=np.float32)
    return blocks, perm, apx, apx_valid


def kernel(L0_attn_x, L0_attn_ln1_g, L0_attn_ln1_b, L0_attn_in_w,
           L0_attn_out_w, L0_attn_ln2_g, L0_attn_ln2_b, L0_attn_slopes,
           L0_attn_smear, L0_ffn_w_gate, L0_ffn_w_up, L0_ffn_w_down):
    x = np.ascontiguousarray(np.asarray(L0_attn_x, dtype=np.float32))
    winT = np.ascontiguousarray(np.asarray(L0_attn_in_w, np.float32).T)
    woutT = np.ascontiguousarray(np.asarray(L0_attn_out_w, np.float32).T)
    wgT = np.ascontiguousarray(np.asarray(L0_ffn_w_gate, np.int32).T)
    wuT = np.ascontiguousarray(np.asarray(L0_ffn_w_up, np.int32).T)
    wdT = np.ascontiguousarray(np.asarray(L0_ffn_w_down, np.int32).T)
    g1c = np.ascontiguousarray(np.asarray(L0_attn_ln1_g, np.float32)
                               .reshape(8, 128).T)
    b1c = np.ascontiguousarray(np.asarray(L0_attn_ln1_b, np.float32)
                               .reshape(8, 128).T)
    g2c = np.ascontiguousarray(np.asarray(L0_attn_ln2_g, np.float32)
                               .reshape(8, 128).T)
    b2c = np.ascontiguousarray(np.asarray(L0_attn_ln2_b, np.float32)
                               .reshape(8, 128).T)
    slopes = np.ascontiguousarray(
        np.asarray(L0_attn_slopes, np.float32).reshape(H, 1))
    smear = np.asarray(L0_attn_smear, np.float32)
    smearb = np.ascontiguousarray(np.repeat(smear, 64).reshape(8, 128).T)
    maskA = np.where(np.arange(128)[:, None] > np.arange(128)[None, :],
                     np.float32(NEG), np.float32(0.0)).astype(np.float32)

    nc = build_nc()
    in_maps = []
    metas = []
    for c in range(8):
        b, r = c // 2, c % 2
        blocks, perm, apx, apx_valid = _core_layout(r)
        xTb = np.ascontiguousarray(x[b].T)
        xcore = xTb[:, perm]
        xapx = np.zeros((D, 128), dtype=np.float32)
        xapx[:, 0:NB] = xTb[:, apx]
        xin = np.ascontiguousarray(np.concatenate([xcore, xapx], axis=1))
        kpos = np.ascontiguousarray(
            np.broadcast_to(perm.astype(np.float32)[None, :], (16, L)))
        maskB = (np.zeros((128, 128), np.float32) if r == 1
                 else np.full((128, 128), NEG, np.float32))
        bmask = np.ascontiguousarray(
            np.broadcast_to(apx_valid[None, :], (128, NB)).copy())
        in_maps.append({
            "xT": xin, "winT": winT, "woutT": woutT,
            "wgTi": wgT, "wuTi": wuT, "wdTi": wdT,
            "g1i": g1c, "b1i": b1c, "g2i": g2c, "b2i": b2c,
            "slopesi": slopes, "smeari": smearb, "kposi": kpos,
            "maskAi": maskA, "maskBi": maskB, "bmaski": bmask,
        })
        metas.append((b, perm))

    res = run_bass_kernel_spmd(nc, in_maps, core_ids=list(range(8)))

    out = np.empty((4, L, D), dtype=np.float32)
    for c in range(8):
        b, perm = metas[c]
        yTc = np.asarray(res.results[c]["yT"])
        out[b, perm[:NQ], :] = yTc.T
    return out


# version 14
# speedup vs baseline: 1.2558x; 1.0524x over previous
import numpy as np

import concourse.bass as bass
import concourse.mybir as mybir
from concourse.bass_utils import run_bass_kernel_spmd

F32 = mybir.dt.float32
I32 = mybir.dt.int32
AF = mybir.ActivationFunctionType
ALU = mybir.AluOpType

D = 1024
H = 16
DH = 64
FF = 3072
L = 4096
NB = 32
TT = 512
NT = 8
NQ = 2048
NEG = -1.0e9
CSHIFT = -4.0
EPS = 1e-5
DSCALE = 1.0 / (127.0 * 127.0)
ENGS = ("sync", "tensor", "vector", "scalar", "gpsimd")
OWN = {"tensor": "pe", "vector": "dv", "scalar": "ac", "gpsimd": "gp"}
ALL_SEMS = ("pre", "x0", "x1", "k2a", "k2b", "v2a", "v2b", "o3",
            "w3a", "w3b", "g3a", "g3b", "d3a", "d3b",
            "sps", "spv", "spo", "wcv", "out", "pe", "dv", "ac", "gp")


class Plan:

    def __init__(self):
        self.cnt = {}
        self.q = {e: [] for e in ENGS}

    def c(self, s):
        return self.cnt.get(s, 0)

    ENG_SEMS = ("pe", "dv", "ac", "gp")

    def step(self, eng, fn, waits=(), dma=None, chain=True):
        ws = {}
        for item in waits:
            if item is None:
                continue
            s, v = item
            if v:
                ws[s] = max(ws.get(s, 0), v)
        inc = None
        if dma is not None:
            prior = self.cnt.get(dma, 0)
            if prior:
                ws[dma] = max(ws.get(dma, 0), prior)
            inc = (dma, 16)
        elif eng in OWN:
            own = OWN[eng]
            if chain and eng in ("vector", "scalar", "gpsimd"):
                prev = self.cnt.get(own, 0)
                if prev:
                    ws[own] = max(ws.get(own, 0), prev)
            inc = (own, 1)
        if inc:
            self.cnt[inc[0]] = self.cnt.get(inc[0], 0) + inc[1]
        self.q[eng].append((fn, sorted(ws.items()), inc))
        return self.cnt.get(inc[0], 0) if inc else 0

    def now(self, *sems):
        return [(s, self.cnt.get(s, 0)) for s in sems]


def build_nc(debug=False):
    nc = bass.Bass("TRN2", target_bir_lowering=False)

    xT = nc.dram_tensor("xT", [D, L + 128], F32, kind="ExternalInput").ap()
    winT = nc.dram_tensor("winT", [D, 3 * D], F32, kind="ExternalInput").ap()
    woutT = nc.dram_tensor("woutT", [D, D], F32, kind="ExternalInput").ap()
    wgTi = nc.dram_tensor("wgTi", [D, FF], I32, kind="ExternalInput").ap()
    wuTi = nc.dram_tensor("wuTi", [D, FF], I32, kind="ExternalInput").ap()
    wdTi = nc.dram_tensor("wdTi", [FF, D], I32, kind="ExternalInput").ap()
    g1i = nc.dram_tensor("g1i", [128, 8], F32, kind="ExternalInput").ap()
    b1i = nc.dram_tensor("b1i", [128, 8], F32, kind="ExternalInput").ap()
    g2i = nc.dram_tensor("g2i", [128, 8], F32, kind="ExternalInput").ap()
    b2i = nc.dram_tensor("b2i", [128, 8], F32, kind="ExternalInput").ap()
    slopesi = nc.dram_tensor("slopesi", [16, 1], F32, kind="ExternalInput").ap()
    smeari = nc.dram_tensor("smeari", [128, 8], F32, kind="ExternalInput").ap()
    kposi = nc.dram_tensor("kposi", [16, L], F32, kind="ExternalInput").ap()
    maskAi = nc.dram_tensor("maskAi", [128, 128], F32, kind="ExternalInput").ap()
    maskBi = nc.dram_tensor("maskBi", [128, 128], F32, kind="ExternalInput").ap()
    bmaski = nc.dram_tensor("bmaski", [128, NB], F32, kind="ExternalInput").ap()
    identi = nc.inline_tensor(np.eye(128, dtype=np.float32), name="identi").ap()
    yT = nc.dram_tensor("yT", [D, NQ], F32, kind="ExternalOutput").ap()

    spk = "ExternalOutput" if debug else "Internal"
    kspill = nc.dram_tensor("kspill", [H * 66, L], F32, kind=spk).ap()
    vspill = nc.dram_tensor("vspill", [H * NB * 128, DH], F32, kind=spk).ap()
    qspill = nc.dram_tensor("qspill", [H * 66, NQ], F32, kind=spk).ap()
    ospill = nc.dram_tensor("ospill", [D, NQ], F32, kind=spk).ap()
    BF16 = mybir.dt.bfloat16
    wgb = nc.dram_tensor("wgb", [D, FF], BF16, kind="Internal").ap()
    wub = nc.dram_tensor("wub", [D, FF], BF16, kind="Internal").ap()
    wdb = nc.dram_tensor("wdb", [FF, D], BF16, kind="Internal").ap()

    ARENA = 206 * 1024
    arena = nc.alloc_sbuf_tensor("arena", [128, ARENA], mybir.dt.uint8)
    base0 = nc.lookup_mloc(arena).addr
    cur = [base0]

    def place(name, shape, dtype=F32):
        nbytes = int(np.prod(shape[1:])) * mybir.dt.size(dtype)
        off = (cur[0] + 31) // 32 * 32
        cur[0] = off + nbytes
        assert cur[0] <= base0 + ARENA, f"arena overflow at {name}"
        return nc.alloc_sbuf_tensor_at(name, shape, dtype, offset=off).ap()

    consts = place("consts", [128, 4])
    ones128 = place("ones128", [128, 128])
    maskA = place("maskA", [128, 128])
    maskB = place("maskB", [128, 128])
    g1s = place("g1s", [128, 8]); b1s = place("b1s", [128, 8])
    g2s = place("g2s", [128, 8]); b2s = place("b2s", [128, 8])
    smear_s = place("smear_s", [128, 8])
    ident = place("ident", [128, 128])
    pbase = cur[0]

    wi_s = [place(f"wi_s{i}", [128, 8 * 128]) for i in range(2)]
    kpos_s = place("kpos_s", [16, L])
    slopes_s = place("slopes_s", [16, 1])
    bmask = place("bmask", [128, NB])
    xt = [place(f"xt{i}", [128, 8 * TT]) for i in range(2)]
    sqc = place("sqc", [128, TT])
    meanb = place("meanb", [128, TT])
    rstdb = place("rstdb", [128, TT])
    tmpb = place("tmpb", [128, TT])
    kstage = place("kstage", [128, TT])
    ksm = place("ksm", [128, TT])
    kprevs = place("kprevs", [128, 8 * NB])
    vstage = place("vstage", [128, TT])
    vtstage = place("vtstage", [128, TT])
    qstage = place("qstage", [128, TT])
    arow = place("arow", [16, L])
    qr64 = place("qr64", [16, NQ])
    qr65 = place("qr65", [16, NQ])

    kbuf = [place(f"kbuf{i}", [128, L]) for i in range(2)]
    vbuf = [place(f"vbuf{i}", [128, NB * 65]) for i in range(2)]
    qbuf = [place(f"qbuf{i}", [128, TT]) for i in range(2)]
    pbuf = [place(f"pbuf{i}", [128, TT]) for i in range(2)]
    rs_s = place("rs_s", [65, TT])
    bc_s = place("bc_s", [64, TT])
    ostg = place("ostg", [64, TT])
    wcv_i = [place(f"wcv_i{i}", [128, FF // 2], I32) for i in range(2)]
    wcv_o = [place(f"wcv_o{i}", [128, FF // 2], BF16) for i in range(2)]

    cur[0] = pbase
    ot_s = place("ot_s", [128, 8 * TT])
    xo_s = place("xo_s", [128, 8 * TT])
    po_s = place("po_s", [128, 8 * TT])
    t1_s = place("t1_s", [128, 8 * TT])
    scr_s = place("scr_s", [128, 8 * TT])
    hh_s = place("hh_s", [128, 12 * TT], BF16)
    hl_s = place("hl_s", [128, 12 * TT], BF16)
    htmp = place("htmp", [128, TT])
    x1h = place("x1h", [128, 8 * TT], BF16)
    x1l = place("x1l", [128, 8 * TT], BF16)
    wo_s = [place(f"wo_s{i}", [128, 8 * 128]) for i in range(2)]
    wgb_s = [place(f"wgb_s{i}", [128, 8 * 128], BF16) for i in range(2)]
    wub_s = [place(f"wub_s{i}", [128, 8 * 128], BF16) for i in range(2)]
    wdb_s = [place(f"wdb_s{i}", [128, 4 * TT], BF16) for i in range(2)]
    gs_s2 = [place(f"gs_s{i}", [128, TT]) for i in range(2)]
    mean2 = place("mean2", [128, TT])
    rstd2 = place("rstd2", [128, TT])
    tmp2 = place("tmp2", [128, TT])
    sqc3 = [place(f"sqc3_{i}", [128, TT]) for i in range(2)]

    pb = [nc.alloc_psum_tensor(f"pb{i}", [128, TT], F32).ap() for i in range(8)]

    SEMS = {}
    sem_ctx = []
    for s in ALL_SEMS:
        cm = nc.semaphore("s_" + s)
        SEMS[s] = cm.__enter__()
        sem_ctx.append(cm)

    P = Plan()

    for dst, src in ((g1s, g1i), (b1s, b1i), (g2s, g2i), (b2s, b2i),
                     (slopes_s, slopesi), (smear_s, smeari), (kpos_s, kposi),
                     (maskA, maskAi), (maskB, maskBi), (bmask, bmaski),
                     (ident, identi)):
        P.step("sync", (lambda e, dst=dst, src=src:
                        e.dma_start(out=dst[:], in_=src[:])), dma="pre")
    ld_pre = P.c("pre")

    P.step("vector", lambda e: e.memset(consts[:, 0:1], 1.0))
    P.step("vector", lambda e: e.memset(consts[:, 1:2], CSHIFT))
    P.step("vector", lambda e: e.memset(ones128[:], 1.0))
    dv_pre = P.c("dv")

    P.step("scalar", lambda e: e.activation(smear_s[:], smear_s[:], AF.Sigmoid),
           waits=[("pre", ld_pre)])
    ac_sig = P.c("ac")

    P.step("vector", lambda e: e.memset(qr64[:], 1.0),
           waits=[("pre", ld_pre)])
    P.step("vector", lambda e: e.tensor_copy(qr65[:], kpos_s[:, 0:NQ]),
           waits=[("pre", ld_pre)])
    dv_q45 = P.c("dv")
    P.step("scalar", (lambda e: e.dma_start(
        out=qspill.rearrange("(h r) c -> h r c", r=66)[:, 64:65, :],
        in_=qr64[:].rearrange("h (r c) -> h r c", r=1))),
        waits=[("dv", dv_q45)], dma="sps")
    P.step("scalar", (lambda e: e.dma_start(
        out=qspill.rearrange("(h r) c -> h r c", r=66)[:, 65:66, :],
        in_=qr65[:].rearrange("h (r c) -> h r c", r=1))),
        waits=[("dv", dv_q45)], dma="sps")

    P.step("vector", lambda e: e.tensor_scalar(
        arow[:], kpos_s[:], slopes_s[:, 0:1], None, ALU.mult),
        waits=[("pre", ld_pre)])
    dva = P.c("dv")
    P.step("scalar", (lambda e: e.dma_start(
        out=kspill.rearrange("(h r) c -> h r c", r=66)[:, 64:65, :],
        in_=arow[:].rearrange("h (r c) -> h r c", r=1))),
        waits=[("dv", dva)], dma="sps")
    sp_a1 = P.c("sps")
    P.step("vector", lambda e: e.tensor_scalar(
        arow[:], kpos_s[:], 0.0, 1.0, ALU.mult, ALU.add),
        waits=[("sps", sp_a1)])
    P.step("vector", lambda e: e.tensor_scalar(
        arow[:], arow[:], slopes_s[:, 0:1], -1.0, ALU.mult, ALU.mult))
    dva2 = P.c("dv")
    P.step("scalar", (lambda e: e.dma_start(
        out=kspill.rearrange("(h r) c -> h r c", r=66)[:, 65:66, :],
        in_=arow[:].rearrange("h (r c) -> h r c", r=1))),
        waits=[("dv", dva2)], dma="sps")

    tiles = [("apx", L, 128)] + [("main", t * TT, TT) for t in range(NT)]
    tile_consumed = {}
    bank_free = {4: None, 5: None}
    wi_guard = {}
    sps_at = {}
    spv_at = {}
    S1 = {"kst": None, "vst": None, "qst": None, "statsv": None,
          "vt_evac": None, "sq_pe": None}
    vtt_free = {}

    def emit_tile(ti, pos):
        kind, c0, nw = tiles[ti]
        buf = xt[pos % 2]
        is_own = (kind == "main" and c0 < NQ)
        oc_list = (list(range(8, 16)) if kind == "apx"
                   else (list(range(24)) if is_own else list(range(8, 24))))

        P.step("sync", (lambda e, buf=buf, c0=c0, nw=nw: e.dma_start(
            out=buf[:, 0:8 * nw].rearrange("p (k c) -> p k c", k=8),
            in_=xT[:, c0:c0 + nw].rearrange("(k p) c -> p k c", p=128))),
            waits=[("pe", tile_consumed.get(pos - 2))], dma=("x0" if pos % 2 == 0 else "x1"))
        ld_x = ("x0" if pos % 2 == 0 else "x1", P.c("x0" if pos % 2 == 0 else "x1"))

        P.step("tensor", (lambda e, buf=buf, nw=nw: [
            e.matmul(pb[6][:, 0:nw], ones128[:],
                     buf[:, cc * nw:(cc + 1) * nw],
                     start=(cc == 0), stop=(cc == 7)) for cc in range(8)][-1]),
            waits=[ld_x, ("dv", dv_pre), ("dv", S1["statsv"]),
                   ("dv", S1["vt_evac"])])
        pe_cs = P.c("pe")

        pe_sq = None
        for cc in range(8):
            P.step("scalar", (lambda e, buf=buf, nw=nw, cc=cc: e.activation(
                sqc[:, 0:nw], buf[:, cc * nw:(cc + 1) * nw], AF.Square)),
                waits=[ld_x, ("pe", pe_sq),
                       ("dv", S1["statsv"] if cc == 0 else None)])
            ac_sqd = P.c("ac")
            P.step("tensor", (lambda e, nw=nw, cc=cc: e.matmul(
                pb[7][:, 0:nw], ones128[:], sqc[:, 0:nw],
                start=(cc == 0), stop=(cc == 7))),
                waits=[("ac", ac_sqd), ("pe", pe_cs if cc == 0 else None),
                       ("dv", dv_pre if cc == 0 else None)])
            pe_sq = P.c("pe")
        S1["sq_pe"] = pe_sq

        P.step("vector", (lambda e, nw=nw: e.tensor_scalar(
            meanb[:, 0:nw], pb[6][:, 0:nw], 1.0 / D, None, ALU.mult)),
            waits=[("pe", pe_sq)])
        P.step("vector", (lambda e, nw=nw: e.tensor_scalar(
            tmpb[:, 0:nw], pb[6][:, 0:nw], 1.0 / D, None, ALU.mult)))
        P.step("vector", (lambda e, nw=nw: e.tensor_mul(
            tmpb[:, 0:nw], meanb[:, 0:nw], tmpb[:, 0:nw])))
        P.step("vector", (lambda e, nw=nw: e.scalar_tensor_tensor(
            rstdb[:, 0:nw], pb[7][:, 0:nw], 1.0 / D, tmpb[:, 0:nw],
            ALU.mult, ALU.subtract)))
        P.step("vector", (lambda e, nw=nw: e.tensor_scalar(
            rstdb[:, 0:nw], rstdb[:, 0:nw], EPS, None, ALU.add)))
        P.step("vector", (lambda e, nw=nw: e.reciprocal(
            rstdb[:, 0:nw], rstdb[:, 0:nw])))
        dv_var = P.c("dv")
        P.step("scalar", (lambda e, nw=nw: e.activation(
            rstdb[:, 0:nw], rstdb[:, 0:nw], AF.Sqrt)),
            waits=[("dv", dv_var)])
        ac_rstd = P.c("ac")

        for cc in range(8):
            P.step("vector", (lambda e, buf=buf, nw=nw, cc=cc: e.tensor_sub(
                buf[:, cc * nw:(cc + 1) * nw], buf[:, cc * nw:(cc + 1) * nw],
                meanb[:, 0:nw])), waits=[("ac", ac_rstd)])
            P.step("vector", (lambda e, buf=buf, nw=nw, cc=cc: e.tensor_mul(
                buf[:, cc * nw:(cc + 1) * nw], buf[:, cc * nw:(cc + 1) * nw],
                rstdb[:, 0:nw])))
            P.step("vector", (lambda e, buf=buf, nw=nw, cc=cc: e.tensor_scalar(
                buf[:, cc * nw:(cc + 1) * nw], buf[:, cc * nw:(cc + 1) * nw],
                g1s[:, cc:cc + 1], b1s[:, cc:cc + 1], ALU.mult, ALU.add)))
        dv_std = P.c("dv")
        S1["statsv"] = dv_std

        pe_last = None
        for oi, oc in enumerate(oc_list):
            bank = 4 + oi % 2
            wsl = oi % 2
            wsem = "w3a" if wsl == 0 else "w3b"
            P.step("sync", (lambda e, oc=oc, wsl=wsl: e.dma_start(
                out=wi_s[wsl][:].rearrange("p (k c) -> p k c", k=8),
                in_=winT[:, oc * 128:(oc + 1) * 128]
                    .rearrange("(k p) c -> p k c", p=128))),
                waits=[wi_guard.get(wsl)], dma=wsem)
            ld_wi = (wsem, P.c(wsem))
            P.step("tensor", (lambda e, bank=bank, nw=nw, buf=buf, wsl=wsl: [
                e.matmul(pb[bank][:, 0:nw],
                         wi_s[wsl][:, kc * 128:(kc + 1) * 128],
                         buf[:, kc * nw:(kc + 1) * nw],
                         start=(kc == 0), stop=(kc == 7))
                for kc in range(8)][-1]),
                waits=[("dv", dv_std), bank_free[bank], ld_wi])
            pe_proj = P.c("pe")
            pe_last = pe_proj
            wi_guard[wsl] = ("pe", pe_proj)

            if oc < 8:
                P.step("scalar", (lambda e, bank=bank, nw=nw: e.activation(
                    qstage[:, 0:nw], pb[bank][:, 0:nw], AF.Copy, scale=0.125)),
                    waits=[("pe", pe_proj), S1["qst"]])
                ac_q = P.c("ac")
                bank_free[bank] = ("ac", ac_q)
                P.step("scalar", (lambda e, oc=oc, c0=c0, nw=nw:
                       e.dma_start(
                           out=qspill.rearrange("(h r) c -> h r c", r=66)
                               [2 * oc:2 * oc + 2, 0:64, c0:c0 + nw],
                           in_=qstage[:, 0:nw])),
                       waits=[("ac", ac_q)], dma="sps")
                S1["qst"] = ("sps", P.c("sps"))
            elif oc < 16:
                kc_ = oc - 8
                if kind == "apx":
                    P.step("vector", (lambda e, bank=bank, kc_=kc_:
                           e.tensor_mul(kprevs[:, kc_ * NB:kc_ * NB + NB],
                                        pb[bank][:, 0:NB], bmask[:])),
                           waits=[("pe", pe_proj), ("pre", ld_pre)])
                    bank_free[bank] = ("dv", P.c("dv"))
                else:
                    P.step("scalar", (lambda e, bank=bank, nw=nw:
                           e.activation(kstage[:, 0:nw], pb[bank][:, 0:nw],
                                        AF.Copy)),
                           waits=[("pe", pe_proj), S1["kst"]])
                    ac_kcp = P.c("ac")
                    bank_free[bank] = ("ac", ac_kcp)
                    P.step("vector", (lambda e, nw=nw: e.tensor_sub(
                        ksm[:, 1:nw], kstage[:, 0:nw - 1], kstage[:, 1:nw])),
                        waits=[("ac", ac_kcp), ("ac", ac_sig)])
                    P.step("vector", (lambda e, kc_=kc_, nw=nw:
                           e.scalar_tensor_tensor(
                               ksm[:, 1:nw], ksm[:, 1:nw],
                               smear_s[:, kc_:kc_ + 1], kstage[:, 1:nw],
                               ALU.mult, ALU.add)))
                    blk0 = c0 // 128
                    for bi2 in range(nw // 128):
                        cb = bi2 * 128
                        ap_ = kc_ * NB + blk0 + bi2
                        P.step("vector", (lambda e, cb=cb, ap_=ap_:
                               e.tensor_sub(ksm[:, cb:cb + 1],
                                            kprevs[:, ap_:ap_ + 1],
                                            kstage[:, cb:cb + 1])))
                        P.step("vector", (lambda e, cb=cb, kc_=kc_:
                               e.scalar_tensor_tensor(
                                   ksm[:, cb:cb + 1], ksm[:, cb:cb + 1],
                                   smear_s[:, kc_:kc_ + 1],
                                   kstage[:, cb:cb + 1],
                                   ALU.mult, ALU.add)))
                    dv_smr = P.c("dv")
                    P.step("scalar", (lambda e, kc_=kc_, c0=c0, nw=nw:
                           e.dma_start(
                               out=kspill.rearrange("(h r) c -> h r c", r=66)
                                   [2 * kc_:2 * kc_ + 2, 0:64, c0:c0 + nw],
                               in_=ksm[:, 0:nw])),
                           waits=[("dv", dv_smr)], dma="sps")
                    S1["kst"] = ("sps", P.c("sps"))
            else:
                vc_ = oc - 16
                if kind == "apx":
                    bank_free[bank] = ("pe", pe_proj)
                else:
                    P.step("scalar", (lambda e, bank=bank, nw=nw:
                           e.activation(vstage[:, 0:nw], pb[bank][:, 0:nw],
                                        AF.Copy)),
                           waits=[("pe", pe_proj), S1["vst"]])
                    dv_vcp = P.c("ac")
                    bank_free[bank] = ("ac", dv_vcp)
                    vt_dv = []
                    prev_tr_dv = None
                    for bi2 in range(nw // 128):
                        P.step("tensor", (lambda e, bi2=bi2: e.transpose(
                            pb[6][:, 0:128],
                            vstage[:, bi2 * 128:bi2 * 128 + 128], ident[:])),
                            waits=[("ac", dv_vcp), ("pre", ld_pre),
                                   S1["vst"] if bi2 == 0 else None,
                                   ("dv", prev_tr_dv)])
                        pe_tr = P.c("pe")
                        P.step("vector", (lambda e, bi2=bi2: e.tensor_copy(
                            vtstage[:, bi2 * 128:bi2 * 128 + 128],
                            pb[6][:, 0:128])), waits=[("pe", pe_tr)])
                        vt_dv.append(P.c("dv"))
                        prev_tr_dv = vt_dv[-1]
                        S1["vt_evac"] = vt_dv[-1]
                    blk0v = c0 // 128
                    for hp in range(2):
                        hgl = 2 * vc_ + hp
                        P.step("gpsimd", (lambda e, hgl=hgl, hp=hp,
                                          blk0v=blk0v, nw=nw: e.dma_start(
                               out=vspill.rearrange("(r p) c -> p r c", p=128)
                                   [:, hgl * NB + blk0v:
                                    hgl * NB + blk0v + nw // 128, :],
                               in_=vtstage[:, 0:nw]
                                   .rearrange("p (b q) -> p b q", q=128)
                                   [:, :, hp * 64:hp * 64 + 64])),
                               waits=[("dv", vt_dv[-1])], dma="spv")
                    S1["vst"] = ("spv", P.c("spv"))
        tile_consumed[pos] = pe_last
        sps_at[ti] = P.c("sps")
        spv_at[ti] = P.c("spv")

    P.step("vector", lambda e: e.memset(vbuf[0][:, 64:NB * 65:65], 1.0),
           waits=[("dv", dv_pre)])
    P.step("vector", lambda e: e.memset(vbuf[1][:, 64:NB * 65:65], 1.0))
    dv_vones = P.c("dv")

    buf_free = {0: None, 1: None}
    pbuf_free = {0: None, 1: None}
    psS_free = {0: None, 1: None}

    conv_tasks = ([("g", i, half) for i in range(8) for half in range(2)]
                  + [("u", i, half) for i in range(8) for half in range(2)]
                  + [("d", i, 0) for i in range(24)])
    conv_guard = {}

    def emit_conv(idx):
        kind, i, half = conv_tasks[idx]
        par = idx % 2
        src_t, dst_t = {"g": (wgTi, wgb), "u": (wuTi, wub),
                        "d": (wdTi, wdb)}[kind]
        ncols = FF // 2 if kind != "d" else D
        c0c = half * ncols
        r0 = i * 128
        P.step("sync", (lambda e: e.dma_start(
            out=wcv_i[par][:, 0:ncols],
            in_=src_t[r0:r0 + 128, c0c:c0c + ncols])),
            waits=[("dv", conv_guard.get(("i", par)))],
            dma=("g3a" if par == 0 else "g3b"))
        ld_c = ("g3a" if par == 0 else "g3b",
                P.c("g3a" if par == 0 else "g3b"))
        P.step("vector", (lambda e: e.tensor_copy(
            wcv_o[par][:, 0:ncols], wcv_i[par][:, 0:ncols])),
            waits=[ld_c, ("wcv", conv_guard.get(("o", par)))])
        conv_guard[("i", par)] = P.c("dv")
        P.step("gpsimd", (lambda e: e.dma_start(
            out=dst_t[r0:r0 + 128, c0c:c0c + ncols],
            in_=wcv_o[par][:, 0:ncols])),
            waits=[("dv", conv_guard[("i", par)])], dma="wcv", chain=False)
        conv_guard[("o", par)] = P.c("wcv")

    hts = [(t, h) for t in range(4) for h in range(H)]
    S2 = {"psO_free": None, "ostg_free": None}

    def emit_li(li):
        t, h = hts[li]
        bi = li % 2
        if li < len(conv_tasks):
            emit_conv(li)
        nk = (4 * t + 4) * 128
        njt2 = 4 * t + 4
        njt = 2 * njt2

        P.step("sync", (lambda e, h=h, nk=nk, bi=bi: e.dma_start(
            out=kbuf[bi][0:66, 0:nk],
            in_=kspill[h * 66:h * 66 + 66, 0:nk])),
            waits=[("sps", sps_at[5 + t]), ("dv", dv_vones), buf_free[bi]],
            dma=("k2a" if bi == 0 else "k2b"))
        P.step("sync", (lambda e, h=h, nk=nk, bi=bi: e.dma_start(
            out=kbuf[bi][0:66, nk:2 * nk],
            in_=kspill[h * 66:h * 66 + 66, NQ:NQ + nk])),
            waits=[buf_free[bi]], dma=("k2a" if bi == 0 else "k2b"))
        P.step("gpsimd", (lambda e, h=h, njt2=njt2, bi=bi: e.dma_start(
            out=vbuf[bi][:, 0:njt2 * 65]
                .rearrange("p (j c) -> p j c", c=65)[:, :, 0:DH],
            in_=vspill[(h * NB) * 128:(h * NB + njt2) * 128, :]
                .rearrange("(j p) c -> p j c", p=128))),
            waits=[("spv", spv_at[5 + t]), buf_free[bi]],
            dma=("v2a" if bi == 0 else "v2b"))
        P.step("gpsimd", (lambda e, h=h, njt2=njt2, bi=bi: e.dma_start(
            out=vbuf[bi][:, njt2 * 65:2 * njt2 * 65]
                .rearrange("p (j c) -> p j c", c=65)[:, :, 0:DH],
            in_=vspill[(h * NB + 16) * 128:(h * NB + 16 + njt2) * 128, :]
                .rearrange("(j p) c -> p j c", p=128))),
            waits=[buf_free[bi]], dma=("v2a" if bi == 0 else "v2b"))
        P.step("sync", (lambda e, h=h, t=t, bi=bi: e.dma_start(
            out=qbuf[bi][0:66, :],
            in_=qspill[h * 66:h * 66 + 66, t * TT:(t + 1) * TT])),
            waits=[buf_free[bi]], dma=("k2a" if bi == 0 else "k2b"))
        ld_set = P.now("k2a" if bi == 0 else "k2b", "v2a" if bi == 0 else "v2b")

        exp_done = {}
        pv_done = {}

        def emit_S(j, bi=bi, t=t, njt2=njt2, ld_set=ld_set):
            sbk = j % 2
            own = j < njt2
            m = j if own else j - njt2
            diag = (m - 4 * t) if (4 * t <= m < 4 * t + 4) else None
            P.step("tensor", (lambda e, j=j, sbk=sbk, bi=bi: e.matmul(
                pb[sbk][:, :], kbuf[bi][0:66, j * 128:(j + 1) * 128],
                qbuf[bi][0:66, :], start=True, stop=True)),
                waits=ld_set +
                      [("ac", exp_done.get(j - 2)) if j >= 2 else psS_free[sbk]])
            pe_s = P.c("pe")
            dv_mask = None
            npre = min(m - 4 * t, 4) if m > 4 * t else 0
            if npre > 0:
                P.step("vector", (lambda e, sbk=sbk, npre=npre: e.memset(
                    pb[sbk][:, 0:npre * 128], NEG)),
                    waits=[("pe", pe_s)])
                dv_mask = P.c("dv")
            if diag is not None:
                mk = maskA if own else maskB
                P.step("vector", (lambda e, sbk=sbk, diag=diag, mk=mk:
                       e.tensor_add(pb[sbk][:, diag * 128:(diag + 1) * 128],
                                    pb[sbk][:, diag * 128:(diag + 1) * 128],
                                    mk[:])),
                       waits=[("pe", pe_s), ("pre", ld_pre)])
                dv_mask = P.c("dv")
            P.step("scalar", (lambda e, sbk=sbk: e.activation(
                pbuf[sbk][:], pb[sbk][:], AF.Exp, bias=consts[:, 1:2])),
                waits=[("pe", pe_s), ("dv", dv_pre), ("dv", dv_mask),
                       ("pe", pv_done.get(j - 2)) if j >= 2 else pbuf_free[sbk]])
            exp_done[j] = P.c("ac")

        def emit_PV(j, bi=bi, njt=njt):
            sbk = j % 2
            P.step("tensor", (lambda e, j=j, sbk=sbk, bi=bi, njt=njt: e.matmul(
                pb[2][0:65, :], vbuf[bi][:, j * 65:j * 65 + 65], pbuf[sbk][:],
                start=(j == 0), stop=(j == njt - 1))),
                waits=[("ac", exp_done[j]),
                       S2["psO_free"] if j == 0 else None])
            pv_done[j] = P.c("pe")

        emit_S(0)
        for j in range(1, njt):
            emit_S(j)
            emit_PV(j - 1)
        emit_PV(njt - 1)

        P.step("vector", (lambda e: e.reciprocal(rs_s[64:65, :],
                                                 pb[2][64:65, :])),
               waits=[("pe", pv_done[njt - 1]), S2["ostg_free"]])
        dv_rc = P.c("dv")
        P.step("tensor", (lambda e: e.matmul(
            pb[3][0:64, :], ones128[64:65, 0:64], rs_s[64:65, :],
            start=True, stop=True)), waits=[("dv", dv_rc)])
        pe_bc = P.c("pe")
        P.step("vector", lambda e: e.tensor_copy(bc_s[:], pb[3][0:64, :]),
               waits=[("pe", pe_bc)])
        P.step("vector", lambda e: e.tensor_mul(ostg[:], pb[2][0:64, :],
                                                bc_s[:]))
        dv_div = P.c("dv")
        S2["psO_free"] = ("dv", dv_div)
        P.step("scalar", (lambda e, h=h, t=t: e.dma_start(
            out=ospill[h * 64:(h + 1) * 64, t * TT:(t + 1) * TT],
            in_=ostg[:])), waits=[("dv", dv_div)], dma="spo")
        S2["ostg_free"] = ("spo", P.c("spo"))

        buf_free[bi] = ("pe", pv_done[njt - 1])
        pbuf_free[0] = ("pe", pv_done[njt - 2])
        pbuf_free[1] = ("pe", pv_done[njt - 1])
        psS_free[0] = ("ac", exp_done[njt - 2])
        psS_free[1] = ("ac", exp_done[njt - 1])

    order = [(0, None), (1, None), (5, []), (2, range(0, 4)),
             (6, range(4, 8)), (3, range(8, 12)), (7, range(12, 16)),
             (4, range(16, 24)), (8, range(24, 32))]
    pos = 0
    for ti, lis in order:
        emit_tile(ti, pos)
        pos += 1
        if lis:
            for li in lis:
                emit_li(li)
    for li in range(32, 64):
        emit_li(li)

    ph2_spo = P.c("spo")
    conv_total = P.c("wcv")

    state = {"statsv": None}

    def ln_stats(src_s, waits0):
        pe_sq2 = {}
        for cc in range(8):
            P.step("scalar", (lambda e, cc=cc, src_s=src_s: e.activation(
                sqc3[cc % 2][:], src_s[:, cc * TT:(cc + 1) * TT], AF.Square)),
                waits=list(waits0) + [("pe", pe_sq2.get(cc - 2))])
            ac_sq2 = P.c("ac")
            P.step("tensor", (lambda e, cc=cc: e.matmul(
                pb[3][:], ones128[:], sqc3[cc % 2][:],
                start=(cc == 0), stop=(cc == 7))),
                waits=[("ac", ac_sq2),
                       ("dv", state["statsv"]) if cc == 0 else None])
            pe_sq2[cc] = P.c("pe")
        P.step("tensor", (lambda e, src_s=src_s: [
            e.matmul(pb[2][:], ones128[:], src_s[:, cc * TT:(cc + 1) * TT],
                     start=(cc == 0), stop=(cc == 7)) for cc in range(8)][-1]),
            waits=list(waits0) + [("dv", state["statsv"]), ("dv", dv_pre)])
        pe_cs2 = P.c("pe")
        P.step("vector", lambda e: e.tensor_scalar(
            mean2[:], pb[2][:], 1.0 / D, None, ALU.mult),
            waits=[("pe", pe_cs2), ("pe", pe_sq2[7])])
        P.step("vector", lambda e: e.tensor_scalar(
            tmp2[:], pb[2][:], 1.0 / D, None, ALU.mult))
        P.step("vector", lambda e: e.tensor_mul(tmp2[:], mean2[:], tmp2[:]))
        P.step("vector", lambda e: e.scalar_tensor_tensor(
            rstd2[:], pb[3][:], 1.0 / D, tmp2[:], ALU.mult, ALU.subtract))
        P.step("vector", lambda e: e.tensor_scalar(
            rstd2[:], rstd2[:], EPS, None, ALU.add))
        P.step("vector", lambda e: e.reciprocal(rstd2[:], rstd2[:]))
        dv_sv = P.c("dv")
        state["statsv"] = dv_sv
        P.step("scalar", lambda e: e.activation(rstd2[:], rstd2[:], AF.Sqrt),
               waits=[("dv", dv_sv)])
        return P.c("ac")

    prev_pe_op = None
    prev_dv_t1 = None
    prev_out = None
    prev_ffn_pp = []
    prev_wg_mm = {}
    prev_ds = None

    for tt_i in range(4):
        c0 = tt_i * TT

        P.step("sync", (lambda e, c0=c0: e.dma_start(
            out=ot_s[:].rearrange("p (k c) -> p k c", k=8),
            in_=ospill[:, c0:c0 + TT].rearrange("(k p) c -> p k c", p=128))),
            waits=[("spo", ph2_spo), ("pe", prev_pe_op)], dma="o3")
        P.step("sync", (lambda e, c0=c0: e.dma_start(
            out=xo_s[:].rearrange("p (k c) -> p k c", k=8),
            in_=xT[:, c0:c0 + TT].rearrange("(k p) c -> p k c", p=128))),
            waits=[("dv", prev_dv_t1)], dma="o3")
        ld_o = ("o3", P.c("o3"))

        po_ev = {}
        pe_o2 = {}
        ld_wo = {}
        for oc in range(8):
            P.step("sync", (lambda e, oc=oc: e.dma_start(
                out=wo_s[oc % 2][:].rearrange("p (k c) -> p k c", k=8),
                in_=woutT[:, oc * 128:(oc + 1) * 128]
                    .rearrange("(k p) c -> p k c", p=128))),
                waits=[("pe", pe_o2.get(oc - 2, prev_pe_op))],
                dma=("w3a" if oc % 2 == 0 else "w3b"))
            ld_wo[oc] = ("w3a" if oc % 2 == 0 else "w3b", P.c("w3a" if oc % 2 == 0 else "w3b"))
            P.step("tensor", (lambda e, oc=oc: [
                e.matmul(pb[oc % 2][:],
                         wo_s[oc % 2][:, kc * 128:(kc + 1) * 128],
                         ot_s[:, kc * TT:(kc + 1) * TT],
                         start=(kc == 0), stop=(kc == 7))
                for kc in range(8)][-1]),
                waits=[ld_o, ld_wo[oc],
                       ("dv", po_ev.get(oc - 2, prev_dv_t1))] + prev_ffn_pp)
            pe_o2[oc] = P.c("pe")
            P.step("vector", (lambda e, oc=oc: e.tensor_copy(
                po_s[:, oc * TT:(oc + 1) * TT], pb[oc % 2][:])),
                waits=[("pe", pe_o2[oc])])
            po_ev[oc] = P.c("dv")
        prev_pe_op = pe_o2[7]

        ac_r2 = ln_stats(po_s, [("dv", po_ev[7]), ("out", prev_out)])
        for cc in range(8):
            P.step("vector", (lambda e, cc=cc: e.tensor_sub(
                t1_s[:, cc * TT:(cc + 1) * TT], po_s[:, cc * TT:(cc + 1) * TT],
                mean2[:])), waits=[("ac", ac_r2)])
            P.step("vector", (lambda e, cc=cc: e.tensor_mul(
                t1_s[:, cc * TT:(cc + 1) * TT], t1_s[:, cc * TT:(cc + 1) * TT],
                rstd2[:])))
            P.step("vector", (lambda e, cc=cc: e.tensor_scalar(
                t1_s[:, cc * TT:(cc + 1) * TT], t1_s[:, cc * TT:(cc + 1) * TT],
                g2s[:, cc:cc + 1], b2s[:, cc:cc + 1], ALU.mult, ALU.add)))
            P.step("vector", (lambda e, cc=cc: e.tensor_add(
                t1_s[:, cc * TT:(cc + 1) * TT], t1_s[:, cc * TT:(cc + 1) * TT],
                xo_s[:, cc * TT:(cc + 1) * TT])), waits=[ld_o])
        dv_t1 = P.c("dv")
        prev_dv_t1 = dv_t1

        ac_r3 = ln_stats(t1_s, [("dv", dv_t1)])
        for cc in range(8):
            P.step("vector", (lambda e, cc=cc: e.tensor_sub(
                po_s[:, cc * TT:(cc + 1) * TT], t1_s[:, cc * TT:(cc + 1) * TT],
                mean2[:])), waits=[("ac", ac_r3)])
            P.step("vector", (lambda e, cc=cc: e.tensor_mul(
                po_s[:, cc * TT:(cc + 1) * TT], po_s[:, cc * TT:(cc + 1) * TT],
                rstd2[:])))
        dv_x1 = P.c("dv")

        for cc in range(8):
            P.step("vector", (lambda e, cc=cc: e.tensor_copy(
                x1h[:, cc * TT:(cc + 1) * TT],
                po_s[:, cc * TT:(cc + 1) * TT])), waits=[("dv", dv_x1)])
            P.step("vector", (lambda e, cc=cc: e.tensor_sub(
                x1l[:, cc * TT:(cc + 1) * TT], po_s[:, cc * TT:(cc + 1) * TT],
                x1h[:, cc * TT:(cc + 1) * TT])))
        dv_x1s = P.c("dv")

        umm_done = {}
        sil_done = {}
        hmul_done = {}
        wg_mm = {}
        wd_used = dict(state.get("wd_used_prev", {}))
        ds_ev_last = None
        ds_add_last = None
        for half in range(2):
            for ci in range(12):
                hc = half * 12 + ci
                wb = hc % 2
                wprev = wg_mm.get(hc - 2, prev_wg_mm.get(22 + hc))
                P.step("sync", (lambda e, hc=hc, wb=wb: e.dma_start(
                    out=wgb_s[wb][:].rearrange("p (k c) -> p k c", k=8),
                    in_=wgb[:, hc * 128:(hc + 1) * 128]
                        .rearrange("(k p) c -> p k c", p=128))),
                    waits=[("pe", wprev), ("wcv", conv_total)],
                    dma=("g3a" if wb == 0 else "g3b"))
                P.step("sync", (lambda e, hc=hc, wb=wb: e.dma_start(
                    out=wub_s[wb][:].rearrange("p (k c) -> p k c", k=8),
                    in_=wub[:, hc * 128:(hc + 1) * 128]
                        .rearrange("(k p) c -> p k c", p=128))),
                    dma=("g3a" if wb == 0 else "g3b"))
                ld_gu = ("g3a" if wb == 0 else "g3b",
                         P.c("g3a" if wb == 0 else "g3b"))

                gbk = pb[0] if hc % 2 == 0 else pb[2]
                ubk = pb[1] if hc % 2 == 0 else pb[3]
                gsb = gs_s2[hc % 2]
                P.step("tensor", (lambda e, wb=wb, gbk=gbk: [
                    e.matmul(gbk[:], wgb_s[wb][:, kc * 128:(kc + 1) * 128],
                             xs_[:, kc * TT:(kc + 1) * TT],
                             start=(pi == 0 and kc == 0),
                             stop=(pi == 1 and kc == 7))
                    for pi, xs_ in enumerate((x1h, x1l)) for kc in range(8)][-1]),
                    waits=[ld_gu, ("dv", dv_x1s),
                           ("ac", sil_done.get(hc - 2)),
                           ("dv", po_ev[7] if hc < 2 else None)] +
                          (prev_ffn_pp if hc < 2 else []))
                pe_g = P.c("pe")
                P.step("tensor", (lambda e, wb=wb, ubk=ubk: [
                    e.matmul(ubk[:], wub_s[wb][:, kc * 128:(kc + 1) * 128],
                             xs_[:, kc * TT:(kc + 1) * TT],
                             start=(pi == 0 and kc == 0),
                             stop=(pi == 1 and kc == 7))
                    for pi, xs_ in enumerate((x1h, x1l)) for kc in range(8)][-1]),
                    waits=[ld_gu, ("dv", dv_x1s), ("dv", hmul_done.get(hc - 2)),
                           ("dv", po_ev[7] if hc < 2 else None)] +
                          (prev_ffn_pp if hc < 2 else []))
                umm_done[hc] = P.c("pe")
                wg_mm[hc] = umm_done[hc]

                P.step("scalar", (lambda e, gbk=gbk, gsb=gsb: e.activation(
                    gsb[:], gbk[:], AF.Silu, scale=1.0 / 127.0)),
                    waits=[("pe", pe_g), ("dv", hmul_done.get(hc - 2))])
                sil_done[hc] = P.c("ac")
                P.step("vector", (lambda e, ubk=ubk, gsb=gsb: e.tensor_mul(
                    htmp[:], ubk[:], gsb[:])),
                    waits=[("ac", sil_done[hc]), ("pe", umm_done[hc])])
                P.step("vector", (lambda e, ci=ci: e.tensor_copy(
                    hh_s[:, ci * TT:(ci + 1) * TT], htmp[:])))
                P.step("vector", (lambda e, ci=ci: e.tensor_sub(
                    hl_s[:, ci * TT:(ci + 1) * TT], htmp[:],
                    hh_s[:, ci * TT:(ci + 1) * TT])))
                hmul_done[hc] = P.c("dv")

            for dcg in range(2):
                dmm_last = state.get("dmm_prev")
                for hcg in range(3):
                    gidx = half * 6 + dcg * 3 + hcg
                    wb = gidx % 2
                    r0w = (half * 12 + hcg * 4) * 128
                    P.step("sync", (lambda e, r0w=r0w, wb=wb, dcg=dcg:
                           e.dma_start(
                               out=wdb_s[wb][:]
                                   .rearrange("p (q c) -> p q c", q=4),
                               in_=wdb[r0w:r0w + 512, dcg * TT:(dcg + 1) * TT]
                                   .rearrange("(q p) c -> p q c", p=128))),
                           waits=[("pe", wd_used.get(gidx - 2,
                                                     state.get("dmm_prev"))),
                                  ("wcv", conv_total)],
                           dma=("d3a" if wb == 0 else "d3b"))
                    ld_d = ("d3a" if wb == 0 else "d3b",
                            P.c("d3a" if wb == 0 else "d3b"))
                    for cq in range(4):
                        ci = hcg * 4 + cq
                        P.step("tensor", (lambda e, ci=ci, cq=cq, wb=wb: [
                            e.matmul(pb[4 + dq][:],
                                     wdb_s[wb][:, cq * TT + dq * 128:
                                               cq * TT + (dq + 1) * 128],
                                     hs_[:, ci * TT:(ci + 1) * TT],
                                     start=(ci == 0 and pi == 0),
                                     stop=(ci == 11 and pi == 1))
                            for dq in range(4) for pi, hs_ in
                            enumerate((hh_s, hl_s))][-1]),
                            waits=[ld_d,
                                   ("dv", hmul_done[half * 12 + 11]),
                                   ("ac", ds_ev_last) if (ci == 0 and half == 0 and dcg == 1) else None,
                                   ("dv", ds_add_last) if (ci == 0 and half == 1 and dcg == 1) else None,
                                   ("dv", prev_ds) if (ci == 0 and half == 0 and dcg == 0) else None,
                                   ("ac", ds_ev_last) if (ci == 0 and half == 1 and dcg == 0) else None])
                        dmm_last = P.c("pe")
                    wd_used[gidx] = dmm_last
                state["dmm_prev"] = dmm_last
                if half == 0:
                    for dq in range(4):
                        dc = dcg * 4 + dq
                        P.step("scalar", (lambda e, dq=dq, dc=dc: e.activation(
                            scr_s[:, dc * TT:(dc + 1) * TT], pb[4 + dq][:],
                            AF.Copy, scale=DSCALE)),
                            waits=[("pe", dmm_last), ("out", prev_out)])
                    ds_ev_last = P.c("ac")
                else:
                    for dq in range(4):
                        dc = dcg * 4 + dq
                        P.step("vector", (lambda e, dq=dq, dc=dc:
                               e.scalar_tensor_tensor(
                                   scr_s[:, dc * TT:(dc + 1) * TT],
                                   pb[4 + dq][:], DSCALE,
                                   scr_s[:, dc * TT:(dc + 1) * TT],
                                   ALU.mult, ALU.add)),
                               waits=[("pe", dmm_last), ("ac", ds_ev_last)])
                    ds_add_last = P.c("dv")
        prev_wg_mm = dict(wg_mm)
        state["wd_used_prev"] = {k - 12: v for k, v in wd_used.items()}
        prev_ffn_pp = [("ac", sil_done[23]), ("dv", hmul_done[23])]

        for cc in range(8):
            P.step("vector", (lambda e, cc=cc: e.tensor_add(
                t1_s[:, cc * TT:(cc + 1) * TT], po_s[:, cc * TT:(cc + 1) * TT],
                scr_s[:, cc * TT:(cc + 1) * TT])),
                waits=[("dv", ds_add_last), ("dv", dv_x1)])
        dv_t2 = P.c("dv")
        prev_ds = dv_t2

        ac_r4 = ln_stats(t1_s, [("dv", dv_t2)])
        for cc in range(8):
            P.step("vector", (lambda e, cc=cc: e.tensor_sub(
                scr_s[:, cc * TT:(cc + 1) * TT], t1_s[:, cc * TT:(cc + 1) * TT],
                mean2[:])), waits=[("ac", ac_r4)])
            P.step("vector", (lambda e, cc=cc: e.tensor_mul(
                scr_s[:, cc * TT:(cc + 1) * TT], scr_s[:, cc * TT:(cc + 1) * TT],
                rstd2[:])))
        dv_y = P.c("dv")

        P.step("scalar", (lambda e, c0=c0: e.dma_start(
            out=yT[:, c0:c0 + TT].rearrange("(k p) c -> p k c", p=128),
            in_=scr_s[:].rearrange("p (k c) -> p k c", k=8))),
            waits=[("dv", dv_y)], dma="out")
        prev_out = P.c("out")

    out_total = P.c("out")

    def emitter(ops, final_wait=None):
        def run(eng):
            for fn, waits, inc in ops:
                for s, v in waits:
                    eng.wait_ge(SEMS[s], v)
                hh = fn(eng)
                if inc is not None:
                    if isinstance(hh, list):
                        hh = hh[-1]
                    hh.then_inc(SEMS[inc[0]], inc[1])
            if final_wait is not None:
                eng.wait_ge(SEMS[final_wait[0]], final_wait[1])
        return run

    with nc.Block() as block:
        block.sync(emitter(P.q["sync"], final_wait=("out", out_total)))
        block.tensor(emitter(P.q["tensor"]))
        block.vector(emitter(P.q["vector"]))
        block.scalar(emitter(P.q["scalar"]))
        block.gpsimd(emitter(P.q["gpsimd"]))
    for cm in reversed(sem_ctx):
        cm.__exit__(None, None, None)
    return nc



def _core_layout(r):
    own = [2 * s + r for s in range(16)]
    other = [2 * s + (1 - r) for s in range(16)]
    blocks = own + other
    perm = np.concatenate([np.arange(128 * g, 128 * g + 128) for g in blocks])
    apx = np.array([max(128 * g - 1, 0) for g in blocks], dtype=np.int64)
    apx_valid = np.array([1.0 if g > 0 else 0.0 for g in blocks],
                         dtype=np.float32)
    return blocks, perm, apx, apx_valid


def kernel(L0_attn_x, L0_attn_ln1_g, L0_attn_ln1_b, L0_attn_in_w,
           L0_attn_out_w, L0_attn_ln2_g, L0_attn_ln2_b, L0_attn_slopes,
           L0_attn_smear, L0_ffn_w_gate, L0_ffn_w_up, L0_ffn_w_down):
    x = np.ascontiguousarray(np.asarray(L0_attn_x, dtype=np.float32))
    winT = np.ascontiguousarray(np.asarray(L0_attn_in_w, np.float32).T)
    woutT = np.ascontiguousarray(np.asarray(L0_attn_out_w, np.float32).T)
    wgT = np.ascontiguousarray(np.asarray(L0_ffn_w_gate, np.int32).T)
    wuT = np.ascontiguousarray(np.asarray(L0_ffn_w_up, np.int32).T)
    wdT = np.ascontiguousarray(np.asarray(L0_ffn_w_down, np.int32).T)
    g1c = np.ascontiguousarray(np.asarray(L0_attn_ln1_g, np.float32)
                               .reshape(8, 128).T)
    b1c = np.ascontiguousarray(np.asarray(L0_attn_ln1_b, np.float32)
                               .reshape(8, 128).T)
    g2c = np.ascontiguousarray(np.asarray(L0_attn_ln2_g, np.float32)
                               .reshape(8, 128).T)
    b2c = np.ascontiguousarray(np.asarray(L0_attn_ln2_b, np.float32)
                               .reshape(8, 128).T)
    slopes = np.ascontiguousarray(
        np.asarray(L0_attn_slopes, np.float32).reshape(H, 1))
    smear = np.asarray(L0_attn_smear, np.float32)
    smearb = np.ascontiguousarray(np.repeat(smear, 64).reshape(8, 128).T)
    maskA = np.where(np.arange(128)[:, None] > np.arange(128)[None, :],
                     np.float32(NEG), np.float32(0.0)).astype(np.float32)

    nc = build_nc()
    in_maps = []
    metas = []
    for c in range(8):
        b, r = c // 2, c % 2
        blocks, perm, apx, apx_valid = _core_layout(r)
        xTb = np.ascontiguousarray(x[b].T)
        xcore = xTb[:, perm]
        xapx = np.zeros((D, 128), dtype=np.float32)
        xapx[:, 0:NB] = xTb[:, apx]
        xin = np.ascontiguousarray(np.concatenate([xcore, xapx], axis=1))
        kpos = np.ascontiguousarray(
            np.broadcast_to(perm.astype(np.float32)[None, :], (16, L)))
        maskB = (np.zeros((128, 128), np.float32) if r == 1
                 else np.full((128, 128), NEG, np.float32))
        bmask = np.ascontiguousarray(
            np.broadcast_to(apx_valid[None, :], (128, NB)).copy())
        in_maps.append({
            "xT": xin, "winT": winT, "woutT": woutT,
            "wgTi": wgT, "wuTi": wuT, "wdTi": wdT,
            "g1i": g1c, "b1i": b1c, "g2i": g2c, "b2i": b2c,
            "slopesi": slopes, "smeari": smearb, "kposi": kpos,
            "maskAi": maskA, "maskBi": maskB, "bmaski": bmask,
        })
        metas.append((b, perm))

    res = run_bass_kernel_spmd(nc, in_maps, core_ids=list(range(8)))

    out = np.empty((4, L, D), dtype=np.float32)
    for c in range(8):
        b, perm = metas[c]
        yTc = np.asarray(res.results[c]["yT"])
        out[b, perm[:NQ], :] = yTc.T
    return out
